# version 03
# baseline (speedup 1.0000x reference)
import numpy as np
from contextlib import ExitStack

import concourse.bass as bass
import concourse.mybir as mybir
from concourse.bass_utils import run_bass_kernel_spmd
from concourse.aot_env import allow_ici_all_to_all

F32 = mybir.dt.float32
F32R = mybir.dt.float32r
BF16 = mybir.dt.bfloat16
AF = mybir.ActivationFunctionType
ALU = mybir.AluOpType
AX = mybir.AxisListType

D, DSL, NB, L, NST, RK, E, HE = 2048, 256, 4, 4096, 16, 128, 8, 1024
T = NB * L
LC = 512
LCH = LC + 3
NJ = L // LC
NCH = NB * NJ
TOK3 = 2048
NQ = 4
EPS = 1e-5
INVD = 1.0 / D

DEBUG_TAPS = []
SKIP_P3 = False


def _r(x):
    return np.ascontiguousarray(x)


def stage_inputs(inputs):
    x = np.asarray(inputs["L0_attn_x"], dtype=np.float32)
    x_dmaj = _r(x.transpose(0, 2, 1))
    conv_w = np.asarray(inputs["L0_attn_conv_w"], np.float32)[:, 0, :]
    conv_b = np.asarray(inputs["L0_attn_conv_b"], np.float32)
    xprojT = _r(np.asarray(inputs["L0_attn_xproj_w"], np.float32).T)
    dtprojT = _r(np.asarray(inputs["L0_attn_dtproj_w"], np.float32).T)
    dtproj_b = np.asarray(inputs["L0_attn_dtproj_b"], np.float32)
    A_log = np.asarray(inputs["L0_attn_A_log"], np.float32)
    D_skip = np.asarray(inputs["L0_attn_D_skip"], np.float32)
    Wg1T = _r(np.asarray(inputs["L0_ffn_Wg1"], np.float32).T)
    Wg2T = _r(np.asarray(inputs["L0_ffn_Wg2"], np.float32).T)
    Wup = np.asarray(inputs["L0_ffn_Wup"], np.float32)
    bup = np.asarray(inputs["L0_ffn_bup"], np.float32)
    Wdown = np.asarray(inputs["L0_ffn_Wdown"], np.float32)
    bdown = np.asarray(inputs["L0_ffn_bdown"], np.float32)
    cc = np.asarray(inputs["L0_c"], np.float32)
    wmod = np.asarray(inputs["L0_wmod"], np.float32)

    WupT = Wup.transpose(0, 2, 1)
    wup_tiles = _r(WupT.reshape(E, 16, 128, 8, 128).transpose(0, 3, 1, 2, 4))
    WdownT = Wdown.transpose(0, 2, 1)
    a = WdownT.reshape(E, 8, 128, 16, 128).transpose(3, 0, 1, 2, 4)
    wdown_tiles = _r(a.reshape(8, 2, E, 8, 128, 128).transpose(0, 2, 1, 3, 4, 5))
    wmod_r = wmod.reshape(D, 6, 8, DSL)
    cT = _r(cc.T)

    in_maps = []
    for c in range(8):
        sl = slice(DSL * c, DSL * (c + 1))
        bc, hc3 = c // 2, c % 2
        bsel = np.zeros((128, NB), np.float32)
        bsel[:, bc] = 1.0
        in_maps.append({
            "xp1": _r(x_dmaj[:, sl, :]),
            "xp3": _r(x_dmaj[bc, :, hc3 * TOK3:(hc3 + 1) * TOK3]),
            "cT": cT,
            "wmod_sl": _r(wmod_r[:, :, c, :].reshape(D, 6 * DSL)),
            "xprojT_sl": _r(xprojT[sl, :]),
            "dtprojT_sl": _r(dtprojT[:, sl]),
            "dtproj_b_sl": _r(dtproj_b[sl].reshape(2, 128).T),
            "conv_w_sl": _r(conv_w[sl].reshape(2, 128, 4).transpose(1, 0, 2)
                            .reshape(128, 8)),
            "conv_b_sl": _r(conv_b[sl].reshape(2, 128).T),
            "A_log_sl": _r(A_log[sl].reshape(2, 128, NST).transpose(1, 0, 2)
                           .reshape(128, 2 * NST)),
            "D_skip_sl": _r(D_skip[sl].reshape(2, 128).T),
            "Wg1T_t": _r(Wg1T.reshape(16, 128, 8)),
            "Wg2T_t": Wg2T,
            "wup_tiles": wup_tiles,
            "wdown_tiles": wdown_tiles,
            "bup_t": _r(bup.reshape(E, 8, 128).transpose(2, 0, 1).reshape(128, E * 8)),
            "bdown_t": bdown,
            "iota8": _r(np.tile(np.arange(8, dtype=np.float32), (128, 1))),
            "iotap": _r(np.arange(128, dtype=np.float32).reshape(128, 1)),
            "iotap32": _r(np.arange(32, dtype=np.float32).reshape(32, 1)),
            "bsel": bsel,
        })
    return in_maps


class Ch:
    def __init__(self, stk, nc, name):
        self.h = stk.enter_context(nc.semaphore(name))
        self.n = 0

    def bump(self, k=1):
        self.n += k
        return self.n


class Prog:
    ENGS = ("sync", "scalar", "vector", "gpsimd", "tensor")

    def __init__(self):
        self.th = {e: [] for e in self.ENGS}
        self.waited = {e: {} for e in self.ENGS}

    def on(self, eng, fn):
        self.th[eng].append(fn)

    def wait(self, eng, ch, val=None):
        v = ch.n if val is None else val
        if v <= 0:
            return
        if self.waited[eng].get(id(ch), -1) >= v:
            return
        self.waited[eng][id(ch)] = v
        self.on(eng, lambda e, h=ch.h, v=v: e.wait_ge(h, v))

    def finish(self, block):
        for name in self.ENGS:
            fns = self.th[name]
            if fns:
                getattr(block, name)(lambda e, fns=fns: [f(e) for f in fns])


def build_nc(taps=(), skip_p3=False):
    nc = bass.Bass("TRN2", target_bir_lowering=False,
                   dynamic_dma_scratch_size=10240)
    stk = ExitStack()
    e = stk.enter_context

    def dram(name, shape, dtype=F32, kind="Internal"):
        return nc.dram_tensor(name, list(shape), dtype, kind=kind)

    xp1 = dram("xp1", [NB, DSL, L], F32R, kind="ExternalInput")
    xp3 = dram("xp3", [D, TOK3], F32R, kind="ExternalInput")
    cT_h = dram("cT", [D, NB], kind="ExternalInput")
    wmod_h = dram("wmod_sl", [D, 1536], F32R, kind="ExternalInput")
    xprojT_h = dram("xprojT_sl", [DSL, 160], F32R, kind="ExternalInput")
    dtprojT_h = dram("dtprojT_sl", [RK, DSL], F32R, kind="ExternalInput")
    dtb_h = dram("dtproj_b_sl", [128, 2], kind="ExternalInput")
    convw_h = dram("conv_w_sl", [128, 8], kind="ExternalInput")
    convb_h = dram("conv_b_sl", [128, 2], kind="ExternalInput")
    alog_h = dram("A_log_sl", [128, 2 * NST], kind="ExternalInput")
    dskip_h = dram("D_skip_sl", [128, 2], kind="ExternalInput")
    wg1_h = dram("Wg1T_t", [16, 128, 8], kind="ExternalInput")
    wg2_h = dram("Wg2T_t", [8, 8], kind="ExternalInput")
    wup_h = dram("wup_tiles", [E, 8, 16, 128, 128], kind="ExternalInput")
    wdown_h = dram("wdown_tiles", [8, E, 2, 8, 128, 128], kind="ExternalInput")
    bup_h = dram("bup_t", [128, E * 8], kind="ExternalInput")
    bdown_h = dram("bdown_t", [8, D], F32R, kind="ExternalInput")
    iota8_h = dram("iota8", [128, 8], kind="ExternalInput")
    iotap_h = dram("iotap", [128, 1], kind="ExternalInput")
    iotap32_h = dram("iotap32", [32, 1], kind="ExternalInput")
    bsel_h = dram("bsel", [128, NB], kind="ExternalInput")
    out_h = dram("out", [D, TOK3], kind="ExternalOutput")

    tap_h = {}
    for tname, shape in (("y", [NCH, DSL, LC]), ("h1", [NCH, DSL, LC]),
                         ("u", [NCH, DSL, LC]), ("delta", [NCH, DSL, LC]),
                         ("xdbl", [NCH, 160, LC]), ("mod", [128, 48]),
                         ("scores", [8, TOK3]), ("x1", [D, TOK3]),
                         ("stp", [NCH, 2, LC]), ("stats", [NCH, 2, LC]),
                         ("rows", [NCH, 1, 3 * LC]), ("h1m", [NCH, DSL, LC]),
                         ("cst", [NCH, 128]), ("cmod", [128, 64]),
                         ("ncaw", [3, 128, 32]), ("x1e", [16, 128, LC]),
                         ("evt", [16, 128, LC])):
        if tname in taps:
            tap_h[tname] = dram("tap_" + tname, shape, kind="ExternalOutput")
    if "h2" in taps:
        tap_h["h2"] = dram("tap_h2", [D, TOK3], BF16, kind="ExternalOutput")
    if "win" in taps:
        tap_h["winf"] = dram("tap_winf", [2, 2912], kind="ExternalOutput")
        tap_h["winb"] = dram("tap_winb", [2, 4096], BF16, kind="ExternalOutput")

    wup_bf = dram("wup_bf", [E, 8, 16, 128, 128], BF16)
    wdown_bf = dram("wdown_bf", [8, E, 2, 8, 128, 128], BF16)
    wg1_bf = dram("wg1_bf", [16, 128, 8], BF16)
    wg2_bf = dram("wg2_bf", [8, 8], BF16)
    xdblr_bf = dram("xdblr_bf", [NCH, 32, LC], BF16)
    ar1_in = dram("ar1_in", [NCH, 2, LC])
    ar1_out = dram("ar1_out", [NCH, 2, LC])
    ar2_in = dram("ar2_in", [NCH, 160, LC])
    ar2_out = dram("ar2_out", [NCH, 160, LC])
    ag_in = dram("ag_in", [1536, NB])
    ag_out = dram("ag_out", [8 * 1536, NB])
    a2a_in = [dram(f"a2a_in{g}", [8, DSL, LC]) for g in range(NQ)]
    a2a_out = [dram(f"a2a_out{g}", [8, DSL, LC]) for g in range(NQ)]
    x1_d = dram("x1_d", [D, TOK3])

    def sb(name, p, f, dtype=F32):
        return e(nc.sbuf_tensor(name, [p, f], dtype))

    eqm = sb("eqm", 128, 128)
    onesr = sb("onesr", 1, 128)
    onesc = sb("onesc", 128, 1)
    onesc_bf = sb("onesc_bf", 128, 1, BF16)
    selmat = sb("selmat", 32, 32 * 128, BF16)
    sel8 = sb("sel8", 8, 8 * 128)
    convd = sb("convd", 128, 8 * 128)
    xprojT_t = sb("xprojT_t", 128, 2 * 160)
    dtprojT_t = sb("dtprojT_t", 128, 2 * 128)
    acoef = sb("acoef", 128, 2 * NST)
    convw_t = sb("convw_t", 128, 8)
    convb_t = sb("convb_t", 128, 2)
    dskip_t = sb("dskip_t", 128, 2)
    cbds_t = sb("cbds_t", 128, 2)
    dtb_t = sb("dtb_t", 128, 2)
    can0 = sb("can0", 128, 8)
    ca1pp = sb("ca1pp", 128, 8)
    nca1pp = sb("nca1pp", 128, 8)
    can1 = sb("can1", 128, 8)
    gapp = sb("gapp", 128, 16)
    gbpp = sb("gbpp", 128, 16)
    sbpp = sb("sbpp", 128, 16)
    cb1pp = sb("cb1pp", 128, 16)
    ncb1pp = sb("ncb1pp", 128, 16)
    gall = sb("gall", 128, 4 * 64)
    msc = sb("msc", 128, 16)
    msc2 = sb("msc2", 128, 16)
    bsel_t = sb("bsel_t", 128, NB)
    wg1_t = sb("wg1_t", 128, 16 * 8, BF16)
    wg2_t = sb("wg2_t", 8, 8, BF16)
    bup_t = sb("bup_t_s", 128, E * 8)
    iota8_t = sb("iota8_t", 128, 8)
    c8_t = sb("c8_t", 128, 8)
    negbig_t = sb("negbig_t", 128, 8)
    iotap_t = sb("iotap_t", 128, 1)
    iotap32_t = sb("iotap32_t", 32, 1)
    ct_t = sb("ct_t", 128, 16 * NB)
    sct_t = sb("sct_t", 128, 16 * NB)
    modsl = sb("modsl", 128, 12 * NB)

    xs = [sb(f"xs{i}", 128, 2 * LCH) for i in range(2)]
    x2t = sb("x2t", 128, 2 * LC, BF16)
    statb = sb("statb", 1, 2 * LC)
    rows = sb("rows", 1, 3 * LC)
    ut = sb("ut", 128, 2 * LC)
    xpa_sb = sb("xpa_sb", 128, LC)
    xpb_sb = sb("xpb_sb", 32, LC)
    dtt = sb("dtt", 128, LC)
    bct = sb("bct", 32, LC, BF16)
    azt = sb("azt", 128, 2 * LC)
    rlt = sb("rlt", 128, 2 * LC)
    dAt = [sb(f"dAt{i}", 128, LC) for i in range(2)]
    htraj = [sb(f"htraj{i}", 128, LC) for i in range(2)]
    yct = [sb(f"yct{i}", 128, LC) for i in range(2)]
    yt = [sb("yt0", 128, 2 * LC)]
    state = sb("state", 128, 128)
    halo = sb("halo", 128, 8 * 3)

    if not skip_p3:
        h2q = sb("h2q", 128, 16 * LC, BF16)
        act_t = sb("act_t", 128, E * 8 * LC, BF16)
        yq = sb("yq", 128, 2 * LC)
        x1q = sb("x1q", 128, 2 * LC)
        x1h = sb("x1h", 128, 2 * LC)
        rows3 = sb("rows3", 1, 3 * LC)
        stat3 = sb("stat3", 1, 2 * LC)
        th1 = sb("th1", 8, LC, BF16)
        lgt = sb("lgt", 8, LC)
        tl = sb("tl", 128, 8)
        tsc = sb("tsc", 128, 12 * 8)
        msk8 = sb("msk8", 128, 3 * 8, mybir.dt.uint8)
        scoreq = [sb(f"scoreq{i}", 8, LC) for i in range(2)]
        wup_ring = [sb(f"wupr{i}", 128, 16 * 128, BF16) for i in range(2)]
        wdn_ring = [sb(f"wdnr{i}", 128, 16 * 128, BF16) for i in range(2)]
        bdg = [sb(f"bdg{i}", 8, 2 * 128) for i in range(2)]
        evt = [sb("evt0", 128, LC)]
        x1r = sb("x1r", 128, LC)
        x1e = [sb("x1e0", 128, LC)]

    p1ps = [e(nc.psum_tensor(f"p1ps{i}", [128, 512], F32)) for i in range(2)]
    repps = [e(nc.psum_tensor(f"repps{i}", [128, 512], F32)) for i in range(2)]
    upps = [e(nc.psum_tensor(f"upps{i}", [128, 512], F32)) for i in range(2)]
    dnps = [e(nc.psum_tensor(f"dnps{i}", [128, 512], F32)) for i in range(2)]

    chn = {}

    def ch(name):
        if name not in chn:
            chn[name] = Ch(stk, nc, name)
        return chn[name]

    P = Prog()
    W = P.wait
    O = P.on
    last_coll = [None]

    def coll_chain():
        if last_coll[0] is not None:
            ch_, v_ = last_coll[0]
            W("gpsimd", ch_, v_)

    def coll_mark(ch_):
        last_coll[0] = (ch_, ch_.n)

    def dma(eng, out, in_, c):
        c.bump(16)
        O(eng, lambda en, o=out, i=in_, h=c.h: en.dma_start(out=o, in_=i).then_inc(h, 16))

    cS_ld = ch("cS_ld")
    dma("sync", iota8_t[:, :], iota8_h.ap(), cS_ld)
    dma("sync", iotap_t[:, :], iotap_h.ap(), cS_ld)
    dma("sync", iotap32_t[:, :], iotap32_h.ap(), cS_ld)
    dma("sync", bsel_t[:, :], bsel_h.ap(), cS_ld)
    dma("sync", xprojT_t[:, :].rearrange("p (dc f) -> p dc f", dc=2)
        .bitcast(F32R),
        xprojT_h.ap().rearrange("(dc p) f -> p dc f", p=128), cS_ld)
    dma("sync", dtprojT_t[:, :].rearrange("p (dc f) -> p dc f", dc=2)
        .bitcast(F32R),
        dtprojT_h.ap().rearrange("p (dc f) -> p dc f", dc=2), cS_ld)
    dma("sync", acoef[:, :], alog_h.ap(), cS_ld)
    dma("sync", convw_t[:, :], convw_h.ap(), cS_ld)
    dma("sync", convb_t[:, :], convb_h.ap(), cS_ld)
    dma("sync", dskip_t[:, :], dskip_h.ap(), cS_ld)
    dma("sync", dtb_t[:, :], dtb_h.ap(), cS_ld)
    dma("sync", ct_t[:, :].rearrange("p (kc b) -> p kc b", kc=16),
        cT_h.ap().rearrange("(kc p) b -> p kc b", p=128), cS_ld)
    dma("sync", bup_t[:, :], bup_h.ap(), cS_ld)

    cS_cast = ch("cS_cast")
    for ei in range(E):
        dma("gpsimd", wup_bf.ap()[ei], wup_h.ap()[ei], cS_cast)
        dma("gpsimd", wdown_bf.ap()[:, ei], wdown_h.ap()[:, ei], cS_cast)
    dma("gpsimd", wg1_bf.ap(), wg1_h.ap(), cS_cast)
    dma("gpsimd", wg2_bf.ap(), wg2_h.ap(), cS_cast)
    cS_gl = ch("cS_gl")
    W("sync", cS_cast)
    dma("sync", wg1_t[:, :].rearrange("p (kc f) -> p kc f", kc=16),
        wg1_bf.ap().rearrange("kc p f -> p kc f"), cS_gl)
    dma("sync", wg2_t[:, :], wg2_bf.ap(), cS_gl)

    cS_const = ch("cS_const")

    def _vconst(v):
        v.wait_ge(cS_ld.h, cS_ld.n)
        v.memset(onesr[:, :], 1.0)
        v.memset(onesc[:, :], 1.0)
        v.memset(onesc_bf[:, :], 1.0)
        v.tensor_copy(out=onesr[:, :].bitcast(F32R), in_=onesr[:, :])
        v.tensor_copy(out=onesc[:, :].bitcast(F32R), in_=onesc[:, :])
        v.memset(can0[:, :], 99.0)
        v.memset(can1[:, :], 99.0)
        v.memset(c8_t[:, :], 8.0)
        v.memset(negbig_t[:, :], -1e30)
        v.memset(state[:, :], 0.0)
        for n in range(128):
            v.memset(eqm[:, n:n + 1], float(n))
        v.tensor_single_scalar(out=eqm[:, :], in_=eqm[:, :],
                               scalar=iotap_t[:, :], op=ALU.is_equal)
        for n in range(32):
            v.memset(selmat[:, n * 128:(n + 1) * 128], float(n))
        v.tensor_single_scalar(out=selmat[:, :], in_=selmat[:, :],
                               scalar=iotap32_t[:, :], op=ALU.is_equal)
        for n in range(8):
            v.memset(sel8[:, n * 128:(n + 1) * 128], float(n))
        v.tensor_single_scalar(out=sel8[:, :].bitcast(F32R), in_=sel8[:, :],
                               scalar=iotap_t[0:8, :], op=ALU.is_equal)
        for dc in range(2):
            for k in range(4):
                ix = dc * 4 + k
                v.tensor_single_scalar(
                    out=convd[:, ix * 128:(ix + 1) * 128].bitcast(F32R),
                    in_=eqm[:, :], scalar=convw_t[:, ix:ix + 1], op=ALU.mult)
        v.tensor_mul(out=cbds_t[:, :], in0=convb_t[:, :],
                     in1=dskip_t[:, :]).then_inc(cS_const.h, 1)

    O("vector", _vconst)
    cS_const.bump()

    cS_sc = ch("cS_sc")

    def _ssetup(s):
        s.wait_ge(cS_ld.h, cS_ld.n)
        s.activation(out=acoef[:, :], in_=acoef[:, :], func=AF.Exp)
        s.activation(out=sct_t[:, :].bitcast(F32R), in_=ct_t[:, :],
                     func=AF.Silu).then_inc(cS_sc.h, 1)

    O("scalar", _ssetup)
    cS_sc.bump()

    cS_acf = ch("cS_acf")
    W("vector", cS_sc)
    O("vector", lambda v: v.tensor_single_scalar(
        out=acoef[:, :], in_=acoef[:, :], scalar=-1.0,
        op=ALU.mult).then_inc(cS_acf.h, 1))
    cS_acf.bump()

    cS_wmod = ch("cS_wmod")
    cS_modmm = ch("cS_modmm")
    cS_modev = ch("cS_modev")
    kwg = 0
    for mt in range(12):
        for grp in range(4):
            rb = kwg % 2
            W("sync", cS_modmm, kwg - 1)
            dma("sync", yct[rb][:, :].rearrange("p (kc f) -> p kc f", kc=4)
                .bitcast(F32R),
                wmod_h.ap().rearrange("(kc p) (m f) -> m p kc f", p=128, f=128)[
                    mt][:, grp * 4:(grp + 1) * 4], cS_wmod)
            W("tensor", cS_wmod)
            W("tensor", cS_sc)
            if grp == 0:
                W("tensor", cS_modev, mt - 1)

            def _modmm(t, mt=mt, grp=grp, rb=rb):
                for kc in range(4):
                    mm = t.matmul(p1ps[mt % 2][0:128, 0:NB],
                                  yct[rb][:, kc * 128:(kc + 1) * 128].bitcast(F32R),
                                  sct_t[:, (grp * 4 + kc) * NB:(grp * 4 + kc + 1) * NB]
                                  .bitcast(F32R),
                                  start=(grp == 0 and kc == 0),
                                  stop=(grp == 3 and kc == 3))
                mm.then_inc(cS_modmm.h, 1)

            O("tensor", _modmm)
            cS_modmm.bump()
            kwg += 1
        W("scalar", cS_modmm)
        O("scalar", lambda s, mt=mt: s.copy(
            out=modsl[:, mt * NB:(mt + 1) * NB],
            in_=p1ps[mt % 2][0:128, 0:NB]).then_inc(cS_modev.h, 1))
        cS_modev.bump()
    v_wmod_done = cS_modmm.n

    c_tap = ch("c_tap")
    if "mod" in tap_h:
        W("scalar", cS_modev)
        dma("scalar", tap_h["mod"].ap(), modsl[:, :], c_tap)

    cS_agw = ch("cS_agw")
    W("scalar", cS_modev)
    dma("scalar", ag_in.ap().rearrange("(m p) b -> p m b", p=128),
        modsl[:, :].rearrange("p (m b) -> p m b", m=12), cS_agw)
    cS_ag = ch("cS_ag")

    coll_chain()

    def _ag(g):
        g.wait_ge(cS_agw.h, cS_agw.n)
        g.collective_compute(
            "AllGather", ALU.bypass, replica_groups=[list(range(8))],
            ins=[ag_in.ap().opt()], outs=[ag_out.ap().opt()],
        ).then_inc(cS_ag.h, 1)

    O("gpsimd", _ag)
    cS_ag.bump()
    coll_mark(cS_ag)

    cS_agr = ch("cS_agr")
    W("sync", cS_ag)
    agv = ag_out.ap()
    for pj in range(4):
        j_ = pj + 2
        for dcg in range(16):
            row0 = (dcg // 2) * 1536 + j_ * DSL + (dcg % 2) * 128
            dma("sync", gall[:, (pj * 16 + dcg) * 4:(pj * 16 + dcg + 1) * 4],
                agv[row0:row0 + 128, :], cS_agr)

    cS_modp = ch("cS_modp")

    def _modp(v):
        v.wait_ge(cS_modev.h, cS_modev.n)
        v.wait_ge(cS_agr.h, cS_agr.n)
        ca_view = modsl[:, 2 * NB:4 * NB].rearrange("p (dc b) -> p b dc", dc=2)
        v.tensor_single_scalar(
            out=ca1pp[:, :].rearrange("p (b dc) -> p b dc", dc=2),
            in_=ca_view, scalar=1.0, op=ALU.add)
        v.tensor_scalar(
            out=nca1pp[:, :].rearrange("p (b dc) -> p b dc", dc=2),
            in0=ca_view, scalar1=-1.0,
            scalar2=-1.0, op0=ALU.mult, op1=ALU.add)
        dsts = (gapp, sbpp, cb1pp, gbpp)
        for pj, dst in enumerate(dsts):
            base = pj * 64
            v.tensor_single_scalar(out=dst[:, :], in_=gall[:, base:base + 64:4],
                                   scalar=bsel_t[:, 0:1], op=ALU.mult)
        for b in range(1, NB):
            for pj, dst in enumerate(dsts):
                base = pj * 64
                v.scalar_tensor_tensor(out=dst[:, :],
                                       in0=gall[:, base + b:base + 64:4],
                                       scalar=bsel_t[:, b:b + 1], in1=dst[:, :],
                                       op0=ALU.mult, op1=ALU.add)
        v.tensor_single_scalar(out=msc[:, :], in_=gapp[:, :], scalar=0.0,
                               op=ALU.is_ge)
        v.tensor_single_scalar(out=msc2[:, :], in_=gbpp[:, :], scalar=0.0,
                               op=ALU.is_ge)
        v.tensor_scalar(out=ncb1pp[:, :], in0=cb1pp[:, :], scalar1=-1.0,
                        scalar2=-1.0, op0=ALU.mult, op1=ALU.add)
        v.tensor_scalar(out=msc[:, :], in0=msc[:, :], scalar1=2.0, scalar2=-1.0,
                        op0=ALU.mult, op1=ALU.add)
        v.tensor_scalar(out=msc2[:, :], in0=msc2[:, :], scalar1=2.0,
                        scalar2=-1.0, op0=ALU.mult, op1=ALU.add)
        v.tensor_single_scalar(out=cb1pp[:, :], in_=cb1pp[:, :], scalar=1.0,
                               op=ALU.add)
        v.tensor_mul(out=gapp[:, :], in0=gapp[:, :], in1=msc[:, :])
        v.tensor_mul(out=gbpp[:, :], in0=gbpp[:, :], in1=msc2[:, :])
        v.tensor_single_scalar(out=gapp[:, :], in_=gapp[:, :], scalar=0.25,
                               op=ALU.max)
        v.tensor_single_scalar(out=gbpp[:, :], in_=gbpp[:, :], scalar=0.25,
                               op=ALU.max)
        v.tensor_mul(out=gapp[:, :], in0=gapp[:, :], in1=msc[:, :])
        v.tensor_mul(out=gbpp[:, :], in0=gbpp[:, :],
                     in1=msc2[:, :]).then_inc(cS_modp.h, 1)

    O("vector", _modp)
    cS_modp.bump()

    def _ncaw_dump(cap):
        dma("sync", tap_h["ncaw"].ap()[cap, :, 0:8], ca1pp[:, :], c_tap)
        dma("sync", tap_h["ncaw"].ap()[cap, :, 8:16], nca1pp[:, :], c_tap)
        dma("sync", tap_h["ncaw"].ap()[cap, :, 16:24], can0[:, :], c_tap)
        dma("sync", tap_h["ncaw"].ap()[cap, :, 24:32], can1[:, :], c_tap)

    if "ncaw" in tap_h:
        W("sync", cS_modp)
        _ncaw_dump(0)

    def _win_dump(row):
        offs = [0]
        for tile, n in ((eqm, 128), (onesr, 128), (sel8, 1024), (convd, 1024),
                        (xprojT_t, 320), (dtprojT_t, 256), (acoef, 32)):
            o = offs[-1]
            dma("sync", tap_h["winf"].ap()[row:row + 1, o:o + n],
                tile[0:1, 0:n], c_tap)
            offs.append(o + n)
        dma("sync", tap_h["winb"].ap()[row:row + 1, :], selmat[0:1, :], c_tap)

    if "winf" in tap_h:
        W("sync", cS_const)
        W("sync", cS_ld)
        _win_dump(0)

    c1 = {k: ch("c1_" + k) for k in (
        "x", "x2", "stmm", "stev", "ar1w", "ar1", "ar1r", "rows_s", "rows",
        "bkA", "bkB", "rrep", "h1a", "h1b", "hsv", "u", "uev", "xp", "xpe",
        "ar2w", "ar2", "ar2r", "bcc", "dp", "spa", "spg", "dbuN", "brep",
        "brf", "crep", "crf", "dA", "dAf", "yc", "ycf", "yg", "ywr")}
    c1_a2a = [ch(f"c1_a2a{g}") for g in range(NQ)]

    k_dA = 0
    k_yc = 0

    for j in range(NJ):
        for b in range(NB):
            ci = j * NB + b
            rc = ci % 2

            if tap_h:
                for _e in ("sync", "scalar", "vector", "gpsimd", "tensor"):
                    W(_e, c_tap)

            W("sync", c1["u"], 2 * (ci - 1))
            W("sync", c1["hsv"], ci - 1)
            dma("sync",
                xs[rc][:, :].rearrange("p (dc l3) -> p dc l3", dc=2)[
                    :, :, 3:LCH].bitcast(F32R),
                xp1.ap()[b].rearrange("(dc p) l -> p dc l", p=128)[
                    :, :, j * LC:(j + 1) * LC], c1["x"])
            vx = c1["x"].n

            W("vector", c1["x"], vx)
            W("vector", c1["stmm"], ci)
            O("vector", lambda v, rc=rc: v.tensor_mul(
                out=x2t[:, :].rearrange("p (dc l) -> p dc l", dc=2),
                in0=xs[rc][:, :].rearrange("p (dc l3) -> p dc l3", dc=2)[:, :, 3:LCH],
                in1=xs[rc][:, :].rearrange("p (dc l3) -> p dc l3", dc=2)[:, :, 3:LCH])
              .then_inc(c1["x2"].h, 1))
            c1["x2"].bump()

            W("tensor", c1["x"], vx)
            W("tensor", c1["x2"])
            W("tensor", cS_const)
            W("tensor", c1["bkA"], 4 * ci)
            W("tensor", c1["xpe"], ci)

            def _stmm(t, rc=rc):
                for dc in range(2):
                    t.matmul(p1ps[0][0:1, :], onesc[:, :].bitcast(F32R),
                             xs[rc][:, dc * LCH + 3:(dc + 1) * LCH].bitcast(F32R),
                             start=(dc == 0), stop=(dc == 1))
                for dc in range(2):
                    mm = t.matmul(p1ps[1][0:1, :], onesc_bf[:, :],
                                  x2t[:, dc * LC:(dc + 1) * LC],
                                  start=(dc == 0), stop=(dc == 1))
                mm.then_inc(c1["stmm"].h, 1)

            O("tensor", _stmm)
            c1["stmm"].bump()

            W("scalar", c1["stmm"])
            W("scalar", c1["ar1w"], 16 * ci)
            def _stev(s):
                s.copy(out=statb[:, 0:LC], in_=p1ps[0][0:1, :]) \
                    .then_inc(c1["bkA"].h, 1)
                s.copy(out=statb[:, LC:2 * LC], in_=p1ps[1][0:1, :]) \
                    .then_inc(c1["bkB"].h, 1)

            O("scalar", _stev)
            c1["bkA"].bump()
            c1["bkB"].bump()
            W("sync", c1["bkB"], 3 * ci + 1)
            dma("sync", ar1_in.ap()[ci], statb[:, :], c1["ar1w"])
            if "stp" in tap_h:
                dma("sync", tap_h["stp"].ap()[ci], statb[:, :], c_tap)
            W("gpsimd", c1["ar1w"])
            coll_chain()
            O("gpsimd", lambda g, ci=ci: g.collective_compute(
                "AllReduce", ALU.add, replica_groups=[list(range(8))],
                ins=[ar1_in.ap()[ci].opt()], outs=[ar1_out.ap()[ci].opt()],
            ).then_inc(c1["ar1"].h, 1))
            c1["ar1"].bump()
            coll_mark(c1["ar1"])
            W("sync", c1["ar1"])
            W("sync", c1["rows_s"], ci)
            W("sync", c1["ar1w"], 16 * (ci + 1))
            dma("sync", statb[:, :], ar1_out.ap()[ci], c1["ar1r"])

            if "stats" in tap_h:
                W("sync", c1["ar1r"])
                dma("sync", tap_h["stats"].ap()[ci], statb[:, :], c_tap)

            W("scalar", c1["ar1r"])
            W("scalar", c1["rrep"], 2 * ci)
            if "rows" in tap_h:
                W("scalar", c_tap)

            def _rows_s(s):
                s.mul(out=rows[:, 0:LC].bitcast(F32R), in_=statb[:, 0:LC],
                      mul=INVD)
                s.mul(out=rows[:, 2 * LC:3 * LC].bitcast(F32R),
                      in_=statb[:, LC:2 * LC],
                      mul=INVD).then_inc(c1["rows_s"].h, 1)

            O("scalar", _rows_s)
            c1["rows_s"].bump()
            if "rows" in tap_h:
                pass
            W("vector", c1["rows_s"])

            def _rows_v(v):
                v.tensor_mul(out=rows[:, LC:2 * LC].bitcast(F32R),
                             in0=rows[:, 0:LC], in1=rows[:, 0:LC])
                v.tensor_sub(out=rows[:, 2 * LC:3 * LC].bitcast(F32R),
                             in0=rows[:, 2 * LC:3 * LC],
                             in1=rows[:, LC:2 * LC])
                v.tensor_single_scalar(out=rows[:, 2 * LC:3 * LC].bitcast(F32R),
                                       in_=rows[:, 2 * LC:3 * LC],
                                       scalar=EPS, op=ALU.add).then_inc(c1["rows"].h, 1)

            O("vector", _rows_v)
            c1["rows"].bump()
            W("scalar", c1["rows"])

            def _rows_s2(s):
                s.activation(out=rows[:, LC:2 * LC].bitcast(F32R),
                             in_=rows[:, 2 * LC:3 * LC], func=AF.Ln)
                s.activation(out=rows[:, LC:2 * LC].bitcast(F32R),
                             in_=rows[:, LC:2 * LC],
                             func=AF.Exp, scale=-0.5).then_inc(c1["rows"].h, 1)

            O("scalar", _rows_s2)
            c1["rows"].bump()
            W("vector", c1["rows"])
            O("vector", lambda v: v.tensor_mul(
                out=rows[:, 2 * LC:3 * LC].bitcast(F32R), in0=rows[:, 0:LC],
                in1=rows[:, LC:2 * LC]).then_inc(c1["rows"].h, 1))
            c1["rows"].bump()
            vrows = c1["rows"].n

            if "rows" in tap_h:
                W("sync", c1["rows"], vrows)
                dma("sync", tap_h["rows"].ap()[ci], rows[:, :], c_tap)

            W("tensor", c1["rows"], vrows)
            W("tensor", c1["bkB"], 3 * ci + 1)
            O("tensor", lambda t: t.matmul(
                p1ps[1][:, :], onesr[:, :].bitcast(F32R),
                rows[:, LC:2 * LC].bitcast(F32R), start=True, stop=True)
                .then_inc(c1["rrep"].h, 1))
            c1["rrep"].bump()
            v_rrep = c1["rrep"].n

            W("vector", cS_modp)
            W("vector", c1["rrep"], v_rrep)
            W("vector", c1["stmm"])

            def _h1a(v, rc=rc, b=b):
                for dc in range(2):
                    a = v.scalar_tensor_tensor(
                        out=xs[rc][:, dc * LCH + 3:(dc + 1) * LCH].bitcast(F32R),
                        in0=xs[rc][:, dc * LCH + 3:(dc + 1) * LCH],
                        scalar=ca1pp[:, (b * 2 + dc):(b * 2 + dc) + 1],
                        in1=p1ps[1][:, :], op0=ALU.mult, op1=ALU.mult)
                a.then_inc(c1["bkB"].h, 1)

            O("vector", _h1a)
            c1["bkB"].bump()

            if "h1m" in tap_h:
                W("scalar", c1["bkB"], 3 * ci + 2)
                dma("scalar",
                    tap_h["h1m"].ap()[ci].rearrange("(dc p) l -> p dc l", p=128),
                    xs[rc][:, :].rearrange("p (dc l3) -> p dc l3", dc=2)[:, :, 3:LCH],
                    c_tap)
                W("vector", c_tap)

            W("tensor", c1["bkB"], 3 * ci + 2)
            O("tensor", lambda t: t.matmul(
                p1ps[1][:, :], onesr[:, :].bitcast(F32R),
                rows[:, 2 * LC:3 * LC].bitcast(F32R), start=True, stop=True)
                .then_inc(c1["rrep"].h, 1))
            c1["rrep"].bump()
            v_mrep = c1["rrep"].n

            if "cst" in tap_h:
                W("sync", c1["rrep"], v_rrep)
                dma("sync", tap_h["cst"].ap()[ci:ci + 1, :], onesr[0:1, :], c_tap)

            if "ncaw" in tap_h and ci == 16:
                _ncaw_dump(1)

            W("vector", c1["rrep"], v_mrep)

            def _h1b(v, rc=rc, b=b, j=j):
                for dc in range(2):
                    a = v.scalar_tensor_tensor(
                        out=xs[rc][:, dc * LCH + 3:(dc + 1) * LCH].bitcast(F32R),
                        in0=p1ps[1][:, :],
                        scalar=nca1pp[:, (b * 2 + dc):(b * 2 + dc) + 1],
                        in1=xs[rc][:, dc * LCH + 3:(dc + 1) * LCH],
                        op0=ALU.mult, op1=ALU.add)
                a.then_inc(c1["bkB"].h, 1)
                for dc in range(2):
                    if j == 0:
                        a = v.memset(xs[rc][:, dc * LCH:dc * LCH + 3], 0.0)
                    else:
                        a = v.tensor_copy(
                            out=xs[rc][:, dc * LCH:dc * LCH + 3].bitcast(F32R),
                            in_=halo[:, (b * 2 + dc) * 3:(b * 2 + dc + 1) * 3])
                a.then_inc(c1["h1b"].h, 1)

            O("vector", _h1b)
            c1["bkB"].bump()
            c1["h1b"].bump()

            W("scalar", c1["bkB"], 3 * ci + 3)
            W("scalar", cS_modp)

            def _h1c(s, rc=rc, b=b):
                for dc in range(2):
                    a = s.activation(
                        out=xs[rc][:, dc * LCH + 3:(dc + 1) * LCH].bitcast(F32R),
                        in_=xs[rc][:, dc * LCH + 3:(dc + 1) * LCH],
                        func=AF.Identity,
                        bias=modsl[:, dc * NB + b:dc * NB + b + 1])
                a.then_inc(c1["h1a"].h, 1)

            O("scalar", _h1c)
            c1["h1a"].bump()

            W("vector", c1["h1a"])

            def _hsv(v, rc=rc, b=b):
                for dc in range(2):
                    a = v.tensor_copy(
                        out=halo[:, (b * 2 + dc) * 3:(b * 2 + dc + 1) * 3],
                        in_=xs[rc][:, dc * LCH + LC:(dc + 1) * LCH])
                a.then_inc(c1["hsv"].h, 1)

            O("vector", _hsv)
            c1["hsv"].bump()

            if "h1" in tap_h:
                W("scalar", c1["h1a"])
                dma("scalar", tap_h["h1"].ap()[ci].rearrange("(dc p) l -> p dc l", p=128),
                    xs[rc][:, :].rearrange("p (dc l3) -> p dc l3", dc=2)[:, :, 3:LCH],
                    c_tap)

            W("tensor", c1["h1a"])
            W("tensor", c1["h1b"])
            W("tensor", c1["bkA"], 4 * ci + 1)
            W("scalar", c1["brf"], 32 * ci)

            def _conv0(t, rc=rc):
                for k in range(4):
                    mm = t.matmul(
                        p1ps[0][:, :],
                        convd[:, k * 128:(k + 1) * 128].bitcast(F32R),
                        xs[rc][:, k:k + LC].bitcast(F32R),
                        start=(k == 0), stop=(k == 3))
                mm.then_inc(c1["u"].h, 1)

            O("tensor", _conv0)
            c1["u"].bump()
            W("scalar", c1["u"])
            W("scalar", c1["ywr"], 16 * ci)

            def _uev0(s, rc=rc):
                s.activation(out=ut[:, 0:LC].bitcast(F32R), in_=p1ps[0][:, :],
                             func=AF.Identity, bias=convb_t[:, 0:1])
                a = s.activation(out=yt[0][:, 0:LC], in_=p1ps[0][:, :],
                                 func=AF.Identity, scale=dskip_t[:, 0:1],
                                 bias=cbds_t[:, 0:1])
                a.then_inc(c1["uev"].h, 1)

            O("scalar", _uev0)
            c1["uev"].bump()
            W("tensor", c1["uev"])
            W("tensor", c1["bkA"], 4 * ci + 1)

            def _conv1(t, rc=rc):
                for k in range(4):
                    mm = t.matmul(
                        p1ps[0][:, :],
                        convd[:, (4 + k) * 128:(5 + k) * 128].bitcast(F32R),
                        xs[rc][:, LCH + k:LCH + k + LC].bitcast(F32R),
                        start=(k == 0), stop=(k == 3))
                mm.then_inc(c1["u"].h, 1)

            O("tensor", _conv1)
            c1["u"].bump()
            W("scalar", c1["u"])

            def _uev1(s, rc=rc):
                s.activation(out=ut[:, LC:2 * LC].bitcast(F32R), in_=p1ps[0][:, :],
                             func=AF.Identity, bias=convb_t[:, 1:2])
                a = s.activation(out=yt[0][:, LC:2 * LC], in_=p1ps[0][:, :],
                                 func=AF.Identity, scale=dskip_t[:, 1:2],
                                 bias=cbds_t[:, 1:2])
                a.then_inc(c1["uev"].h, 1)

            O("scalar", _uev1)
            c1["uev"].bump()
            v_uev = c1["uev"].n

            if "u" in tap_h:
                W("scalar", c1["uev"], v_uev)
                dma("scalar", tap_h["u"].ap()[ci].rearrange("(dc p) l -> p dc l", p=128),
                    ut[:, :].rearrange("p (dc l) -> p dc l", dc=2), c_tap)

            W("tensor", c1["uev"], v_uev)
            W("tensor", c1["bkA"], 4 * ci + 1)
            W("tensor", c1["bkB"], 3 * ci + 3)

            def _xproj(t):
                for dc in range(2):
                    t.matmul(p1ps[0][0:128, :],
                             xprojT_t[:, dc * 160:dc * 160 + 128].bitcast(F32R),
                             ut[:, dc * LC:(dc + 1) * LC].bitcast(F32R),
                             start=(dc == 0), stop=(dc == 1))
                for dc in range(2):
                    mm = t.matmul(p1ps[1][0:32, :],
                                  xprojT_t[:, dc * 160 + 128:(dc + 1) * 160]
                                  .bitcast(F32R),
                                  ut[:, dc * LC:(dc + 1) * LC].bitcast(F32R),
                                  start=(dc == 0), stop=(dc == 1))
                mm.then_inc(c1["xp"].h, 1)

            O("tensor", _xproj)
            c1["xp"].bump()

            W("scalar", c1["xp"])
            W("scalar", c1["ar2w"], 32 * ci)

            def _xpev(s):
                s.copy(out=xpa_sb[:, :], in_=p1ps[0][0:128, :]) \
                    .then_inc(c1["bkA"].h, 1)
                s.copy(out=xpb_sb[:, :], in_=p1ps[1][0:32, :]) \
                    .then_inc(c1["xpe"].h, 1)

            O("scalar", _xpev)
            c1["xpe"].bump()
            c1["bkA"].bump()

            W("sync", c1["xpe"])
            dma("sync", ar2_in.ap()[ci][0:128, :], xpa_sb[:, :], c1["ar2w"])
            dma("sync", ar2_in.ap()[ci][128:160, :], xpb_sb[:, :], c1["ar2w"])
            W("gpsimd", c1["ar2w"])
            coll_chain()
            O("gpsimd", lambda g, ci=ci: g.collective_compute(
                "AllReduce", ALU.add, replica_groups=[list(range(8))],
                ins=[ar2_in.ap()[ci].opt()], outs=[ar2_out.ap()[ci].opt()],
            ).then_inc(c1["ar2"].h, 1))
            c1["ar2"].bump()
            coll_mark(c1["ar2"])

            W("gpsimd", c1["ar2"])
            W("gpsimd", c1["dp"], 2 * ci)
            dma("gpsimd", dtt[:, :].bitcast(F32R), ar2_out.ap()[ci][0:128, :],
                c1["ar2r"])
            var2r = c1["ar2r"].n
            W("gpsimd", c1["ar2"])
            W("gpsimd", c1["crep"], 16 * ci)
            dma("gpsimd", xdblr_bf.ap()[ci], ar2_out.ap()[ci][128:160, :], c1["bcc"])
            W("sync", c1["bcc"])
            dma("sync", bct[:, :], xdblr_bf.ap()[ci], c1["ar2r"])
            var2rb = c1["ar2r"].n

            if "xdbl" in tap_h:
                W("sync", c1["ar2r"], var2r)
                dma("sync", tap_h["xdbl"].ap()[ci][0:128, :], dtt[:, :], c_tap)

            for dc in range(2):
                W("tensor", c1["ar2r"], var2r)
                W("tensor", c1["bkA"], 4 * ci + 2 + dc)
                O("tensor", lambda t, dc=dc: t.matmul(
                    p1ps[0][:, :],
                    dtprojT_t[:, dc * 128:(dc + 1) * 128].bitcast(F32R),
                    dtt[:, :].bitcast(F32R), start=True,
                    stop=True).then_inc(c1["dp"].h, 1))
                c1["dp"].bump()
                W("scalar", c1["dp"])
                W("scalar", c1["dA"], 32 * ci)
                W("scalar", c1["spg"], ci)

                def _sp(s, dc=dc):
                    sl_ = slice(dc * LC, (dc + 1) * LC)
                    s.activation(out=azt[:, sl_], in_=p1ps[0][:, :], func=AF.Abs,
                                 bias=dtb_t[:, dc:dc + 1])
                    a = s.activation(out=rlt[:, sl_], in_=p1ps[0][:, :],
                                     func=AF.Relu, bias=dtb_t[:, dc:dc + 1])
                    a.then_inc(c1["bkA"].h, 1)
                    s.activation(out=azt[:, sl_], in_=azt[:, sl_], func=AF.Exp,
                                 scale=-1.0)
                    a2 = s.activation(out=azt[:, sl_], in_=azt[:, sl_],
                                      func=AF.Ln, bias=1.0)
                    a2.then_inc(c1["spa"].h, 1)

                O("scalar", _sp)
                c1["spa"].bump()
                c1["bkA"].bump()

            W("gpsimd", c1["spa"])
            W("gpsimd", c1["dA"], 32 * ci)
            W("gpsimd", c1["brf"], 32 * ci)
            W("gpsimd", c1["xp"])
            W("gpsimd", c1["uev"], v_uev)

            def _del(g):
                g.tensor_add(out=azt[:, :], in0=azt[:, :], in1=rlt[:, :])
                g.tensor_mul(out=ut[:, :].bitcast(F32R), in0=azt[:, :],
                             in1=ut[:, :]).then_inc(c1["spg"].h, 1)

            O("gpsimd", _del)
            c1["spg"].bump()

            if "delta" in tap_h:
                W("scalar", c1["spg"])
                dma("scalar", tap_h["delta"].ap()[ci].rearrange("(dc p) l -> p dc l", p=128),
                    azt[:, :].rearrange("p (dc l) -> p dc l", dc=2), c_tap)

            for n in range(NST):
                W("tensor", c1["ar2r"], var2rb)
                W("tensor", c1["brf"], 32 * ci + 2 * n)
                O("tensor", lambda t, n=n: t.matmul(
                    repps[0][:, :], selmat[:, n * 128:(n + 1) * 128],
                    bct[:, :], start=True, stop=True).then_inc(c1["brep"].h, 1))
                c1["brep"].bump()
                W("tensor", c1["yc"], 32 * ci + 2 * n)
                O("tensor", lambda t, n=n: t.matmul(
                    repps[1][:, :], selmat[:, (16 + n) * 128:(17 + n) * 128],
                    bct[:, :], start=True, stop=True).then_inc(c1["crep"].h, 1))
                c1["crep"].bump()

                for dc in range(2):
                    scol = (b * 2 + dc) * NST + n
                    kA = k_dA
                    k_dA += 1
                    W("scalar", c1["spg"])
                    W("scalar", c1["dAf"], kA - 1)
                    if kA == 0:
                        W("vector", cS_modmm, v_wmod_done)
                        W("scalar", cS_acf)
                    O("scalar", lambda s, dc=dc, n=n, kA=kA: s.activation(
                        out=dAt[kA % 2][:, :],
                        in_=azt[:, dc * LC:(dc + 1) * LC],
                        func=AF.Exp,
                        scale=acoef[:, dc * NST + n:dc * NST + n + 1])
                        .then_inc(c1["dA"].h, 1))
                    c1["dA"].bump()

                    kY = k_yc
                    k_yc += 1
                    W("vector", c1["spg"])
                    W("vector", c1["brep"])
                    W("vector", c1["dA"])
                    W("vector", c1["ycf"], kY - 1)
                    W("vector", c1["crep"])

                    def _scan(v, dc=dc, kA=kA, kY=kY, j=j, scol=scol):
                        hx = kY % 2
                        v.tensor_mul(out=yct[hx][:, :].bitcast(F32R),
                                     in0=ut[:, dc * LC:(dc + 1) * LC],
                                     in1=repps[0][:, :]) \
                            .then_inc(c1["brf"].h, 1)
                        ini = 0.0 if j == 0 else state[:, scol:scol + 1]
                        v.tensor_tensor_scan(out=htraj[hx][:, :],
                                             data0=dAt[kA % 2][:, :],
                                             data1=yct[hx][:, :], initial=ini,
                                             op0=ALU.mult, op1=ALU.add) \
                            .then_inc(c1["dAf"].h, 1)
                        v.tensor_mul(out=yct[hx][:, :].bitcast(F32R),
                                     in0=htraj[hx][:, :],
                                     in1=repps[1][:, :]) \
                            .then_inc(c1["yc"].h, 1)
                        v.tensor_copy(out=state[:, scol:scol + 1],
                                      in_=htraj[hx][:, LC - 1:LC])

                    O("vector", _scan)
                    c1["dAf"].bump()
                    c1["yc"].bump()
                    c1["brf"].bump()

                    W("gpsimd", c1["yc"])
                    W("gpsimd", c1["uev"], v_uev)

                    def _yadd(g, dc=dc, kY=kY):
                        g.tensor_add(out=yt[0][:, dc * LC:(dc + 1) * LC],
                                     in0=yt[0][:, dc * LC:(dc + 1) * LC],
                                     in1=yct[kY % 2][:, :]) \
                            .then_inc(c1["ycf"].h, 1)

                    O("gpsimd", _yadd)
                    c1["ycf"].bump()

            g_ = j % 4
            dest = 2 * b + (0 if j < 4 else 1)
            W("sync", c1["ycf"], 32 * (ci + 1))
            dma("sync", a2a_in[g_].ap()[dest].rearrange("(dc p) l -> p dc l", p=128),
                yt[0][:, :].rearrange("p (dc l) -> p dc l", dc=2), c1["ywr"])
            if "y" in tap_h:
                dma("sync", tap_h["y"].ap()[ci].rearrange("(dc p) l -> p dc l", p=128),
                    yt[0][:, :].rearrange("p (dc l) -> p dc l", dc=2), c_tap)

        if j >= 4:
            g_ = j - 4
            W("gpsimd", c1["ywr"])
            coll_chain()

            def _a2a(g, g_=g_):
                g.collective_compute(
                    "AllToAll", ALU.bypass, replica_groups=[list(range(8))],
                    ins=[a2a_in[g_].ap().opt()], outs=[a2a_out[g_].ap().opt()],
                ).then_inc(c1_a2a[g_].h, 1)

            O("gpsimd", _a2a)
            c1_a2a[g_].bump()
            coll_mark(c1_a2a[g_])

    c_out = ch("c_out")
    if not skip_p3:
        c3 = {k: ch("c3_" + k) for k in (
            "yx", "x1", "x1w", "stp", "st", "rows_s", "rows", "upf",
            "x1r", "h2", "lg", "lgev", "t2", "sig", "sc", "wup", "wupf",
            "wdn", "wdnf", "bdw", "up", "act", "acts", "dn", "dnev", "er",
            "res")}

        k_up = [0]
        up_rel = []
        up_ring_rel = []

        def upps_acq():
            k = k_up[0]
            k_up[0] += 1
            if k >= 2:
                ch_, v_ = up_rel[k - 2]
                assert (ch_, v_) != (None, None), f"usage {k - 2} never released"
                W("tensor", ch_, v_)
            up_rel.append((None, None))
            return k

        def upps_tile(k):
            return upps[k % 2]

        def upps_mark(k, ch_, v_=None):
            assert up_rel[k] == (None, None)
            up_rel[k] = (ch_, ch_.n if v_ is None else v_)

        for q in range(NQ):
            if tap_h:
                for _e in ("sync", "scalar", "vector", "gpsimd", "tensor"):
                    W(_e, c_tap)
            qr = q % 2
            a2flat = a2a_out[q].ap().rearrange("s d l -> (s d) l").rearrange("(gg p) l -> gg p l", p=128)
            xpflat = xp3.ap().rearrange("(gg p) l -> gg p l", p=128)
            x1flat = x1_d.ap().rearrange("(gg p) l -> gg p l", p=128)
            outflat = out_h.ap().rearrange("(gg p) l -> gg p l", p=128)

            k_st = upps_acq()
            k_st2 = upps_acq()
            st_bank = upps_tile(k_st)
            st_bank2 = upps_tile(k_st2)
            for dcg in range(16):
                gs = dcg % 2
                idx = q * 16 + dcg
                W("sync", c1_a2a[q])
                W("sync", c3["x1"], idx - 1)
                dma("sync", yq[:, gs * LC:(gs + 1) * LC], a2flat[dcg], c3["yx"])
                W("sync", c3["x1w"], 16 * (idx - 1))
                W("sync", c3["stp"], idx - 1)
                dma("sync", x1q[:, gs * LC:(gs + 1) * LC].bitcast(F32R),
                    xpflat[dcg][:, q * LC:(q + 1) * LC], c3["yx"])
                vyx = c3["yx"].n
                W("vector", c3["yx"], vyx)
                W("vector", cS_modp)
                W("vector", c3["stp"], idx - 1)

                def _x1(v, dcg=dcg, gs=gs):
                    sl_ = slice(gs * LC, (gs + 1) * LC)
                    v.scalar_tensor_tensor(out=x1q[:, sl_].bitcast(F32R),
                                           in0=yq[:, sl_],
                                           scalar=gapp[:, dcg:dcg + 1],
                                           in1=x1q[:, sl_],
                                           op0=ALU.mult, op1=ALU.add)
                    v.tensor_mul(out=x1h[:, sl_].bitcast(F32R), in0=x1q[:, sl_],
                                 in1=x1q[:, sl_]).then_inc(c3["x1"].h, 1)

                O("vector", _x1)
                c3["x1"].bump()
                W("sync", c3["x1"])
                dma("sync", x1flat[dcg][:, q * LC:(q + 1) * LC],
                    x1q[:, gs * LC:(gs + 1) * LC], c3["x1w"])
                if "x1" in tap_h:
                    dma("sync", tap_h["x1"].ap().rearrange(
                        "(gg p) l -> gg p l", p=128)[dcg][:, q * LC:(q + 1) * LC],
                        x1q[:, gs * LC:(gs + 1) * LC], c_tap)

                W("tensor", c3["x1"])

                def _st3(t, dcg=dcg, gs=gs, bank=st_bank, bank2=st_bank2):
                    sl_ = slice(gs * LC, (gs + 1) * LC)
                    t.matmul(bank[0:1, :], onesc[:, :].bitcast(F32R),
                             x1q[:, sl_].bitcast(F32R), start=(dcg == 0),
                             stop=(dcg == 15))
                    mm = t.matmul(bank2[0:1, :], onesc[:, :].bitcast(F32R),
                                  x1h[:, sl_].bitcast(F32R), start=(dcg == 0),
                                  stop=(dcg == 15))
                    mm.then_inc(c3["stp"].h, 1)

                O("tensor", _st3)
                c3["stp"].bump()

            W("scalar", c3["stp"], 16 * (q + 1))

            def _r3a(s, bank=st_bank, bank2=st_bank2):
                s.copy(out=stat3[:, 0:LC], in_=bank[0:1, :])
                s.copy(out=stat3[:, LC:2 * LC], in_=bank2[0:1, :])
                s.mul(out=rows3[:, 0:LC].bitcast(F32R),
                      in_=stat3[:, 0:LC], mul=INVD)
                s.mul(out=rows3[:, 2 * LC:3 * LC].bitcast(F32R),
                      in_=stat3[:, LC:2 * LC], mul=INVD) \
                    .then_inc(c3["rows_s"].h, 1)

            O("scalar", _r3a)
            c3["rows_s"].bump()
            upps_mark(k_st, c3["rows_s"])
            upps_mark(k_st2, c3["rows_s"])
            W("vector", c3["rows_s"])

            def _r3c(v):
                v.tensor_mul(out=rows3[:, LC:2 * LC].bitcast(F32R),
                             in0=rows3[:, 0:LC], in1=rows3[:, 0:LC])
                v.tensor_sub(out=rows3[:, 2 * LC:3 * LC].bitcast(F32R),
                             in0=rows3[:, 2 * LC:3 * LC],
                             in1=rows3[:, LC:2 * LC])
                v.tensor_single_scalar(out=rows3[:, 2 * LC:3 * LC].bitcast(F32R),
                                       in_=rows3[:, 2 * LC:3 * LC], scalar=EPS,
                                       op=ALU.add).then_inc(c3["rows"].h, 1)

            O("vector", _r3c)
            c3["rows"].bump()
            W("scalar", c3["rows"])

            def _r3d(s):
                s.activation(out=rows3[:, LC:2 * LC].bitcast(F32R),
                             in_=rows3[:, 2 * LC:3 * LC], func=AF.Ln)
                s.activation(out=rows3[:, LC:2 * LC].bitcast(F32R),
                             in_=rows3[:, LC:2 * LC],
                             func=AF.Exp, scale=-0.5).then_inc(c3["rows"].h, 1)

            O("scalar", _r3d)
            c3["rows"].bump()
            W("vector", c3["rows"])
            O("vector", lambda v: v.tensor_mul(
                out=rows3[:, 2 * LC:3 * LC].bitcast(F32R), in0=rows3[:, 0:LC],
                in1=rows3[:, LC:2 * LC]).then_inc(c3["rows"].h, 1))
            c3["rows"].bump()
            v3rows = c3["rows"].n

            k_rr = upps_acq()
            rrep_bank = upps_tile(k_rr)
            W("tensor", c3["rows"], v3rows)
            O("tensor", lambda t, bank=rrep_bank: t.matmul(
                bank[:, :], onesr[:, :].bitcast(F32R),
                rows3[:, LC:2 * LC].bitcast(F32R), start=True, stop=True))
            k_mr = upps_acq()
            mrep_bank = upps_tile(k_mr)
            O("tensor", lambda t, bank=mrep_bank: t.matmul(
                bank[:, :], onesr[:, :].bitcast(F32R),
                rows3[:, 2 * LC:3 * LC].bitcast(F32R), start=True,
                stop=True).then_inc(c3["rows"].h, 1))
            c3["rows"].bump()
            v3rep = c3["rows"].n

            for dcg in range(16):
                idx = q * 16 + dcg
                W("sync", c3["x1w"], 16 * (idx + 1))
                W("sync", c3["h2"], idx)
                dma("sync", x1r[:, :],
                    x1flat[dcg][:, q * LC:(q + 1) * LC], c3["x1r"])
                W("vector", c3["x1r"])
                W("vector", c3["rows"], v3rep)
                W("vector", cS_modp)

                def _h2(v, dcg=dcg):
                    v.scalar_tensor_tensor(out=x1r[:, :], in0=x1r[:, :],
                                           scalar=cb1pp[:, dcg:dcg + 1],
                                           in1=rrep_bank[:, :],
                                           op0=ALU.mult, op1=ALU.mult)
                    v.scalar_tensor_tensor(out=x1r[:, :], in0=mrep_bank[:, :],
                                           scalar=ncb1pp[:, dcg:dcg + 1],
                                           in1=x1r[:, :], op0=ALU.mult,
                                           op1=ALU.add).then_inc(c3["x1r"].h, 1)

                O("vector", _h2)
                c3["x1r"].bump()
                W("scalar", c3["x1r"])
                if q >= 1:
                    W("scalar", c3["up"], 72 * q)
                    W("scalar", c3["lg"], 2 * q)

                def _h2b(s, dcg=dcg):
                    s.activation(out=h2q[:, dcg * LC:(dcg + 1) * LC],
                                 in_=x1r[:, :], func=AF.Identity,
                                 bias=sbpp[:, dcg:dcg + 1]).then_inc(c3["h2"].h, 1)

                O("scalar", _h2b)
                c3["h2"].bump()
            v_h2q = c3["h2"].n
            upps_mark(k_rr, c3["x1r"])
            upps_mark(k_mr, c3["x1r"])

            if "h2" in tap_h:
                for dcg in range(16):
                    W("scalar", c3["h2"], v_h2q)
                    dma("scalar", tap_h["h2"].ap().rearrange(
                        "(gg p) l -> gg p l", p=128)[dcg][:, q * LC:(q + 1) * LC],
                        h2q[:, dcg * LC:(dcg + 1) * LC], c_tap)

            k_lg = upps_acq()
            lg_bank = upps_tile(k_lg)
            W("tensor", c3["h2"], v_h2q)
            W("tensor", cS_gl)

            def _lg1(t, bank=lg_bank):
                for kc in range(16):
                    mm = t.matmul(bank[0:8, :], wg1_t[:, kc * 8:(kc + 1) * 8],
                                  h2q[:, kc * LC:(kc + 1) * LC],
                                  start=(kc == 0), stop=(kc == 15))
                mm.then_inc(c3["lg"].h, 1)

            O("tensor", _lg1)
            c3["lg"].bump()
            W("scalar", c3["lg"])

            def _lgev1(s, bank=lg_bank):
                s.activation(out=th1[:, :], in_=bank[0:8, :], func=AF.Tanh) \
                    .then_inc(c3["lgev"].h, 1)

            O("scalar", _lgev1)
            c3["lgev"].bump()
            upps_mark(k_lg, c3["lgev"])
            k_lg2 = upps_acq()
            lg2_bank = upps_tile(k_lg2)
            W("tensor", c3["lgev"])
            O("tensor", lambda t, bank=lg2_bank: t.matmul(
                bank[0:8, :], wg2_t[:, :], th1[:, :], start=True,
                stop=True).then_inc(c3["lg"].h, 1))
            c3["lg"].bump()
            W("scalar", c3["lg"])

            def _lgev2(s, bank=lg2_bank):
                s.copy(out=lgt[:, :], in_=bank[0:8, :]) \
                    .then_inc(c3["lgev"].h, 1)

            O("scalar", _lgev2)
            c3["lgev"].bump()
            upps_mark(k_lg2, c3["lgev"])
            v_lgt = c3["lgev"].n

            for tc in range(4):
                k_tp = upps_acq()
                tp_bank = upps_tile(k_tp)
                W("tensor", c3["lgev"], v_lgt)
                W("tensor", cS_const)
                O("tensor", lambda t, tc=tc, bank=tp_bank: t.transpose(
                    bank[0:128, 0:8], lgt[:, tc * 128:(tc + 1) * 128],
                    eqm[0:8, 0:8]).then_inc(c3["t2"].h, 1))
                c3["t2"].bump()
                W("vector", c3["t2"])

                def _top2(v, bank=tp_bank):
                    c = tsc

                    def pad():
                        v.engine_nop()
                        v.engine_nop()
                        v.engine_nop()
                    v.tensor_copy(out=tl[:, :], in_=bank[0:128, 0:8])
                    pad()
                    v.tensor_reduce(out=c[:, 16:17], in_=tl[:, :], axis=AX.X,
                                    op=ALU.max)
                    pad()
                    v.tensor_single_scalar(out=msk8[:, 0:8], in_=tl[:, :],
                                           scalar=c[:, 16:17], op=ALU.is_equal)
                    pad()
                    v.select(out=c[:, 32:40], mask=msk8[:, 0:8],
                             on_true=iota8_t[:, :], on_false=c8_t[:, :])
                    pad()
                    v.tensor_reduce(out=c[:, 17:18], in_=c[:, 32:40], axis=AX.X,
                                    op=ALU.min)
                    pad()
                    v.tensor_single_scalar(out=c[:, 40:48], in_=iota8_t[:, :],
                                           scalar=c[:, 17:18], op=ALU.is_equal)
                    v.tensor_single_scalar(out=msk8[:, 8:16], in_=iota8_t[:, :],
                                           scalar=c[:, 17:18], op=ALU.is_equal)
                    pad()
                    v.select(out=c[:, 48:56], mask=msk8[:, 8:16],
                             on_true=negbig_t[:, :], on_false=tl[:, :])
                    pad()
                    v.tensor_reduce(out=c[:, 18:19], in_=c[:, 48:56], axis=AX.X,
                                    op=ALU.max)
                    pad()
                    v.tensor_single_scalar(out=msk8[:, 16:24], in_=c[:, 48:56],
                                           scalar=c[:, 18:19], op=ALU.is_equal)
                    pad()
                    v.select(out=c[:, 64:72], mask=msk8[:, 16:24],
                             on_true=iota8_t[:, :], on_false=c8_t[:, :])
                    pad()
                    v.tensor_reduce(out=c[:, 19:20], in_=c[:, 64:72], axis=AX.X,
                                    op=ALU.min)
                    pad()
                    v.tensor_single_scalar(out=c[:, 72:80], in_=iota8_t[:, :],
                                           scalar=c[:, 19:20], op=ALU.is_equal)
                    v.tensor_sub(out=c[:, 20:21], in0=c[:, 16:17],
                                 in1=c[:, 18:19]).then_inc(c3["t2"].h, 1)

                O("vector", _top2)
                c3["t2"].bump()
                W("scalar", c3["t2"])

                def _sig(s):
                    s.activation(out=tsc[:, 22:23], in_=tsc[:, 20:21],
                                 func=AF.Sigmoid)
                    s.activation(out=tsc[:, 23:24], in_=tsc[:, 20:21],
                                 func=AF.Sigmoid, scale=-1.0).then_inc(c3["sig"].h, 1)

                O("scalar", _sig)
                c3["sig"].bump()
                W("vector", c3["sig"])

                def _sc2(v, bank=tp_bank):
                    v.tensor_single_scalar(out=tsc[:, 80:88], in_=tsc[:, 40:48],
                                           scalar=tsc[:, 22:23], op=ALU.mult)
                    v.engine_nop()
                    v.engine_nop()
                    v.engine_nop()
                    v.scalar_tensor_tensor(out=tsc[:, 80:88], in0=tsc[:, 72:80],
                                           scalar=tsc[:, 23:24],
                                           in1=tsc[:, 80:88],
                                           op0=ALU.mult, op1=ALU.add) \
                        .then_inc(c3["t2"].h, 1)

                O("vector", _sc2)
                c3["t2"].bump()
                upps_mark(k_tp, c3["t2"])
                k_tb = upps_acq()
                tb_bank = upps_tile(k_tb)
                W("tensor", c3["t2"])
                O("tensor", lambda t, bank=tb_bank: t.transpose(
                    bank[0:8, 0:128], tsc[:, 80:88],
                    eqm[:, :]).then_inc(c3["t2"].h, 1))
                c3["t2"].bump()
                W("scalar", c3["t2"])
                if q >= 2:
                    W("scalar", c3["wdnf"], 64 * (q - 1))

                def _scev(s, tc=tc, qr=qr, bank=tb_bank):
                    s.copy(out=scoreq[qr][:, tc * 128:(tc + 1) * 128]
                           .bitcast(F32R), in_=bank[0:8, 0:128]) \
                        .then_inc(c3["sc"].h, 1)

                O("scalar", _scev)
                c3["sc"].bump()
                upps_mark(k_tb, c3["sc"])
            v_sc = c3["sc"].n

            if "scores" in tap_h:
                W("scalar", c3["sc"], v_sc)
                dma("scalar", tap_h["scores"].ap()[:, q * LC:(q + 1) * LC],
                    scoreq[qr][:, :], c_tap)

            for mk in (up_rel[-2], up_rel[-1]):
                W("tensor", mk[0], mk[1])
            for ei in range(E):
                sr_bank = upps[0]
                W("tensor", c3["sc"], v_sc)
                if ei >= 1:
                    W("tensor", c3["acts"], c3["acts"].n)
                O("tensor", lambda t, ei=ei, qr=qr, bank=sr_bank: t.matmul(
                    bank[:, :], sel8[:, ei * 128:(ei + 1) * 128].bitcast(F32R),
                    scoreq[qr][:, :].bitcast(F32R), start=True,
                    stop=True).then_inc(c3["up"].h, 1))
                c3["up"].bump()
                v_srep = c3["up"].n
                for mc in range(8):
                    gidx = q * 64 + ei * 8 + mc
                    if gidx >= 2:
                        W("sync", c3["up"], up_ring_rel[gidx - 2])
                    dma("sync", wup_ring[gidx % 2][:, :].rearrange(
                        "p (kc f) -> p kc f", kc=16),
                        wup_bf.ap()[ei, mc].rearrange("kc p f -> p kc f"), c3["wup"])
                    vw = c3["wup"].n
                    ub_bank = upps[1]
                    W("tensor", c3["act"], c3["act"].n)
                    W("tensor", c3["wup"], vw)
                    W("tensor", c3["h2"], v_h2q)

                    def _up(t, gidx=gidx, bank=ub_bank):
                        for kc in range(16):
                            mm = t.matmul(
                                bank[:, :],
                                wup_ring[gidx % 2][:, kc * 128:(kc + 1) * 128],
                                h2q[:, kc * LC:(kc + 1) * LC],
                                start=(kc == 0), stop=(kc == 15))
                        mm.then_inc(c3["up"].h, 1)

                    O("tensor", _up)
                    c3["up"].bump()
                    up_ring_rel.append(c3["up"].n)
                    assert len(up_ring_rel) == gidx + 1
                    W("scalar", c3["up"])
                    if q >= 1:
                        W("scalar", c3["wdnf"], 64 * q)

                    def _silu(s, ei=ei, mc=mc, bank=ub_bank):
                        s.activation(
                            out=act_t[:, (ei * 8 + mc) * LC:(ei * 8 + mc + 1) * LC],
                            in_=bank[:, :], func=AF.Silu,
                            bias=bup_t[:, ei * 8 + mc:ei * 8 + mc + 1]) \
                            .then_inc(c3["act"].h, 1)

                    O("scalar", _silu)
                    c3["act"].bump()
                    W("vector", c3["act"])
                    W("vector", c3["up"], v_srep)

                    def _ascale(v, ei=ei, mc=mc, bank=sr_bank):
                        v.tensor_mul(
                            out=act_t[:, (ei * 8 + mc) * LC:(ei * 8 + mc + 1) * LC],
                            in0=act_t[:, (ei * 8 + mc) * LC:(ei * 8 + mc + 1) * LC],
                            in1=bank[:, :]).then_inc(c3["acts"].h, 1)

                    O("vector", _ascale)
                    c3["acts"].bump()
            k_syn0 = upps_acq()
            k_syn1 = upps_acq()
            upps_mark(k_syn0, c3["acts"])
            upps_mark(k_syn1, c3["act"])
            v_acts = c3["acts"].n

            for og in range(8):
                widx = q * 8 + og
                W("sync", c3["wdnf"], 8 * (widx - 1))
                dma("sync", bdg[widx % 2][:, :].bitcast(F32R),
                    bdown_h.ap()[:, og * 256:(og + 1) * 256], c3["bdw"])
                vbd = c3["bdw"].n
                W("tensor", c3["bdw"], vbd)
                W("tensor", c3["acts"], v_acts)
                W("tensor", c3["sc"], v_sc)
                W("tensor", c3["dnev"], 2 * widx)

                def _dstart(t, qr=qr, widx=widx):
                    for ocl in range(2):
                        t.matmul(dnps[ocl][:, :],
                                 bdg[widx % 2][:, ocl * 128:(ocl + 1) * 128]
                                 .bitcast(F32R),
                                 scoreq[qr][:, :].bitcast(F32R),
                                 start=True, stop=False)

                O("tensor", _dstart)
                for ei in range(E):
                    widx2 = widx * 8 + ei
                    W("sync", c3["wdnf"], widx2 - 1)
                    dma("sync", wdn_ring[widx2 % 2][:, :].rearrange(
                        "p (ocl hc f) -> p ocl hc f", ocl=2, f=128),
                        wdown_bf.ap()[og, ei].rearrange("ocl hc p f -> p ocl hc f"),
                        c3["wdn"])
                    vw = c3["wdn"].n
                    W("tensor", c3["wdn"], vw)

                    def _down(t, ei=ei, widx2=widx2):
                        ring = wdn_ring[widx2 % 2]
                        for ocl in range(2):
                            for hc in range(8):
                                tix = ocl * 8 + hc
                                mm = t.matmul(
                                    dnps[ocl][:, :],
                                    ring[:, tix * 128:(tix + 1) * 128],
                                    act_t[:, (ei * 8 + hc) * LC:(ei * 8 + hc + 1) * LC],
                                    start=False,
                                    stop=(ei == E - 1 and hc == 7))
                                if hc == 7 and ocl == 1:
                                    mm.then_inc(c3["wdnf"].h, 1)

                    O("tensor", _down)
                    c3["wdnf"].bump()

                for ocl in range(2):
                    oc = og * 2 + ocl
                    kres = 2 * widx + ocl
                    W("sync", c3["x1w"])
                    W("sync", c3["res"], kres)
                    dma("sync", x1e[0][:, :],
                        x1flat[oc][:, q * LC:(q + 1) * LC], c3["er"])
                    if "x1e" in tap_h and q == 0:
                        W("sync", c3["er"])
                        dma("sync", tap_h["x1e"].ap()[oc], x1e[0][:, :], c_tap)
                    W("vector", c3["er"])
                    W("vector", c3["wdnf"], 8 * (widx + 1))
                    W("vector", c_out, 16 * kres)

                    def _res(v, ocl=ocl, oc=oc):
                        v.scalar_tensor_tensor(out=evt[0][:, :],
                                               in0=dnps[ocl][:, :],
                                               scalar=gbpp[:, oc:oc + 1],
                                               in1=x1e[0][:, :],
                                               op0=ALU.mult, op1=ALU.add) \
                            .then_inc(c3["dnev"].h, 1)
                        v.tensor_copy(out=evt[0][:, :], in_=evt[0][:, :]) \
                            .then_inc(c3["res"].h, 1)

                    O("vector", _res)
                    c3["dnev"].bump()
                    c3["res"].bump()
                    W("sync", c3["res"])
                    if "evt" in tap_h and q == 0:
                        dma("sync", tap_h["evt"].ap()[oc], evt[0][:, :], c_tap)
                        W("sync", c_tap)
                    dma("sync", outflat[oc][:, q * LC:(q + 1) * LC],
                        evt[0][:, :], c_out)

    if "ncaw" in tap_h:
        if skip_p3:
            for g_ in range(NQ):
                W("sync", c1_a2a[g_])
            W("sync", c1["ywr"])
        else:
            W("sync", c_out)
        _ncaw_dump(2)

    if "cmod" in tap_h:
        if skip_p3:
            for g_ in range(NQ):
                W("sync", c1_a2a[g_])
            W("sync", c1["ywr"])
        else:
            W("sync", c_out)
        dma("sync", tap_h["cmod"].ap()[:, 0:48], modsl[:, :], c_tap)
        dma("sync", tap_h["cmod"].ap()[:, 48:56], ca1pp[:, :], c_tap)
        dma("sync", tap_h["cmod"].ap()[:, 56:64], nca1pp[:, :], c_tap)

    W("scalar", c_out)
    if "winf" in tap_h:
        W("sync", c_out)
        _win_dump(1)
    if tap_h:
        W("scalar", c_tap)
    if skip_p3:
        for g_ in range(NQ):
            W("scalar", c1_a2a[g_])
        W("scalar", c1["ywr"])

    block = e(nc.Block())
    with allow_ici_all_to_all():
        P.finish(block)
    stk.close()
    return nc


def kernel(**inputs):
    in_maps = stage_inputs(inputs)
    nc = build_nc(taps=tuple(DEBUG_TAPS), skip_p3=SKIP_P3)
    with allow_ici_all_to_all():
        res = run_bass_kernel_spmd(nc, in_maps, core_ids=list(range(8)))
    out = np.empty((NB, L, D), dtype=np.float32)
    for c in range(8):
        bc, hc3 = c // 2, c % 2
        out[bc, hc3 * TOK3:(hc3 + 1) * TOK3, :] = res.results[c]["out"].T
    kernel.last_results = res.results
    return out


# version 04 (final)
# speedup vs baseline: 1.0023x; 1.0023x over previous
import numpy as np
from contextlib import ExitStack

import concourse.bass as bass
import concourse.mybir as mybir
from concourse.bass_utils import run_bass_kernel_spmd
from concourse.aot_env import allow_ici_all_to_all

F32 = mybir.dt.float32
F32R = mybir.dt.float32r
BF16 = mybir.dt.bfloat16
AF = mybir.ActivationFunctionType
ALU = mybir.AluOpType
AX = mybir.AxisListType

D, DSL, NB, L, NST, RK, E, HE = 2048, 256, 4, 4096, 16, 128, 8, 1024
T = NB * L
LC = 512
LCH = LC + 3
NJ = L // LC
NCH = NB * NJ
TOK3 = 2048
NQ = 4
EPS = 1e-5
INVD = 1.0 / D

DEBUG_TAPS = []
SKIP_P3 = False


def _r(x):
    return np.ascontiguousarray(x)


def stage_inputs(inputs):
    x = np.asarray(inputs["L0_attn_x"], dtype=np.float32)
    x_dmaj = _r(x.transpose(0, 2, 1))
    conv_w = np.asarray(inputs["L0_attn_conv_w"], np.float32)[:, 0, :]
    conv_b = np.asarray(inputs["L0_attn_conv_b"], np.float32)
    xprojT = _r(np.asarray(inputs["L0_attn_xproj_w"], np.float32).T)
    dtprojT = _r(np.asarray(inputs["L0_attn_dtproj_w"], np.float32).T)
    dtproj_b = np.asarray(inputs["L0_attn_dtproj_b"], np.float32)
    A_log = np.asarray(inputs["L0_attn_A_log"], np.float32)
    D_skip = np.asarray(inputs["L0_attn_D_skip"], np.float32)
    Wg1T = _r(np.asarray(inputs["L0_ffn_Wg1"], np.float32).T)
    Wg2T = _r(np.asarray(inputs["L0_ffn_Wg2"], np.float32).T)
    Wup = np.asarray(inputs["L0_ffn_Wup"], np.float32)
    bup = np.asarray(inputs["L0_ffn_bup"], np.float32)
    Wdown = np.asarray(inputs["L0_ffn_Wdown"], np.float32)
    bdown = np.asarray(inputs["L0_ffn_bdown"], np.float32)
    cc = np.asarray(inputs["L0_c"], np.float32)
    wmod = np.asarray(inputs["L0_wmod"], np.float32)

    WupT = Wup.transpose(0, 2, 1)
    wup_tiles = _r(WupT.reshape(E, 16, 128, 8, 128).transpose(0, 3, 1, 2, 4))
    WdownT = Wdown.transpose(0, 2, 1)
    a = WdownT.reshape(E, 8, 128, 16, 128).transpose(3, 0, 1, 2, 4)
    wdown_tiles = _r(a.reshape(8, 2, E, 8, 128, 128).transpose(0, 2, 1, 3, 4, 5))
    wmod_r = wmod.reshape(D, 6, 8, DSL)
    cT = _r(cc.T)

    in_maps = []
    for c in range(8):
        sl = slice(DSL * c, DSL * (c + 1))
        bc, hc3 = c // 2, c % 2
        bsel = np.zeros((128, NB), np.float32)
        bsel[:, bc] = 1.0
        in_maps.append({
            "xp1": _r(x_dmaj[:, sl, :]),
            "xp3": _r(x_dmaj[bc, :, hc3 * TOK3:(hc3 + 1) * TOK3]),
            "cT": cT,
            "wmod_sl": _r(wmod_r[:, :, c, :].reshape(D, 6 * DSL)),
            "xprojT_sl": _r(xprojT[sl, :]),
            "dtprojT_sl": _r(dtprojT[:, sl]),
            "dtproj_b_sl": _r(dtproj_b[sl].reshape(2, 128).T),
            "conv_w_sl": _r(conv_w[sl].reshape(2, 128, 4).transpose(1, 0, 2)
                            .reshape(128, 8)),
            "conv_b_sl": _r(conv_b[sl].reshape(2, 128).T),
            "A_log_sl": _r(A_log[sl].reshape(2, 128, NST).transpose(1, 0, 2)
                           .reshape(128, 2 * NST)),
            "D_skip_sl": _r(D_skip[sl].reshape(2, 128).T),
            "Wg1T_t": _r(Wg1T.reshape(16, 128, 8)),
            "Wg2T_t": Wg2T,
            "wup_tiles": wup_tiles,
            "wdown_tiles": wdown_tiles,
            "bup_t": _r(bup.reshape(E, 8, 128).transpose(2, 0, 1).reshape(128, E * 8)),
            "bdown_t": bdown,
            "iota8": _r(np.tile(np.arange(8, dtype=np.float32), (128, 1))),
            "iotap": _r(np.arange(128, dtype=np.float32).reshape(128, 1)),
            "iotap32": _r(np.arange(32, dtype=np.float32).reshape(32, 1)),
            "bsel": bsel,
        })
    return in_maps


class Ch:
    def __init__(self, stk, nc, name):
        self.h = stk.enter_context(nc.semaphore(name))
        self.n = 0

    def bump(self, k=1):
        self.n += k
        return self.n


class Prog:
    ENGS = ("sync", "scalar", "vector", "gpsimd", "tensor")

    def __init__(self):
        self.th = {e: [] for e in self.ENGS}
        self.waited = {e: {} for e in self.ENGS}

    def on(self, eng, fn):
        self.th[eng].append(fn)

    def wait(self, eng, ch, val=None):
        v = ch.n if val is None else val
        if v <= 0:
            return
        if self.waited[eng].get(id(ch), -1) >= v:
            return
        self.waited[eng][id(ch)] = v
        self.on(eng, lambda e, h=ch.h, v=v: e.wait_ge(h, v))

    def finish(self, block):
        for name in self.ENGS:
            fns = self.th[name]
            if fns:
                getattr(block, name)(lambda e, fns=fns: [f(e) for f in fns])


def build_nc(taps=(), skip_p3=False):
    nc = bass.Bass("TRN2", target_bir_lowering=False,
                   dynamic_dma_scratch_size=10240)
    stk = ExitStack()
    e = stk.enter_context

    def dram(name, shape, dtype=F32, kind="Internal", addr_space="Local"):
        return nc.dram_tensor(name, list(shape), dtype, kind=kind,
                              addr_space=addr_space)

    xp1 = dram("xp1", [NB, DSL, L], F32R, kind="ExternalInput")
    xp3 = dram("xp3", [D, TOK3], F32R, kind="ExternalInput")
    cT_h = dram("cT", [D, NB], kind="ExternalInput")
    wmod_h = dram("wmod_sl", [D, 1536], F32R, kind="ExternalInput")
    xprojT_h = dram("xprojT_sl", [DSL, 160], F32R, kind="ExternalInput")
    dtprojT_h = dram("dtprojT_sl", [RK, DSL], F32R, kind="ExternalInput")
    dtb_h = dram("dtproj_b_sl", [128, 2], kind="ExternalInput")
    convw_h = dram("conv_w_sl", [128, 8], kind="ExternalInput")
    convb_h = dram("conv_b_sl", [128, 2], kind="ExternalInput")
    alog_h = dram("A_log_sl", [128, 2 * NST], kind="ExternalInput")
    dskip_h = dram("D_skip_sl", [128, 2], kind="ExternalInput")
    wg1_h = dram("Wg1T_t", [16, 128, 8], kind="ExternalInput")
    wg2_h = dram("Wg2T_t", [8, 8], kind="ExternalInput")
    wup_h = dram("wup_tiles", [E, 8, 16, 128, 128], kind="ExternalInput")
    wdown_h = dram("wdown_tiles", [8, E, 2, 8, 128, 128], kind="ExternalInput")
    bup_h = dram("bup_t", [128, E * 8], kind="ExternalInput")
    bdown_h = dram("bdown_t", [8, D], F32R, kind="ExternalInput")
    iota8_h = dram("iota8", [128, 8], kind="ExternalInput")
    iotap_h = dram("iotap", [128, 1], kind="ExternalInput")
    iotap32_h = dram("iotap32", [32, 1], kind="ExternalInput")
    bsel_h = dram("bsel", [128, NB], kind="ExternalInput")
    out_h = dram("out", [D, TOK3], kind="ExternalOutput")

    tap_h = {}
    for tname, shape in (("y", [NCH, DSL, LC]), ("h1", [NCH, DSL, LC]),
                         ("u", [NCH, DSL, LC]), ("delta", [NCH, DSL, LC]),
                         ("xdbl", [NCH, 160, LC]), ("mod", [128, 48]),
                         ("scores", [8, TOK3]), ("x1", [D, TOK3]),
                         ("stp", [NCH, 2, LC]), ("stats", [NCH, 2, LC]),
                         ("rows", [NCH, 1, 3 * LC]), ("h1m", [NCH, DSL, LC]),
                         ("cst", [NCH, 128]), ("cmod", [128, 64]),
                         ("ncaw", [3, 128, 32]), ("x1e", [16, 128, LC]),
                         ("evt", [16, 128, LC])):
        if tname in taps:
            tap_h[tname] = dram("tap_" + tname, shape, kind="ExternalOutput")
    if "h2" in taps:
        tap_h["h2"] = dram("tap_h2", [D, TOK3], BF16, kind="ExternalOutput")
    if "win" in taps:
        tap_h["winf"] = dram("tap_winf", [2, 2912], kind="ExternalOutput")
        tap_h["winb"] = dram("tap_winb", [2, 4096], BF16, kind="ExternalOutput")

    wup_bf = dram("wup_bf", [E, 8, 16, 128, 128], BF16)
    wdown_bf = dram("wdown_bf", [8, E, 2, 8, 128, 128], BF16)
    wg1_bf = dram("wg1_bf", [16, 128, 8], BF16)
    wg2_bf = dram("wg2_bf", [8, 8], BF16)
    xdblr_bf = dram("xdblr_bf", [NCH, 32, LC], BF16)
    ar1_in = dram("ar1_in", [NCH, 2, LC])
    ar1_out = dram("ar1_out", [NCH, 2, LC], addr_space="Shared")
    ar2_in = dram("ar2_in", [NCH, 160, LC])
    ar2_out = dram("ar2_out", [NCH, 160, LC], addr_space="Shared")
    ag_in = dram("ag_in", [1536, NB])
    ag_out = dram("ag_out", [8 * 1536, NB], addr_space="Shared")
    a2a_in = [dram(f"a2a_in{g}", [8, DSL, LC]) for g in range(NQ)]
    a2a_out = [dram(f"a2a_out{g}", [8, DSL, LC]) for g in range(NQ)]
    x1_d = dram("x1_d", [D, TOK3])

    def sb(name, p, f, dtype=F32):
        return e(nc.sbuf_tensor(name, [p, f], dtype))

    eqm = sb("eqm", 128, 128)
    onesr = sb("onesr", 1, 128)
    onesc = sb("onesc", 128, 1)
    onesc_bf = sb("onesc_bf", 128, 1, BF16)
    selmat = sb("selmat", 32, 32 * 128, BF16)
    sel8 = sb("sel8", 8, 8 * 128)
    convd = sb("convd", 128, 8 * 128)
    xprojT_t = sb("xprojT_t", 128, 2 * 160)
    dtprojT_t = sb("dtprojT_t", 128, 2 * 128)
    acoef = sb("acoef", 128, 2 * NST)
    convw_t = sb("convw_t", 128, 8)
    convb_t = sb("convb_t", 128, 2)
    dskip_t = sb("dskip_t", 128, 2)
    cbds_t = sb("cbds_t", 128, 2)
    dtb_t = sb("dtb_t", 128, 2)
    can0 = sb("can0", 128, 8)
    ca1pp = sb("ca1pp", 128, 8)
    nca1pp = sb("nca1pp", 128, 8)
    can1 = sb("can1", 128, 8)
    gapp = sb("gapp", 128, 16)
    gbpp = sb("gbpp", 128, 16)
    sbpp = sb("sbpp", 128, 16)
    cb1pp = sb("cb1pp", 128, 16)
    ncb1pp = sb("ncb1pp", 128, 16)
    gall = sb("gall", 128, 4 * 64)
    msc = sb("msc", 128, 16)
    msc2 = sb("msc2", 128, 16)
    bsel_t = sb("bsel_t", 128, NB)
    wg1_t = sb("wg1_t", 128, 16 * 8, BF16)
    wg2_t = sb("wg2_t", 8, 8, BF16)
    bup_t = sb("bup_t_s", 128, E * 8)
    iota8_t = sb("iota8_t", 128, 8)
    c8_t = sb("c8_t", 128, 8)
    negbig_t = sb("negbig_t", 128, 8)
    iotap_t = sb("iotap_t", 128, 1)
    iotap32_t = sb("iotap32_t", 32, 1)
    ct_t = sb("ct_t", 128, 16 * NB)
    sct_t = sb("sct_t", 128, 16 * NB)
    modsl = sb("modsl", 128, 12 * NB)

    xs = [sb(f"xs{i}", 128, 2 * LCH) for i in range(2)]
    x2t = sb("x2t", 128, 2 * LC, BF16)
    statb = sb("statb", 1, 2 * LC)
    rows = sb("rows", 1, 3 * LC)
    ut = sb("ut", 128, 2 * LC)
    xpa_sb = sb("xpa_sb", 128, LC)
    xpb_sb = sb("xpb_sb", 32, LC)
    dtt = sb("dtt", 128, LC)
    bct = sb("bct", 32, LC, BF16)
    azt = sb("azt", 128, 2 * LC)
    rlt = sb("rlt", 128, 2 * LC)
    dAt = [sb(f"dAt{i}", 128, LC) for i in range(2)]
    htraj = [sb(f"htraj{i}", 128, LC) for i in range(2)]
    yct = [sb(f"yct{i}", 128, LC) for i in range(2)]
    yt = [sb("yt0", 128, 2 * LC)]
    state = sb("state", 128, 128)
    halo = sb("halo", 128, 8 * 3)

    if not skip_p3:
        h2q = sb("h2q", 128, 16 * LC, BF16)
        act_t = sb("act_t", 128, E * 8 * LC, BF16)
        yq = sb("yq", 128, 2 * LC)
        x1q = sb("x1q", 128, 2 * LC)
        x1h = sb("x1h", 128, 2 * LC)
        rows3 = sb("rows3", 1, 3 * LC)
        stat3 = sb("stat3", 1, 2 * LC)
        th1 = sb("th1", 8, LC, BF16)
        lgt = sb("lgt", 8, LC)
        tl = sb("tl", 128, 8)
        tsc = sb("tsc", 128, 12 * 8)
        msk8 = sb("msk8", 128, 3 * 8, mybir.dt.uint8)
        scoreq = [sb(f"scoreq{i}", 8, LC) for i in range(2)]
        wup_ring = [sb(f"wupr{i}", 128, 16 * 128, BF16) for i in range(2)]
        wdn_ring = [sb(f"wdnr{i}", 128, 16 * 128, BF16) for i in range(2)]
        bdg = [sb(f"bdg{i}", 8, 2 * 128) for i in range(2)]
        evt = [sb("evt0", 128, LC)]
        x1r = sb("x1r", 128, LC)
        x1e = [sb("x1e0", 128, LC)]

    p1ps = [e(nc.psum_tensor(f"p1ps{i}", [128, 512], F32)) for i in range(2)]
    repps = [e(nc.psum_tensor(f"repps{i}", [128, 512], F32)) for i in range(2)]
    upps = [e(nc.psum_tensor(f"upps{i}", [128, 512], F32)) for i in range(2)]
    dnps = [e(nc.psum_tensor(f"dnps{i}", [128, 512], F32)) for i in range(2)]

    chn = {}

    def ch(name):
        if name not in chn:
            chn[name] = Ch(stk, nc, name)
        return chn[name]

    P = Prog()
    W = P.wait
    O = P.on
    last_coll = [None]

    def coll_chain():
        if last_coll[0] is not None:
            ch_, v_ = last_coll[0]
            W("gpsimd", ch_, v_)

    def coll_mark(ch_):
        last_coll[0] = (ch_, ch_.n)

    def dma(eng, out, in_, c):
        c.bump(16)
        O(eng, lambda en, o=out, i=in_, h=c.h: en.dma_start(out=o, in_=i).then_inc(h, 16))

    cS_ld = ch("cS_ld")
    dma("sync", iota8_t[:, :], iota8_h.ap(), cS_ld)
    dma("sync", iotap_t[:, :], iotap_h.ap(), cS_ld)
    dma("sync", iotap32_t[:, :], iotap32_h.ap(), cS_ld)
    dma("sync", bsel_t[:, :], bsel_h.ap(), cS_ld)
    dma("sync", xprojT_t[:, :].rearrange("p (dc f) -> p dc f", dc=2)
        .bitcast(F32R),
        xprojT_h.ap().rearrange("(dc p) f -> p dc f", p=128), cS_ld)
    dma("sync", dtprojT_t[:, :].rearrange("p (dc f) -> p dc f", dc=2)
        .bitcast(F32R),
        dtprojT_h.ap().rearrange("p (dc f) -> p dc f", dc=2), cS_ld)
    dma("sync", acoef[:, :], alog_h.ap(), cS_ld)
    dma("sync", convw_t[:, :], convw_h.ap(), cS_ld)
    dma("sync", convb_t[:, :], convb_h.ap(), cS_ld)
    dma("sync", dskip_t[:, :], dskip_h.ap(), cS_ld)
    dma("sync", dtb_t[:, :], dtb_h.ap(), cS_ld)
    dma("sync", ct_t[:, :].rearrange("p (kc b) -> p kc b", kc=16),
        cT_h.ap().rearrange("(kc p) b -> p kc b", p=128), cS_ld)
    dma("sync", bup_t[:, :], bup_h.ap(), cS_ld)

    cS_cast = ch("cS_cast")
    for ei in range(E):
        dma("gpsimd", wup_bf.ap()[ei], wup_h.ap()[ei], cS_cast)
        dma("gpsimd", wdown_bf.ap()[:, ei], wdown_h.ap()[:, ei], cS_cast)
    dma("gpsimd", wg1_bf.ap(), wg1_h.ap(), cS_cast)
    dma("gpsimd", wg2_bf.ap(), wg2_h.ap(), cS_cast)
    cS_gl = ch("cS_gl")
    W("sync", cS_cast)
    dma("sync", wg1_t[:, :].rearrange("p (kc f) -> p kc f", kc=16),
        wg1_bf.ap().rearrange("kc p f -> p kc f"), cS_gl)
    dma("sync", wg2_t[:, :], wg2_bf.ap(), cS_gl)

    cS_const = ch("cS_const")

    def _vconst(v):
        v.wait_ge(cS_ld.h, cS_ld.n)
        v.memset(onesr[:, :], 1.0)
        v.memset(onesc[:, :], 1.0)
        v.memset(onesc_bf[:, :], 1.0)
        v.tensor_copy(out=onesr[:, :].bitcast(F32R), in_=onesr[:, :])
        v.tensor_copy(out=onesc[:, :].bitcast(F32R), in_=onesc[:, :])
        v.memset(can0[:, :], 99.0)
        v.memset(can1[:, :], 99.0)
        v.memset(c8_t[:, :], 8.0)
        v.memset(negbig_t[:, :], -1e30)
        v.memset(state[:, :], 0.0)
        for n in range(128):
            v.memset(eqm[:, n:n + 1], float(n))
        v.tensor_single_scalar(out=eqm[:, :], in_=eqm[:, :],
                               scalar=iotap_t[:, :], op=ALU.is_equal)
        for n in range(32):
            v.memset(selmat[:, n * 128:(n + 1) * 128], float(n))
        v.tensor_single_scalar(out=selmat[:, :], in_=selmat[:, :],
                               scalar=iotap32_t[:, :], op=ALU.is_equal)
        for n in range(8):
            v.memset(sel8[:, n * 128:(n + 1) * 128], float(n))
        v.tensor_single_scalar(out=sel8[:, :].bitcast(F32R), in_=sel8[:, :],
                               scalar=iotap_t[0:8, :], op=ALU.is_equal)
        for dc in range(2):
            for k in range(4):
                ix = dc * 4 + k
                v.tensor_single_scalar(
                    out=convd[:, ix * 128:(ix + 1) * 128].bitcast(F32R),
                    in_=eqm[:, :], scalar=convw_t[:, ix:ix + 1], op=ALU.mult)
        v.tensor_mul(out=cbds_t[:, :], in0=convb_t[:, :],
                     in1=dskip_t[:, :]).then_inc(cS_const.h, 1)

    O("vector", _vconst)
    cS_const.bump()

    cS_sc = ch("cS_sc")

    def _ssetup(s):
        s.wait_ge(cS_ld.h, cS_ld.n)
        s.activation(out=acoef[:, :], in_=acoef[:, :], func=AF.Exp)
        s.activation(out=sct_t[:, :].bitcast(F32R), in_=ct_t[:, :],
                     func=AF.Silu).then_inc(cS_sc.h, 1)

    O("scalar", _ssetup)
    cS_sc.bump()

    cS_acf = ch("cS_acf")
    W("vector", cS_sc)
    O("vector", lambda v: v.tensor_single_scalar(
        out=acoef[:, :], in_=acoef[:, :], scalar=-1.0,
        op=ALU.mult).then_inc(cS_acf.h, 1))
    cS_acf.bump()

    cS_wmod = ch("cS_wmod")
    cS_modmm = ch("cS_modmm")
    cS_modev = ch("cS_modev")
    kwg = 0
    for mt in range(12):
        for grp in range(4):
            rb = kwg % 2
            W("sync", cS_modmm, kwg - 1)
            dma("sync", yct[rb][:, :].rearrange("p (kc f) -> p kc f", kc=4)
                .bitcast(F32R),
                wmod_h.ap().rearrange("(kc p) (m f) -> m p kc f", p=128, f=128)[
                    mt][:, grp * 4:(grp + 1) * 4], cS_wmod)
            W("tensor", cS_wmod)
            W("tensor", cS_sc)
            if grp == 0:
                W("tensor", cS_modev, mt - 1)

            def _modmm(t, mt=mt, grp=grp, rb=rb):
                for kc in range(4):
                    mm = t.matmul(p1ps[mt % 2][0:128, 0:NB],
                                  yct[rb][:, kc * 128:(kc + 1) * 128].bitcast(F32R),
                                  sct_t[:, (grp * 4 + kc) * NB:(grp * 4 + kc + 1) * NB]
                                  .bitcast(F32R),
                                  start=(grp == 0 and kc == 0),
                                  stop=(grp == 3 and kc == 3))
                mm.then_inc(cS_modmm.h, 1)

            O("tensor", _modmm)
            cS_modmm.bump()
            kwg += 1
        W("scalar", cS_modmm)
        O("scalar", lambda s, mt=mt: s.copy(
            out=modsl[:, mt * NB:(mt + 1) * NB],
            in_=p1ps[mt % 2][0:128, 0:NB]).then_inc(cS_modev.h, 1))
        cS_modev.bump()
    v_wmod_done = cS_modmm.n

    c_tap = ch("c_tap")
    if "mod" in tap_h:
        W("scalar", cS_modev)
        dma("scalar", tap_h["mod"].ap(), modsl[:, :], c_tap)

    cS_agw = ch("cS_agw")
    W("scalar", cS_modev)
    dma("scalar", ag_in.ap().rearrange("(m p) b -> p m b", p=128),
        modsl[:, :].rearrange("p (m b) -> p m b", m=12), cS_agw)
    cS_ag = ch("cS_ag")

    coll_chain()

    def _ag(g):
        g.wait_ge(cS_agw.h, cS_agw.n)
        g.collective_compute(
            "AllGather", ALU.bypass, replica_groups=[list(range(8))],
            ins=[ag_in.ap().opt()], outs=[ag_out.ap().opt()],
        ).then_inc(cS_ag.h, 1)

    O("gpsimd", _ag)
    cS_ag.bump()
    coll_mark(cS_ag)

    cS_agr = ch("cS_agr")
    W("sync", cS_ag)
    agv = ag_out.ap()
    for pj in range(4):
        j_ = pj + 2
        for dcg in range(16):
            row0 = (dcg // 2) * 1536 + j_ * DSL + (dcg % 2) * 128
            dma("sync", gall[:, (pj * 16 + dcg) * 4:(pj * 16 + dcg + 1) * 4],
                agv[row0:row0 + 128, :], cS_agr)

    cS_modp = ch("cS_modp")

    def _modp(v):
        v.wait_ge(cS_modev.h, cS_modev.n)
        v.wait_ge(cS_agr.h, cS_agr.n)
        ca_view = modsl[:, 2 * NB:4 * NB].rearrange("p (dc b) -> p b dc", dc=2)
        v.tensor_single_scalar(
            out=ca1pp[:, :].rearrange("p (b dc) -> p b dc", dc=2),
            in_=ca_view, scalar=1.0, op=ALU.add)
        v.tensor_scalar(
            out=nca1pp[:, :].rearrange("p (b dc) -> p b dc", dc=2),
            in0=ca_view, scalar1=-1.0,
            scalar2=-1.0, op0=ALU.mult, op1=ALU.add)
        dsts = (gapp, sbpp, cb1pp, gbpp)
        for pj, dst in enumerate(dsts):
            base = pj * 64
            v.tensor_single_scalar(out=dst[:, :], in_=gall[:, base:base + 64:4],
                                   scalar=bsel_t[:, 0:1], op=ALU.mult)
        for b in range(1, NB):
            for pj, dst in enumerate(dsts):
                base = pj * 64
                v.scalar_tensor_tensor(out=dst[:, :],
                                       in0=gall[:, base + b:base + 64:4],
                                       scalar=bsel_t[:, b:b + 1], in1=dst[:, :],
                                       op0=ALU.mult, op1=ALU.add)
        v.tensor_single_scalar(out=msc[:, :], in_=gapp[:, :], scalar=0.0,
                               op=ALU.is_ge)
        v.tensor_single_scalar(out=msc2[:, :], in_=gbpp[:, :], scalar=0.0,
                               op=ALU.is_ge)
        v.tensor_scalar(out=ncb1pp[:, :], in0=cb1pp[:, :], scalar1=-1.0,
                        scalar2=-1.0, op0=ALU.mult, op1=ALU.add)
        v.tensor_scalar(out=msc[:, :], in0=msc[:, :], scalar1=2.0, scalar2=-1.0,
                        op0=ALU.mult, op1=ALU.add)
        v.tensor_scalar(out=msc2[:, :], in0=msc2[:, :], scalar1=2.0,
                        scalar2=-1.0, op0=ALU.mult, op1=ALU.add)
        v.tensor_single_scalar(out=cb1pp[:, :], in_=cb1pp[:, :], scalar=1.0,
                               op=ALU.add)
        v.tensor_mul(out=gapp[:, :], in0=gapp[:, :], in1=msc[:, :])
        v.tensor_mul(out=gbpp[:, :], in0=gbpp[:, :], in1=msc2[:, :])
        v.tensor_single_scalar(out=gapp[:, :], in_=gapp[:, :], scalar=0.25,
                               op=ALU.max)
        v.tensor_single_scalar(out=gbpp[:, :], in_=gbpp[:, :], scalar=0.25,
                               op=ALU.max)
        v.tensor_mul(out=gapp[:, :], in0=gapp[:, :], in1=msc[:, :])
        v.tensor_mul(out=gbpp[:, :], in0=gbpp[:, :],
                     in1=msc2[:, :]).then_inc(cS_modp.h, 1)

    O("vector", _modp)
    cS_modp.bump()

    def _ncaw_dump(cap):
        dma("sync", tap_h["ncaw"].ap()[cap, :, 0:8], ca1pp[:, :], c_tap)
        dma("sync", tap_h["ncaw"].ap()[cap, :, 8:16], nca1pp[:, :], c_tap)
        dma("sync", tap_h["ncaw"].ap()[cap, :, 16:24], can0[:, :], c_tap)
        dma("sync", tap_h["ncaw"].ap()[cap, :, 24:32], can1[:, :], c_tap)

    if "ncaw" in tap_h:
        W("sync", cS_modp)
        _ncaw_dump(0)

    def _win_dump(row):
        offs = [0]
        for tile, n in ((eqm, 128), (onesr, 128), (sel8, 1024), (convd, 1024),
                        (xprojT_t, 320), (dtprojT_t, 256), (acoef, 32)):
            o = offs[-1]
            dma("sync", tap_h["winf"].ap()[row:row + 1, o:o + n],
                tile[0:1, 0:n], c_tap)
            offs.append(o + n)
        dma("sync", tap_h["winb"].ap()[row:row + 1, :], selmat[0:1, :], c_tap)

    if "winf" in tap_h:
        W("sync", cS_const)
        W("sync", cS_ld)
        _win_dump(0)

    c1 = {k: ch("c1_" + k) for k in (
        "x", "x2", "stmm", "stev", "ar1w", "ar1", "ar1r", "rows_s", "rows",
        "bkA", "bkB", "rrep", "h1a", "h1b", "hsv", "u", "uev", "xp", "xpe",
        "ar2w", "ar2", "ar2r", "bcc", "dp", "spa", "spg", "dbuN", "brep",
        "brf", "crep", "crf", "dA", "dAf", "yc", "ycf", "yg", "ywr")}
    c1_a2a = [ch(f"c1_a2a{g}") for g in range(NQ)]

    k_dA = 0
    k_yc = 0

    for j in range(NJ):
        for b in range(NB):
            ci = j * NB + b
            rc = ci % 2

            if tap_h:
                for _e in ("sync", "scalar", "vector", "gpsimd", "tensor"):
                    W(_e, c_tap)

            W("sync", c1["u"], 2 * (ci - 1))
            W("sync", c1["hsv"], ci - 1)
            dma("sync",
                xs[rc][:, :].rearrange("p (dc l3) -> p dc l3", dc=2)[
                    :, :, 3:LCH].bitcast(F32R),
                xp1.ap()[b].rearrange("(dc p) l -> p dc l", p=128)[
                    :, :, j * LC:(j + 1) * LC], c1["x"])
            vx = c1["x"].n

            W("vector", c1["x"], vx)
            W("vector", c1["stmm"], ci)
            O("vector", lambda v, rc=rc: v.tensor_mul(
                out=x2t[:, :].rearrange("p (dc l) -> p dc l", dc=2),
                in0=xs[rc][:, :].rearrange("p (dc l3) -> p dc l3", dc=2)[:, :, 3:LCH],
                in1=xs[rc][:, :].rearrange("p (dc l3) -> p dc l3", dc=2)[:, :, 3:LCH])
              .then_inc(c1["x2"].h, 1))
            c1["x2"].bump()

            W("tensor", c1["x"], vx)
            W("tensor", c1["x2"])
            W("tensor", cS_const)
            W("tensor", c1["bkA"], 4 * ci)
            W("tensor", c1["xpe"], ci)

            def _stmm(t, rc=rc):
                for dc in range(2):
                    t.matmul(p1ps[0][0:1, :], onesc[:, :].bitcast(F32R),
                             xs[rc][:, dc * LCH + 3:(dc + 1) * LCH].bitcast(F32R),
                             start=(dc == 0), stop=(dc == 1))
                for dc in range(2):
                    mm = t.matmul(p1ps[1][0:1, :], onesc_bf[:, :],
                                  x2t[:, dc * LC:(dc + 1) * LC],
                                  start=(dc == 0), stop=(dc == 1))
                mm.then_inc(c1["stmm"].h, 1)

            O("tensor", _stmm)
            c1["stmm"].bump()

            W("scalar", c1["stmm"])
            W("scalar", c1["ar1w"], 16 * ci)
            def _stev(s):
                s.copy(out=statb[:, 0:LC], in_=p1ps[0][0:1, :]) \
                    .then_inc(c1["bkA"].h, 1)
                s.copy(out=statb[:, LC:2 * LC], in_=p1ps[1][0:1, :]) \
                    .then_inc(c1["bkB"].h, 1)

            O("scalar", _stev)
            c1["bkA"].bump()
            c1["bkB"].bump()
            W("sync", c1["bkB"], 3 * ci + 1)
            dma("sync", ar1_in.ap()[ci], statb[:, :], c1["ar1w"])
            if "stp" in tap_h:
                dma("sync", tap_h["stp"].ap()[ci], statb[:, :], c_tap)
            W("gpsimd", c1["ar1w"])
            coll_chain()
            O("gpsimd", lambda g, ci=ci: g.collective_compute(
                "AllReduce", ALU.add, replica_groups=[list(range(8))],
                ins=[ar1_in.ap()[ci].opt()], outs=[ar1_out.ap()[ci].opt()],
            ).then_inc(c1["ar1"].h, 1))
            c1["ar1"].bump()
            coll_mark(c1["ar1"])
            W("sync", c1["ar1"])
            W("sync", c1["rows_s"], ci)
            W("sync", c1["ar1w"], 16 * (ci + 1))
            dma("sync", statb[:, :], ar1_out.ap()[ci], c1["ar1r"])

            if "stats" in tap_h:
                W("sync", c1["ar1r"])
                dma("sync", tap_h["stats"].ap()[ci], statb[:, :], c_tap)

            W("scalar", c1["ar1r"])
            W("scalar", c1["rrep"], 2 * ci)
            if "rows" in tap_h:
                W("scalar", c_tap)

            def _rows_s(s):
                s.mul(out=rows[:, 0:LC].bitcast(F32R), in_=statb[:, 0:LC],
                      mul=INVD)
                s.mul(out=rows[:, 2 * LC:3 * LC].bitcast(F32R),
                      in_=statb[:, LC:2 * LC],
                      mul=INVD).then_inc(c1["rows_s"].h, 1)

            O("scalar", _rows_s)
            c1["rows_s"].bump()
            if "rows" in tap_h:
                pass
            W("vector", c1["rows_s"])

            def _rows_v(v):
                v.tensor_mul(out=rows[:, LC:2 * LC].bitcast(F32R),
                             in0=rows[:, 0:LC], in1=rows[:, 0:LC])
                v.tensor_sub(out=rows[:, 2 * LC:3 * LC].bitcast(F32R),
                             in0=rows[:, 2 * LC:3 * LC],
                             in1=rows[:, LC:2 * LC])
                v.tensor_single_scalar(out=rows[:, 2 * LC:3 * LC].bitcast(F32R),
                                       in_=rows[:, 2 * LC:3 * LC],
                                       scalar=EPS, op=ALU.add).then_inc(c1["rows"].h, 1)

            O("vector", _rows_v)
            c1["rows"].bump()
            W("scalar", c1["rows"])

            def _rows_s2(s):
                s.activation(out=rows[:, LC:2 * LC].bitcast(F32R),
                             in_=rows[:, 2 * LC:3 * LC], func=AF.Ln)
                s.activation(out=rows[:, LC:2 * LC].bitcast(F32R),
                             in_=rows[:, LC:2 * LC],
                             func=AF.Exp, scale=-0.5).then_inc(c1["rows"].h, 1)

            O("scalar", _rows_s2)
            c1["rows"].bump()
            W("vector", c1["rows"])
            O("vector", lambda v: v.tensor_mul(
                out=rows[:, 2 * LC:3 * LC].bitcast(F32R), in0=rows[:, 0:LC],
                in1=rows[:, LC:2 * LC]).then_inc(c1["rows"].h, 1))
            c1["rows"].bump()
            vrows = c1["rows"].n

            if "rows" in tap_h:
                W("sync", c1["rows"], vrows)
                dma("sync", tap_h["rows"].ap()[ci], rows[:, :], c_tap)

            W("tensor", c1["rows"], vrows)
            W("tensor", c1["bkB"], 3 * ci + 1)
            O("tensor", lambda t: t.matmul(
                p1ps[1][:, :], onesr[:, :].bitcast(F32R),
                rows[:, LC:2 * LC].bitcast(F32R), start=True, stop=True)
                .then_inc(c1["rrep"].h, 1))
            c1["rrep"].bump()
            v_rrep = c1["rrep"].n

            W("vector", cS_modp)
            W("vector", c1["rrep"], v_rrep)
            W("vector", c1["stmm"])

            def _h1a(v, rc=rc, b=b):
                for dc in range(2):
                    a = v.scalar_tensor_tensor(
                        out=xs[rc][:, dc * LCH + 3:(dc + 1) * LCH].bitcast(F32R),
                        in0=xs[rc][:, dc * LCH + 3:(dc + 1) * LCH],
                        scalar=ca1pp[:, (b * 2 + dc):(b * 2 + dc) + 1],
                        in1=p1ps[1][:, :], op0=ALU.mult, op1=ALU.mult)
                a.then_inc(c1["bkB"].h, 1)

            O("vector", _h1a)
            c1["bkB"].bump()

            if "h1m" in tap_h:
                W("scalar", c1["bkB"], 3 * ci + 2)
                dma("scalar",
                    tap_h["h1m"].ap()[ci].rearrange("(dc p) l -> p dc l", p=128),
                    xs[rc][:, :].rearrange("p (dc l3) -> p dc l3", dc=2)[:, :, 3:LCH],
                    c_tap)
                W("vector", c_tap)

            W("tensor", c1["bkB"], 3 * ci + 2)
            O("tensor", lambda t: t.matmul(
                p1ps[1][:, :], onesr[:, :].bitcast(F32R),
                rows[:, 2 * LC:3 * LC].bitcast(F32R), start=True, stop=True)
                .then_inc(c1["rrep"].h, 1))
            c1["rrep"].bump()
            v_mrep = c1["rrep"].n

            if "cst" in tap_h:
                W("sync", c1["rrep"], v_rrep)
                dma("sync", tap_h["cst"].ap()[ci:ci + 1, :], onesr[0:1, :], c_tap)

            if "ncaw" in tap_h and ci == 16:
                _ncaw_dump(1)

            W("vector", c1["rrep"], v_mrep)

            def _h1b(v, rc=rc, b=b, j=j):
                for dc in range(2):
                    a = v.scalar_tensor_tensor(
                        out=xs[rc][:, dc * LCH + 3:(dc + 1) * LCH].bitcast(F32R),
                        in0=p1ps[1][:, :],
                        scalar=nca1pp[:, (b * 2 + dc):(b * 2 + dc) + 1],
                        in1=xs[rc][:, dc * LCH + 3:(dc + 1) * LCH],
                        op0=ALU.mult, op1=ALU.add)
                a.then_inc(c1["bkB"].h, 1)
                for dc in range(2):
                    if j == 0:
                        a = v.memset(xs[rc][:, dc * LCH:dc * LCH + 3], 0.0)
                    else:
                        a = v.tensor_copy(
                            out=xs[rc][:, dc * LCH:dc * LCH + 3].bitcast(F32R),
                            in_=halo[:, (b * 2 + dc) * 3:(b * 2 + dc + 1) * 3])
                a.then_inc(c1["h1b"].h, 1)

            O("vector", _h1b)
            c1["bkB"].bump()
            c1["h1b"].bump()

            W("scalar", c1["bkB"], 3 * ci + 3)
            W("scalar", cS_modp)

            def _h1c(s, rc=rc, b=b):
                for dc in range(2):
                    a = s.activation(
                        out=xs[rc][:, dc * LCH + 3:(dc + 1) * LCH].bitcast(F32R),
                        in_=xs[rc][:, dc * LCH + 3:(dc + 1) * LCH],
                        func=AF.Identity,
                        bias=modsl[:, dc * NB + b:dc * NB + b + 1])
                a.then_inc(c1["h1a"].h, 1)

            O("scalar", _h1c)
            c1["h1a"].bump()

            W("vector", c1["h1a"])

            def _hsv(v, rc=rc, b=b):
                for dc in range(2):
                    a = v.tensor_copy(
                        out=halo[:, (b * 2 + dc) * 3:(b * 2 + dc + 1) * 3],
                        in_=xs[rc][:, dc * LCH + LC:(dc + 1) * LCH])
                a.then_inc(c1["hsv"].h, 1)

            O("vector", _hsv)
            c1["hsv"].bump()

            if "h1" in tap_h:
                W("scalar", c1["h1a"])
                dma("scalar", tap_h["h1"].ap()[ci].rearrange("(dc p) l -> p dc l", p=128),
                    xs[rc][:, :].rearrange("p (dc l3) -> p dc l3", dc=2)[:, :, 3:LCH],
                    c_tap)

            W("tensor", c1["h1a"])
            W("tensor", c1["h1b"])
            W("tensor", c1["bkA"], 4 * ci + 1)
            W("scalar", c1["brf"], 32 * ci)

            def _conv0(t, rc=rc):
                for k in range(4):
                    mm = t.matmul(
                        p1ps[0][:, :],
                        convd[:, k * 128:(k + 1) * 128].bitcast(F32R),
                        xs[rc][:, k:k + LC].bitcast(F32R),
                        start=(k == 0), stop=(k == 3))
                mm.then_inc(c1["u"].h, 1)

            O("tensor", _conv0)
            c1["u"].bump()
            W("scalar", c1["u"])
            W("scalar", c1["ywr"], 16 * ci)

            def _uev0(s, rc=rc):
                s.activation(out=ut[:, 0:LC].bitcast(F32R), in_=p1ps[0][:, :],
                             func=AF.Identity, bias=convb_t[:, 0:1])
                a = s.activation(out=yt[0][:, 0:LC], in_=p1ps[0][:, :],
                                 func=AF.Identity, scale=dskip_t[:, 0:1],
                                 bias=cbds_t[:, 0:1])
                a.then_inc(c1["uev"].h, 1)

            O("scalar", _uev0)
            c1["uev"].bump()
            W("tensor", c1["uev"])
            W("tensor", c1["bkA"], 4 * ci + 1)

            def _conv1(t, rc=rc):
                for k in range(4):
                    mm = t.matmul(
                        p1ps[0][:, :],
                        convd[:, (4 + k) * 128:(5 + k) * 128].bitcast(F32R),
                        xs[rc][:, LCH + k:LCH + k + LC].bitcast(F32R),
                        start=(k == 0), stop=(k == 3))
                mm.then_inc(c1["u"].h, 1)

            O("tensor", _conv1)
            c1["u"].bump()
            W("scalar", c1["u"])

            def _uev1(s, rc=rc):
                s.activation(out=ut[:, LC:2 * LC].bitcast(F32R), in_=p1ps[0][:, :],
                             func=AF.Identity, bias=convb_t[:, 1:2])
                a = s.activation(out=yt[0][:, LC:2 * LC], in_=p1ps[0][:, :],
                                 func=AF.Identity, scale=dskip_t[:, 1:2],
                                 bias=cbds_t[:, 1:2])
                a.then_inc(c1["uev"].h, 1)

            O("scalar", _uev1)
            c1["uev"].bump()
            v_uev = c1["uev"].n

            if "u" in tap_h:
                W("scalar", c1["uev"], v_uev)
                dma("scalar", tap_h["u"].ap()[ci].rearrange("(dc p) l -> p dc l", p=128),
                    ut[:, :].rearrange("p (dc l) -> p dc l", dc=2), c_tap)

            W("tensor", c1["uev"], v_uev)
            W("tensor", c1["bkA"], 4 * ci + 1)
            W("tensor", c1["bkB"], 3 * ci + 3)

            def _xproj(t):
                for dc in range(2):
                    t.matmul(p1ps[0][0:128, :],
                             xprojT_t[:, dc * 160:dc * 160 + 128].bitcast(F32R),
                             ut[:, dc * LC:(dc + 1) * LC].bitcast(F32R),
                             start=(dc == 0), stop=(dc == 1))
                for dc in range(2):
                    mm = t.matmul(p1ps[1][0:32, :],
                                  xprojT_t[:, dc * 160 + 128:(dc + 1) * 160]
                                  .bitcast(F32R),
                                  ut[:, dc * LC:(dc + 1) * LC].bitcast(F32R),
                                  start=(dc == 0), stop=(dc == 1))
                mm.then_inc(c1["xp"].h, 1)

            O("tensor", _xproj)
            c1["xp"].bump()

            W("scalar", c1["xp"])
            W("scalar", c1["ar2w"], 32 * ci)

            def _xpev(s):
                s.copy(out=xpa_sb[:, :], in_=p1ps[0][0:128, :]) \
                    .then_inc(c1["bkA"].h, 1)
                s.copy(out=xpb_sb[:, :], in_=p1ps[1][0:32, :]) \
                    .then_inc(c1["xpe"].h, 1)

            O("scalar", _xpev)
            c1["xpe"].bump()
            c1["bkA"].bump()

            W("sync", c1["xpe"])
            dma("sync", ar2_in.ap()[ci][0:128, :], xpa_sb[:, :], c1["ar2w"])
            dma("sync", ar2_in.ap()[ci][128:160, :], xpb_sb[:, :], c1["ar2w"])
            W("gpsimd", c1["ar2w"])
            coll_chain()
            O("gpsimd", lambda g, ci=ci: g.collective_compute(
                "AllReduce", ALU.add, replica_groups=[list(range(8))],
                ins=[ar2_in.ap()[ci].opt()], outs=[ar2_out.ap()[ci].opt()],
            ).then_inc(c1["ar2"].h, 1))
            c1["ar2"].bump()
            coll_mark(c1["ar2"])

            W("gpsimd", c1["ar2"])
            W("gpsimd", c1["dp"], 2 * ci)
            dma("gpsimd", dtt[:, :].bitcast(F32R), ar2_out.ap()[ci][0:128, :],
                c1["ar2r"])
            var2r = c1["ar2r"].n
            W("gpsimd", c1["ar2"])
            W("gpsimd", c1["crep"], 16 * ci)
            dma("gpsimd", xdblr_bf.ap()[ci], ar2_out.ap()[ci][128:160, :], c1["bcc"])
            W("sync", c1["bcc"])
            dma("sync", bct[:, :], xdblr_bf.ap()[ci], c1["ar2r"])
            var2rb = c1["ar2r"].n

            if "xdbl" in tap_h:
                W("sync", c1["ar2r"], var2r)
                dma("sync", tap_h["xdbl"].ap()[ci][0:128, :], dtt[:, :], c_tap)

            for dc in range(2):
                W("tensor", c1["ar2r"], var2r)
                W("tensor", c1["bkA"], 4 * ci + 2 + dc)
                O("tensor", lambda t, dc=dc: t.matmul(
                    p1ps[0][:, :],
                    dtprojT_t[:, dc * 128:(dc + 1) * 128].bitcast(F32R),
                    dtt[:, :].bitcast(F32R), start=True,
                    stop=True).then_inc(c1["dp"].h, 1))
                c1["dp"].bump()
                W("scalar", c1["dp"])
                W("scalar", c1["dA"], 32 * ci)
                W("scalar", c1["spg"], ci)

                def _sp(s, dc=dc):
                    sl_ = slice(dc * LC, (dc + 1) * LC)
                    s.activation(out=azt[:, sl_], in_=p1ps[0][:, :], func=AF.Abs,
                                 bias=dtb_t[:, dc:dc + 1])
                    a = s.activation(out=rlt[:, sl_], in_=p1ps[0][:, :],
                                     func=AF.Relu, bias=dtb_t[:, dc:dc + 1])
                    a.then_inc(c1["bkA"].h, 1)
                    s.activation(out=azt[:, sl_], in_=azt[:, sl_], func=AF.Exp,
                                 scale=-1.0)
                    a2 = s.activation(out=azt[:, sl_], in_=azt[:, sl_],
                                      func=AF.Ln, bias=1.0)
                    a2.then_inc(c1["spa"].h, 1)

                O("scalar", _sp)
                c1["spa"].bump()
                c1["bkA"].bump()

            W("gpsimd", c1["spa"])
            W("gpsimd", c1["dA"], 32 * ci)
            W("gpsimd", c1["brf"], 32 * ci)
            W("gpsimd", c1["xp"])
            W("gpsimd", c1["uev"], v_uev)

            def _del(g):
                g.tensor_add(out=azt[:, :], in0=azt[:, :], in1=rlt[:, :])
                g.tensor_mul(out=ut[:, :].bitcast(F32R), in0=azt[:, :],
                             in1=ut[:, :]).then_inc(c1["spg"].h, 1)

            O("gpsimd", _del)
            c1["spg"].bump()

            if "delta" in tap_h:
                W("scalar", c1["spg"])
                dma("scalar", tap_h["delta"].ap()[ci].rearrange("(dc p) l -> p dc l", p=128),
                    azt[:, :].rearrange("p (dc l) -> p dc l", dc=2), c_tap)

            for n in range(NST):
                W("tensor", c1["ar2r"], var2rb)
                W("tensor", c1["brf"], 32 * ci + 2 * n)
                O("tensor", lambda t, n=n: t.matmul(
                    repps[0][:, :], selmat[:, n * 128:(n + 1) * 128],
                    bct[:, :], start=True, stop=True).then_inc(c1["brep"].h, 1))
                c1["brep"].bump()
                W("tensor", c1["yc"], 32 * ci + 2 * n)
                O("tensor", lambda t, n=n: t.matmul(
                    repps[1][:, :], selmat[:, (16 + n) * 128:(17 + n) * 128],
                    bct[:, :], start=True, stop=True).then_inc(c1["crep"].h, 1))
                c1["crep"].bump()

                for dc in range(2):
                    scol = (b * 2 + dc) * NST + n
                    kA = k_dA
                    k_dA += 1
                    W("scalar", c1["spg"])
                    W("scalar", c1["dAf"], kA - 1)
                    if kA == 0:
                        W("vector", cS_modmm, v_wmod_done)
                        W("scalar", cS_acf)
                    O("scalar", lambda s, dc=dc, n=n, kA=kA: s.activation(
                        out=dAt[kA % 2][:, :],
                        in_=azt[:, dc * LC:(dc + 1) * LC],
                        func=AF.Exp,
                        scale=acoef[:, dc * NST + n:dc * NST + n + 1])
                        .then_inc(c1["dA"].h, 1))
                    c1["dA"].bump()

                    kY = k_yc
                    k_yc += 1
                    W("vector", c1["spg"])
                    W("vector", c1["brep"])
                    W("vector", c1["dA"])
                    W("vector", c1["ycf"], kY - 1)
                    W("vector", c1["crep"])

                    def _scan(v, dc=dc, kA=kA, kY=kY, j=j, scol=scol):
                        hx = kY % 2
                        v.tensor_mul(out=yct[hx][:, :].bitcast(F32R),
                                     in0=ut[:, dc * LC:(dc + 1) * LC],
                                     in1=repps[0][:, :]) \
                            .then_inc(c1["brf"].h, 1)
                        ini = 0.0 if j == 0 else state[:, scol:scol + 1]
                        v.tensor_tensor_scan(out=htraj[hx][:, :],
                                             data0=dAt[kA % 2][:, :],
                                             data1=yct[hx][:, :], initial=ini,
                                             op0=ALU.mult, op1=ALU.add) \
                            .then_inc(c1["dAf"].h, 1)
                        v.tensor_mul(out=yct[hx][:, :].bitcast(F32R),
                                     in0=htraj[hx][:, :],
                                     in1=repps[1][:, :]) \
                            .then_inc(c1["yc"].h, 1)
                        v.tensor_copy(out=state[:, scol:scol + 1],
                                      in_=htraj[hx][:, LC - 1:LC])

                    O("vector", _scan)
                    c1["dAf"].bump()
                    c1["yc"].bump()
                    c1["brf"].bump()

                    W("gpsimd", c1["yc"])
                    W("gpsimd", c1["uev"], v_uev)

                    def _yadd(g, dc=dc, kY=kY):
                        g.tensor_add(out=yt[0][:, dc * LC:(dc + 1) * LC],
                                     in0=yt[0][:, dc * LC:(dc + 1) * LC],
                                     in1=yct[kY % 2][:, :]) \
                            .then_inc(c1["ycf"].h, 1)

                    O("gpsimd", _yadd)
                    c1["ycf"].bump()

            g_ = j % 4
            dest = 2 * b + (0 if j < 4 else 1)
            W("sync", c1["ycf"], 32 * (ci + 1))
            dma("sync", a2a_in[g_].ap()[dest].rearrange("(dc p) l -> p dc l", p=128),
                yt[0][:, :].rearrange("p (dc l) -> p dc l", dc=2), c1["ywr"])
            if "y" in tap_h:
                dma("sync", tap_h["y"].ap()[ci].rearrange("(dc p) l -> p dc l", p=128),
                    yt[0][:, :].rearrange("p (dc l) -> p dc l", dc=2), c_tap)

        if j >= 4:
            g_ = j - 4
            W("gpsimd", c1["ywr"])
            coll_chain()

            def _a2a(g, g_=g_):
                g.collective_compute(
                    "AllToAll", ALU.bypass, replica_groups=[list(range(8))],
                    ins=[a2a_in[g_].ap().opt()], outs=[a2a_out[g_].ap().opt()],
                ).then_inc(c1_a2a[g_].h, 1)

            O("gpsimd", _a2a)
            c1_a2a[g_].bump()
            coll_mark(c1_a2a[g_])

    c_out = ch("c_out")
    if not skip_p3:
        c3 = {k: ch("c3_" + k) for k in (
            "yx", "x1", "x1w", "stp", "st", "rows_s", "rows", "upf",
            "x1r", "h2", "lg", "lgev", "t2", "sig", "sc", "wup", "wupf",
            "wdn", "wdnf", "bdw", "up", "act", "acts", "dn", "dnev", "er",
            "res")}

        k_up = [0]
        up_rel = []
        up_ring_rel = []

        def upps_acq():
            k = k_up[0]
            k_up[0] += 1
            if k >= 2:
                ch_, v_ = up_rel[k - 2]
                assert (ch_, v_) != (None, None), f"usage {k - 2} never released"
                W("tensor", ch_, v_)
            up_rel.append((None, None))
            return k

        def upps_tile(k):
            return upps[k % 2]

        def upps_mark(k, ch_, v_=None):
            assert up_rel[k] == (None, None)
            up_rel[k] = (ch_, ch_.n if v_ is None else v_)

        for q in range(NQ):
            if tap_h:
                for _e in ("sync", "scalar", "vector", "gpsimd", "tensor"):
                    W(_e, c_tap)
            qr = q % 2
            a2flat = a2a_out[q].ap().rearrange("s d l -> (s d) l").rearrange("(gg p) l -> gg p l", p=128)
            xpflat = xp3.ap().rearrange("(gg p) l -> gg p l", p=128)
            x1flat = x1_d.ap().rearrange("(gg p) l -> gg p l", p=128)
            outflat = out_h.ap().rearrange("(gg p) l -> gg p l", p=128)

            k_st = upps_acq()
            k_st2 = upps_acq()
            st_bank = upps_tile(k_st)
            st_bank2 = upps_tile(k_st2)
            for dcg in range(16):
                gs = dcg % 2
                idx = q * 16 + dcg
                W("sync", c1_a2a[q])
                W("sync", c3["x1"], idx - 1)
                dma("sync", yq[:, gs * LC:(gs + 1) * LC], a2flat[dcg], c3["yx"])
                W("sync", c3["x1w"], 16 * (idx - 1))
                W("sync", c3["stp"], idx - 1)
                dma("sync", x1q[:, gs * LC:(gs + 1) * LC].bitcast(F32R),
                    xpflat[dcg][:, q * LC:(q + 1) * LC], c3["yx"])
                vyx = c3["yx"].n
                W("vector", c3["yx"], vyx)
                W("vector", cS_modp)
                W("vector", c3["stp"], idx - 1)

                def _x1(v, dcg=dcg, gs=gs):
                    sl_ = slice(gs * LC, (gs + 1) * LC)
                    v.scalar_tensor_tensor(out=x1q[:, sl_].bitcast(F32R),
                                           in0=yq[:, sl_],
                                           scalar=gapp[:, dcg:dcg + 1],
                                           in1=x1q[:, sl_],
                                           op0=ALU.mult, op1=ALU.add)
                    v.tensor_mul(out=x1h[:, sl_].bitcast(F32R), in0=x1q[:, sl_],
                                 in1=x1q[:, sl_]).then_inc(c3["x1"].h, 1)

                O("vector", _x1)
                c3["x1"].bump()
                W("sync", c3["x1"])
                dma("sync", x1flat[dcg][:, q * LC:(q + 1) * LC],
                    x1q[:, gs * LC:(gs + 1) * LC], c3["x1w"])
                if "x1" in tap_h:
                    dma("sync", tap_h["x1"].ap().rearrange(
                        "(gg p) l -> gg p l", p=128)[dcg][:, q * LC:(q + 1) * LC],
                        x1q[:, gs * LC:(gs + 1) * LC], c_tap)

                W("tensor", c3["x1"])

                def _st3(t, dcg=dcg, gs=gs, bank=st_bank, bank2=st_bank2):
                    sl_ = slice(gs * LC, (gs + 1) * LC)
                    t.matmul(bank[0:1, :], onesc[:, :].bitcast(F32R),
                             x1q[:, sl_].bitcast(F32R), start=(dcg == 0),
                             stop=(dcg == 15))
                    mm = t.matmul(bank2[0:1, :], onesc[:, :].bitcast(F32R),
                                  x1h[:, sl_].bitcast(F32R), start=(dcg == 0),
                                  stop=(dcg == 15))
                    mm.then_inc(c3["stp"].h, 1)

                O("tensor", _st3)
                c3["stp"].bump()

            W("scalar", c3["stp"], 16 * (q + 1))

            def _r3a(s, bank=st_bank, bank2=st_bank2):
                s.copy(out=stat3[:, 0:LC], in_=bank[0:1, :])
                s.copy(out=stat3[:, LC:2 * LC], in_=bank2[0:1, :])
                s.mul(out=rows3[:, 0:LC].bitcast(F32R),
                      in_=stat3[:, 0:LC], mul=INVD)
                s.mul(out=rows3[:, 2 * LC:3 * LC].bitcast(F32R),
                      in_=stat3[:, LC:2 * LC], mul=INVD) \
                    .then_inc(c3["rows_s"].h, 1)

            O("scalar", _r3a)
            c3["rows_s"].bump()
            upps_mark(k_st, c3["rows_s"])
            upps_mark(k_st2, c3["rows_s"])
            W("vector", c3["rows_s"])

            def _r3c(v):
                v.tensor_mul(out=rows3[:, LC:2 * LC].bitcast(F32R),
                             in0=rows3[:, 0:LC], in1=rows3[:, 0:LC])
                v.tensor_sub(out=rows3[:, 2 * LC:3 * LC].bitcast(F32R),
                             in0=rows3[:, 2 * LC:3 * LC],
                             in1=rows3[:, LC:2 * LC])
                v.tensor_single_scalar(out=rows3[:, 2 * LC:3 * LC].bitcast(F32R),
                                       in_=rows3[:, 2 * LC:3 * LC], scalar=EPS,
                                       op=ALU.add).then_inc(c3["rows"].h, 1)

            O("vector", _r3c)
            c3["rows"].bump()
            W("scalar", c3["rows"])

            def _r3d(s):
                s.activation(out=rows3[:, LC:2 * LC].bitcast(F32R),
                             in_=rows3[:, 2 * LC:3 * LC], func=AF.Ln)
                s.activation(out=rows3[:, LC:2 * LC].bitcast(F32R),
                             in_=rows3[:, LC:2 * LC],
                             func=AF.Exp, scale=-0.5).then_inc(c3["rows"].h, 1)

            O("scalar", _r3d)
            c3["rows"].bump()
            W("vector", c3["rows"])
            O("vector", lambda v: v.tensor_mul(
                out=rows3[:, 2 * LC:3 * LC].bitcast(F32R), in0=rows3[:, 0:LC],
                in1=rows3[:, LC:2 * LC]).then_inc(c3["rows"].h, 1))
            c3["rows"].bump()
            v3rows = c3["rows"].n

            k_rr = upps_acq()
            rrep_bank = upps_tile(k_rr)
            W("tensor", c3["rows"], v3rows)
            O("tensor", lambda t, bank=rrep_bank: t.matmul(
                bank[:, :], onesr[:, :].bitcast(F32R),
                rows3[:, LC:2 * LC].bitcast(F32R), start=True, stop=True))
            k_mr = upps_acq()
            mrep_bank = upps_tile(k_mr)
            O("tensor", lambda t, bank=mrep_bank: t.matmul(
                bank[:, :], onesr[:, :].bitcast(F32R),
                rows3[:, 2 * LC:3 * LC].bitcast(F32R), start=True,
                stop=True).then_inc(c3["rows"].h, 1))
            c3["rows"].bump()
            v3rep = c3["rows"].n

            for dcg in range(16):
                idx = q * 16 + dcg
                W("sync", c3["x1w"], 16 * (idx + 1))
                W("sync", c3["h2"], idx)
                dma("sync", x1r[:, :],
                    x1flat[dcg][:, q * LC:(q + 1) * LC], c3["x1r"])
                W("vector", c3["x1r"])
                W("vector", c3["rows"], v3rep)
                W("vector", cS_modp)

                def _h2(v, dcg=dcg):
                    v.scalar_tensor_tensor(out=x1r[:, :], in0=x1r[:, :],
                                           scalar=cb1pp[:, dcg:dcg + 1],
                                           in1=rrep_bank[:, :],
                                           op0=ALU.mult, op1=ALU.mult)
                    v.scalar_tensor_tensor(out=x1r[:, :], in0=mrep_bank[:, :],
                                           scalar=ncb1pp[:, dcg:dcg + 1],
                                           in1=x1r[:, :], op0=ALU.mult,
                                           op1=ALU.add).then_inc(c3["x1r"].h, 1)

                O("vector", _h2)
                c3["x1r"].bump()
                W("scalar", c3["x1r"])
                if q >= 1:
                    W("scalar", c3["up"], 72 * q)
                    W("scalar", c3["lg"], 2 * q)

                def _h2b(s, dcg=dcg):
                    s.activation(out=h2q[:, dcg * LC:(dcg + 1) * LC],
                                 in_=x1r[:, :], func=AF.Identity,
                                 bias=sbpp[:, dcg:dcg + 1]).then_inc(c3["h2"].h, 1)

                O("scalar", _h2b)
                c3["h2"].bump()
            v_h2q = c3["h2"].n
            upps_mark(k_rr, c3["x1r"])
            upps_mark(k_mr, c3["x1r"])

            if "h2" in tap_h:
                for dcg in range(16):
                    W("scalar", c3["h2"], v_h2q)
                    dma("scalar", tap_h["h2"].ap().rearrange(
                        "(gg p) l -> gg p l", p=128)[dcg][:, q * LC:(q + 1) * LC],
                        h2q[:, dcg * LC:(dcg + 1) * LC], c_tap)

            k_lg = upps_acq()
            lg_bank = upps_tile(k_lg)
            W("tensor", c3["h2"], v_h2q)
            W("tensor", cS_gl)

            def _lg1(t, bank=lg_bank):
                for kc in range(16):
                    mm = t.matmul(bank[0:8, :], wg1_t[:, kc * 8:(kc + 1) * 8],
                                  h2q[:, kc * LC:(kc + 1) * LC],
                                  start=(kc == 0), stop=(kc == 15))
                mm.then_inc(c3["lg"].h, 1)

            O("tensor", _lg1)
            c3["lg"].bump()
            W("scalar", c3["lg"])

            def _lgev1(s, bank=lg_bank):
                s.activation(out=th1[:, :], in_=bank[0:8, :], func=AF.Tanh) \
                    .then_inc(c3["lgev"].h, 1)

            O("scalar", _lgev1)
            c3["lgev"].bump()
            upps_mark(k_lg, c3["lgev"])
            k_lg2 = upps_acq()
            lg2_bank = upps_tile(k_lg2)
            W("tensor", c3["lgev"])
            O("tensor", lambda t, bank=lg2_bank: t.matmul(
                bank[0:8, :], wg2_t[:, :], th1[:, :], start=True,
                stop=True).then_inc(c3["lg"].h, 1))
            c3["lg"].bump()
            W("scalar", c3["lg"])

            def _lgev2(s, bank=lg2_bank):
                s.copy(out=lgt[:, :], in_=bank[0:8, :]) \
                    .then_inc(c3["lgev"].h, 1)

            O("scalar", _lgev2)
            c3["lgev"].bump()
            upps_mark(k_lg2, c3["lgev"])
            v_lgt = c3["lgev"].n

            for tc in range(4):
                k_tp = upps_acq()
                tp_bank = upps_tile(k_tp)
                W("tensor", c3["lgev"], v_lgt)
                W("tensor", cS_const)
                O("tensor", lambda t, tc=tc, bank=tp_bank: t.transpose(
                    bank[0:128, 0:8], lgt[:, tc * 128:(tc + 1) * 128],
                    eqm[0:8, 0:8]).then_inc(c3["t2"].h, 1))
                c3["t2"].bump()
                W("vector", c3["t2"])

                def _top2(v, bank=tp_bank):
                    c = tsc

                    def pad():
                        v.engine_nop()
                        v.engine_nop()
                        v.engine_nop()
                    v.tensor_copy(out=tl[:, :], in_=bank[0:128, 0:8])
                    pad()
                    v.tensor_reduce(out=c[:, 16:17], in_=tl[:, :], axis=AX.X,
                                    op=ALU.max)
                    pad()
                    v.tensor_single_scalar(out=msk8[:, 0:8], in_=tl[:, :],
                                           scalar=c[:, 16:17], op=ALU.is_equal)
                    pad()
                    v.select(out=c[:, 32:40], mask=msk8[:, 0:8],
                             on_true=iota8_t[:, :], on_false=c8_t[:, :])
                    pad()
                    v.tensor_reduce(out=c[:, 17:18], in_=c[:, 32:40], axis=AX.X,
                                    op=ALU.min)
                    pad()
                    v.tensor_single_scalar(out=c[:, 40:48], in_=iota8_t[:, :],
                                           scalar=c[:, 17:18], op=ALU.is_equal)
                    v.tensor_single_scalar(out=msk8[:, 8:16], in_=iota8_t[:, :],
                                           scalar=c[:, 17:18], op=ALU.is_equal)
                    pad()
                    v.select(out=c[:, 48:56], mask=msk8[:, 8:16],
                             on_true=negbig_t[:, :], on_false=tl[:, :])
                    pad()
                    v.tensor_reduce(out=c[:, 18:19], in_=c[:, 48:56], axis=AX.X,
                                    op=ALU.max)
                    pad()
                    v.tensor_single_scalar(out=msk8[:, 16:24], in_=c[:, 48:56],
                                           scalar=c[:, 18:19], op=ALU.is_equal)
                    pad()
                    v.select(out=c[:, 64:72], mask=msk8[:, 16:24],
                             on_true=iota8_t[:, :], on_false=c8_t[:, :])
                    pad()
                    v.tensor_reduce(out=c[:, 19:20], in_=c[:, 64:72], axis=AX.X,
                                    op=ALU.min)
                    pad()
                    v.tensor_single_scalar(out=c[:, 72:80], in_=iota8_t[:, :],
                                           scalar=c[:, 19:20], op=ALU.is_equal)
                    v.tensor_sub(out=c[:, 20:21], in0=c[:, 16:17],
                                 in1=c[:, 18:19]).then_inc(c3["t2"].h, 1)

                O("vector", _top2)
                c3["t2"].bump()
                W("scalar", c3["t2"])

                def _sig(s):
                    s.activation(out=tsc[:, 22:23], in_=tsc[:, 20:21],
                                 func=AF.Sigmoid)
                    s.activation(out=tsc[:, 23:24], in_=tsc[:, 20:21],
                                 func=AF.Sigmoid, scale=-1.0).then_inc(c3["sig"].h, 1)

                O("scalar", _sig)
                c3["sig"].bump()
                W("vector", c3["sig"])

                def _sc2(v, bank=tp_bank):
                    v.tensor_single_scalar(out=tsc[:, 80:88], in_=tsc[:, 40:48],
                                           scalar=tsc[:, 22:23], op=ALU.mult)
                    v.engine_nop()
                    v.engine_nop()
                    v.engine_nop()
                    v.scalar_tensor_tensor(out=tsc[:, 80:88], in0=tsc[:, 72:80],
                                           scalar=tsc[:, 23:24],
                                           in1=tsc[:, 80:88],
                                           op0=ALU.mult, op1=ALU.add) \
                        .then_inc(c3["t2"].h, 1)

                O("vector", _sc2)
                c3["t2"].bump()
                upps_mark(k_tp, c3["t2"])
                k_tb = upps_acq()
                tb_bank = upps_tile(k_tb)
                W("tensor", c3["t2"])
                O("tensor", lambda t, bank=tb_bank: t.transpose(
                    bank[0:8, 0:128], tsc[:, 80:88],
                    eqm[:, :]).then_inc(c3["t2"].h, 1))
                c3["t2"].bump()
                W("scalar", c3["t2"])
                if q >= 2:
                    W("scalar", c3["wdnf"], 64 * (q - 1))

                def _scev(s, tc=tc, qr=qr, bank=tb_bank):
                    s.copy(out=scoreq[qr][:, tc * 128:(tc + 1) * 128]
                           .bitcast(F32R), in_=bank[0:8, 0:128]) \
                        .then_inc(c3["sc"].h, 1)

                O("scalar", _scev)
                c3["sc"].bump()
                upps_mark(k_tb, c3["sc"])
            v_sc = c3["sc"].n

            if "scores" in tap_h:
                W("scalar", c3["sc"], v_sc)
                dma("scalar", tap_h["scores"].ap()[:, q * LC:(q + 1) * LC],
                    scoreq[qr][:, :], c_tap)

            for mk in (up_rel[-2], up_rel[-1]):
                W("tensor", mk[0], mk[1])
            for ei in range(E):
                sr_bank = upps[0]
                W("tensor", c3["sc"], v_sc)
                if ei >= 1:
                    W("tensor", c3["acts"], c3["acts"].n)
                O("tensor", lambda t, ei=ei, qr=qr, bank=sr_bank: t.matmul(
                    bank[:, :], sel8[:, ei * 128:(ei + 1) * 128].bitcast(F32R),
                    scoreq[qr][:, :].bitcast(F32R), start=True,
                    stop=True).then_inc(c3["up"].h, 1))
                c3["up"].bump()
                v_srep = c3["up"].n
                for mc in range(8):
                    gidx = q * 64 + ei * 8 + mc
                    if gidx >= 2:
                        W("sync", c3["up"], up_ring_rel[gidx - 2])
                    dma("sync", wup_ring[gidx % 2][:, :].rearrange(
                        "p (kc f) -> p kc f", kc=16),
                        wup_bf.ap()[ei, mc].rearrange("kc p f -> p kc f"), c3["wup"])
                    vw = c3["wup"].n
                    ub_bank = upps[1]
                    W("tensor", c3["act"], c3["act"].n)
                    W("tensor", c3["wup"], vw)
                    W("tensor", c3["h2"], v_h2q)

                    def _up(t, gidx=gidx, bank=ub_bank):
                        for kc in range(16):
                            mm = t.matmul(
                                bank[:, :],
                                wup_ring[gidx % 2][:, kc * 128:(kc + 1) * 128],
                                h2q[:, kc * LC:(kc + 1) * LC],
                                start=(kc == 0), stop=(kc == 15))
                        mm.then_inc(c3["up"].h, 1)

                    O("tensor", _up)
                    c3["up"].bump()
                    up_ring_rel.append(c3["up"].n)
                    assert len(up_ring_rel) == gidx + 1
                    W("scalar", c3["up"])
                    if q >= 1:
                        W("scalar", c3["wdnf"], 64 * q)

                    def _silu(s, ei=ei, mc=mc, bank=ub_bank):
                        s.activation(
                            out=act_t[:, (ei * 8 + mc) * LC:(ei * 8 + mc + 1) * LC],
                            in_=bank[:, :], func=AF.Silu,
                            bias=bup_t[:, ei * 8 + mc:ei * 8 + mc + 1]) \
                            .then_inc(c3["act"].h, 1)

                    O("scalar", _silu)
                    c3["act"].bump()
                    W("vector", c3["act"])
                    W("vector", c3["up"], v_srep)

                    def _ascale(v, ei=ei, mc=mc, bank=sr_bank):
                        v.tensor_mul(
                            out=act_t[:, (ei * 8 + mc) * LC:(ei * 8 + mc + 1) * LC],
                            in0=act_t[:, (ei * 8 + mc) * LC:(ei * 8 + mc + 1) * LC],
                            in1=bank[:, :]).then_inc(c3["acts"].h, 1)

                    O("vector", _ascale)
                    c3["acts"].bump()
            k_syn0 = upps_acq()
            k_syn1 = upps_acq()
            upps_mark(k_syn0, c3["acts"])
            upps_mark(k_syn1, c3["act"])
            v_acts = c3["acts"].n

            for og in range(8):
                widx = q * 8 + og
                W("sync", c3["wdnf"], 8 * (widx - 1))
                dma("sync", bdg[widx % 2][:, :].bitcast(F32R),
                    bdown_h.ap()[:, og * 256:(og + 1) * 256], c3["bdw"])
                vbd = c3["bdw"].n
                W("tensor", c3["bdw"], vbd)
                W("tensor", c3["acts"], v_acts)
                W("tensor", c3["sc"], v_sc)
                W("tensor", c3["dnev"], 2 * widx)

                def _dstart(t, qr=qr, widx=widx):
                    for ocl in range(2):
                        t.matmul(dnps[ocl][:, :],
                                 bdg[widx % 2][:, ocl * 128:(ocl + 1) * 128]
                                 .bitcast(F32R),
                                 scoreq[qr][:, :].bitcast(F32R),
                                 start=True, stop=False)

                O("tensor", _dstart)
                for ei in range(E):
                    widx2 = widx * 8 + ei
                    W("sync", c3["wdnf"], widx2 - 1)
                    dma("sync", wdn_ring[widx2 % 2][:, :].rearrange(
                        "p (ocl hc f) -> p ocl hc f", ocl=2, f=128),
                        wdown_bf.ap()[og, ei].rearrange("ocl hc p f -> p ocl hc f"),
                        c3["wdn"])
                    vw = c3["wdn"].n
                    W("tensor", c3["wdn"], vw)

                    def _down(t, ei=ei, widx2=widx2):
                        ring = wdn_ring[widx2 % 2]
                        for ocl in range(2):
                            for hc in range(8):
                                tix = ocl * 8 + hc
                                mm = t.matmul(
                                    dnps[ocl][:, :],
                                    ring[:, tix * 128:(tix + 1) * 128],
                                    act_t[:, (ei * 8 + hc) * LC:(ei * 8 + hc + 1) * LC],
                                    start=False,
                                    stop=(ei == E - 1 and hc == 7))
                                if hc == 7 and ocl == 1:
                                    mm.then_inc(c3["wdnf"].h, 1)

                    O("tensor", _down)
                    c3["wdnf"].bump()

                for ocl in range(2):
                    oc = og * 2 + ocl
                    kres = 2 * widx + ocl
                    W("sync", c3["x1w"])
                    W("sync", c3["res"], kres)
                    dma("sync", x1e[0][:, :],
                        x1flat[oc][:, q * LC:(q + 1) * LC], c3["er"])
                    if "x1e" in tap_h and q == 0:
                        W("sync", c3["er"])
                        dma("sync", tap_h["x1e"].ap()[oc], x1e[0][:, :], c_tap)
                    W("vector", c3["er"])
                    W("vector", c3["wdnf"], 8 * (widx + 1))
                    W("vector", c_out, 16 * kres)

                    def _res(v, ocl=ocl, oc=oc):
                        v.scalar_tensor_tensor(out=evt[0][:, :],
                                               in0=dnps[ocl][:, :],
                                               scalar=gbpp[:, oc:oc + 1],
                                               in1=x1e[0][:, :],
                                               op0=ALU.mult, op1=ALU.add) \
                            .then_inc(c3["dnev"].h, 1)
                        v.tensor_copy(out=evt[0][:, :], in_=evt[0][:, :]) \
                            .then_inc(c3["res"].h, 1)

                    O("vector", _res)
                    c3["dnev"].bump()
                    c3["res"].bump()
                    W("sync", c3["res"])
                    if "evt" in tap_h and q == 0:
                        dma("sync", tap_h["evt"].ap()[oc], evt[0][:, :], c_tap)
                        W("sync", c_tap)
                    dma("sync", outflat[oc][:, q * LC:(q + 1) * LC],
                        evt[0][:, :], c_out)

    if "ncaw" in tap_h:
        if skip_p3:
            for g_ in range(NQ):
                W("sync", c1_a2a[g_])
            W("sync", c1["ywr"])
        else:
            W("sync", c_out)
        _ncaw_dump(2)

    if "cmod" in tap_h:
        if skip_p3:
            for g_ in range(NQ):
                W("sync", c1_a2a[g_])
            W("sync", c1["ywr"])
        else:
            W("sync", c_out)
        dma("sync", tap_h["cmod"].ap()[:, 0:48], modsl[:, :], c_tap)
        dma("sync", tap_h["cmod"].ap()[:, 48:56], ca1pp[:, :], c_tap)
        dma("sync", tap_h["cmod"].ap()[:, 56:64], nca1pp[:, :], c_tap)

    W("scalar", c_out)
    if "winf" in tap_h:
        W("sync", c_out)
        _win_dump(1)
    if tap_h:
        W("scalar", c_tap)
    if skip_p3:
        for g_ in range(NQ):
            W("scalar", c1_a2a[g_])
        W("scalar", c1["ywr"])

    block = e(nc.Block())
    with allow_ici_all_to_all():
        P.finish(block)
    stk.close()
    return nc


def kernel(**inputs):
    in_maps = stage_inputs(inputs)
    nc = build_nc(taps=tuple(DEBUG_TAPS), skip_p3=SKIP_P3)
    with allow_ici_all_to_all():
        res = run_bass_kernel_spmd(nc, in_maps, core_ids=list(range(8)))
    out = np.empty((NB, L, D), dtype=np.float32)
    for c in range(8):
        bc, hc3 = c // 2, c % 2
        out[bc, hc3 * TOK3:(hc3 + 1) * TOK3, :] = res.results[c]["out"].T
    kernel.last_results = res.results
    return out


# version 05
# speedup vs baseline: 1.0148x; 1.0124x over previous
import numpy as np
from contextlib import ExitStack

import concourse.bass as bass
import concourse.mybir as mybir
from concourse.bass_utils import run_bass_kernel_spmd
from concourse.aot_env import allow_ici_all_to_all

F32 = mybir.dt.float32
F32R = mybir.dt.float32r
BF16 = mybir.dt.bfloat16
AF = mybir.ActivationFunctionType
ALU = mybir.AluOpType
AX = mybir.AxisListType

D, DSL, NB, L, NST, RK, E, HE = 2048, 256, 4, 4096, 16, 128, 8, 1024
T = NB * L
LC = 512
LCH = LC + 3
NJ = L // LC
NCH = NB * NJ
TOK3 = 2048
NQ = 4
EPS = 1e-5
INVD = 1.0 / D

DEBUG_TAPS = []
SKIP_P3 = False


def _r(x):
    return np.ascontiguousarray(x)


def stage_inputs(inputs):
    x = np.asarray(inputs["L0_attn_x"], dtype=np.float32)
    x_dmaj = _r(x.transpose(0, 2, 1))
    conv_w = np.asarray(inputs["L0_attn_conv_w"], np.float32)[:, 0, :]
    conv_b = np.asarray(inputs["L0_attn_conv_b"], np.float32)
    xprojT = _r(np.asarray(inputs["L0_attn_xproj_w"], np.float32).T)
    dtprojT = _r(np.asarray(inputs["L0_attn_dtproj_w"], np.float32).T)
    dtproj_b = np.asarray(inputs["L0_attn_dtproj_b"], np.float32)
    A_log = np.asarray(inputs["L0_attn_A_log"], np.float32)
    D_skip = np.asarray(inputs["L0_attn_D_skip"], np.float32)
    Wg1T = _r(np.asarray(inputs["L0_ffn_Wg1"], np.float32).T)
    Wg2T = _r(np.asarray(inputs["L0_ffn_Wg2"], np.float32).T)
    Wup = np.asarray(inputs["L0_ffn_Wup"], np.float32)
    bup = np.asarray(inputs["L0_ffn_bup"], np.float32)
    Wdown = np.asarray(inputs["L0_ffn_Wdown"], np.float32)
    bdown = np.asarray(inputs["L0_ffn_bdown"], np.float32)
    cc = np.asarray(inputs["L0_c"], np.float32)
    wmod = np.asarray(inputs["L0_wmod"], np.float32)

    WupT = Wup.transpose(0, 2, 1)
    wup_tiles = _r(WupT.reshape(E, 16, 128, 8, 128).transpose(0, 3, 1, 2, 4))
    WdownT = Wdown.transpose(0, 2, 1)
    a = WdownT.reshape(E, 8, 128, 16, 128).transpose(3, 0, 1, 2, 4)
    wdown_tiles = _r(a.reshape(8, 2, E, 8, 128, 128).transpose(0, 2, 1, 3, 4, 5))
    wmod_r = wmod.reshape(D, 6, 8, DSL)
    cT = _r(cc.T)

    in_maps = []
    for c in range(8):
        sl = slice(DSL * c, DSL * (c + 1))
        bc, hc3 = c // 2, c % 2
        bsel = np.zeros((128, NB), np.float32)
        bsel[:, bc] = 1.0
        in_maps.append({
            "xp1": _r(x_dmaj[:, sl, :]),
            "xp3": _r(x_dmaj[bc, :, hc3 * TOK3:(hc3 + 1) * TOK3]),
            "cT": cT,
            "wmod_sl": _r(wmod_r[:, :, c, :].reshape(D, 6 * DSL)),
            "xprojT_sl": _r(xprojT[sl, :]),
            "dtprojT_sl": _r(dtprojT[:, sl]),
            "dtproj_b_sl": _r(dtproj_b[sl].reshape(2, 128).T),
            "conv_w_sl": _r(conv_w[sl].reshape(2, 128, 4).transpose(1, 0, 2)
                            .reshape(128, 8)),
            "conv_b_sl": _r(conv_b[sl].reshape(2, 128).T),
            "A_log_sl": _r(A_log[sl].reshape(2, 128, NST).transpose(1, 0, 2)
                           .reshape(128, 2 * NST)),
            "D_skip_sl": _r(D_skip[sl].reshape(2, 128).T),
            "Wg1T_t": _r(Wg1T.reshape(16, 128, 8)),
            "Wg2T_t": Wg2T,
            "wup_tiles": wup_tiles,
            "wdown_tiles": wdown_tiles,
            "bup_t": _r(bup.reshape(E, 8, 128).transpose(2, 0, 1).reshape(128, E * 8)),
            "bdown_t": bdown,
            "iota8": _r(np.tile(np.arange(8, dtype=np.float32), (128, 1))),
            "iotap": _r(np.arange(128, dtype=np.float32).reshape(128, 1)),
            "iotap32": _r(np.arange(32, dtype=np.float32).reshape(32, 1)),
            "bsel": bsel,
        })
    return in_maps


class Ch:
    def __init__(self, stk, nc, name):
        self.h = stk.enter_context(nc.semaphore(name))
        self.n = 0

    def bump(self, k=1):
        self.n += k
        return self.n


class Prog:
    ENGS = ("sync", "scalar", "vector", "gpsimd", "tensor")

    def __init__(self):
        self.th = {e: [] for e in self.ENGS}
        self.waited = {e: {} for e in self.ENGS}

    def on(self, eng, fn):
        self.th[eng].append(fn)

    def wait(self, eng, ch, val=None):
        v = ch.n if val is None else val
        if v <= 0:
            return
        if self.waited[eng].get(id(ch), -1) >= v:
            return
        self.waited[eng][id(ch)] = v
        self.on(eng, lambda e, h=ch.h, v=v: e.wait_ge(h, v))

    def finish(self, block):
        for name in self.ENGS:
            fns = self.th[name]
            if fns:
                getattr(block, name)(lambda e, fns=fns: [f(e) for f in fns])


def build_nc(taps=(), skip_p3=False):
    nc = bass.Bass("TRN2", target_bir_lowering=False,
                   dynamic_dma_scratch_size=10240)
    stk = ExitStack()
    e = stk.enter_context

    def dram(name, shape, dtype=F32, kind="Internal", addr_space="Local"):
        return nc.dram_tensor(name, list(shape), dtype, kind=kind,
                              addr_space=addr_space)

    xp1 = dram("xp1", [NB, DSL, L], F32R, kind="ExternalInput")
    xp3 = dram("xp3", [D, TOK3], F32R, kind="ExternalInput")
    cT_h = dram("cT", [D, NB], kind="ExternalInput")
    wmod_h = dram("wmod_sl", [D, 1536], F32R, kind="ExternalInput")
    xprojT_h = dram("xprojT_sl", [DSL, 160], F32R, kind="ExternalInput")
    dtprojT_h = dram("dtprojT_sl", [RK, DSL], F32R, kind="ExternalInput")
    dtb_h = dram("dtproj_b_sl", [128, 2], kind="ExternalInput")
    convw_h = dram("conv_w_sl", [128, 8], kind="ExternalInput")
    convb_h = dram("conv_b_sl", [128, 2], kind="ExternalInput")
    alog_h = dram("A_log_sl", [128, 2 * NST], kind="ExternalInput")
    dskip_h = dram("D_skip_sl", [128, 2], kind="ExternalInput")
    wg1_h = dram("Wg1T_t", [16, 128, 8], kind="ExternalInput")
    wg2_h = dram("Wg2T_t", [8, 8], kind="ExternalInput")
    wup_h = dram("wup_tiles", [E, 8, 16, 128, 128], kind="ExternalInput")
    wdown_h = dram("wdown_tiles", [8, E, 2, 8, 128, 128], kind="ExternalInput")
    bup_h = dram("bup_t", [128, E * 8], kind="ExternalInput")
    bdown_h = dram("bdown_t", [8, D], F32R, kind="ExternalInput")
    iota8_h = dram("iota8", [128, 8], kind="ExternalInput")
    iotap_h = dram("iotap", [128, 1], kind="ExternalInput")
    iotap32_h = dram("iotap32", [32, 1], kind="ExternalInput")
    bsel_h = dram("bsel", [128, NB], kind="ExternalInput")
    out_h = dram("out", [D, TOK3], kind="ExternalOutput")

    tap_h = {}
    for tname, shape in (("y", [NCH, DSL, LC]), ("h1", [NCH, DSL, LC]),
                         ("u", [NCH, DSL, LC]), ("delta", [NCH, DSL, LC]),
                         ("xdbl", [NCH, 160, LC]), ("mod", [128, 48]),
                         ("scores", [8, TOK3]), ("x1", [D, TOK3]),
                         ("stp", [NCH, 2, LC]), ("stats", [NCH, 2, LC]),
                         ("rows", [NCH, 1, 3 * LC]), ("h1m", [NCH, DSL, LC]),
                         ("cst", [NCH, 128]), ("cmod", [128, 64]),
                         ("ncaw", [3, 128, 32]), ("x1e", [16, 128, LC]),
                         ("evt", [16, 128, LC])):
        if tname in taps:
            tap_h[tname] = dram("tap_" + tname, shape, kind="ExternalOutput")
    if "h2" in taps:
        tap_h["h2"] = dram("tap_h2", [D, TOK3], BF16, kind="ExternalOutput")
    if "win" in taps:
        tap_h["winf"] = dram("tap_winf", [2, 2912], kind="ExternalOutput")
        tap_h["winb"] = dram("tap_winb", [2, 4096], BF16, kind="ExternalOutput")

    wup_bf = dram("wup_bf", [E, 8, 16, 128, 128], BF16)
    wdown_bf = dram("wdown_bf", [8, E, 2, 8, 128, 128], BF16)
    wg1_bf = dram("wg1_bf", [16, 128, 8], BF16)
    wg2_bf = dram("wg2_bf", [8, 8], BF16)
    xdblr_bf = dram("xdblr_bf", [NCH, 32, LC], BF16)
    ar1_in = dram("ar1_in", [NCH, 2, LC])
    ar1_out = dram("ar1_out", [NCH, 2, LC], addr_space="Shared")
    ar2_in = dram("ar2_in", [NCH, 160, LC])
    ar2_out = dram("ar2_out", [NCH, 160, LC], addr_space="Shared")
    ag_in = dram("ag_in", [1536, NB])
    ag_out = dram("ag_out", [8 * 1536, NB], addr_space="Shared")
    a2a_in = [dram(f"a2a_in{g}", [8, DSL, LC]) for g in range(NQ)]
    a2a_out = [dram(f"a2a_out{g}", [8, DSL, LC]) for g in range(NQ)]
    x1_d = dram("x1_d", [D, TOK3])

    def sb(name, p, f, dtype=F32):
        return e(nc.sbuf_tensor(name, [p, f], dtype))

    eqm = sb("eqm", 128, 128)
    onesr = sb("onesr", 1, 128)
    onesc = sb("onesc", 128, 1)
    onesc_bf = sb("onesc_bf", 128, 1, BF16)
    selmat = sb("selmat", 32, 32 * 128, BF16)
    sel8 = sb("sel8", 8, 8 * 128)
    convd = sb("convd", 128, 8 * 128)
    xprojT_t = sb("xprojT_t", 128, 2 * 160)
    dtprojT_t = sb("dtprojT_t", 128, 2 * 128)
    acoef = sb("acoef", 128, 2 * NST)
    convw_t = sb("convw_t", 128, 8)
    convb_t = sb("convb_t", 128, 2)
    dskip_t = sb("dskip_t", 128, 2)
    cbds_t = sb("cbds_t", 128, 2)
    dtb_t = sb("dtb_t", 128, 2)
    can0 = sb("can0", 128, 8)
    ca1pp = sb("ca1pp", 128, 8)
    nca1pp = sb("nca1pp", 128, 8)
    can1 = sb("can1", 128, 8)
    gapp = sb("gapp", 128, 16)
    gbpp = sb("gbpp", 128, 16)
    sbpp = sb("sbpp", 128, 16)
    cb1pp = sb("cb1pp", 128, 16)
    ncb1pp = sb("ncb1pp", 128, 16)
    gall = sb("gall", 128, 4 * 64)
    msc = sb("msc", 128, 16)
    msc2 = sb("msc2", 128, 16)
    bsel_t = sb("bsel_t", 128, NB)
    wg1_t = sb("wg1_t", 128, 16 * 8, BF16)
    wg2_t = sb("wg2_t", 8, 8, BF16)
    bup_t = sb("bup_t_s", 128, E * 8)
    iota8_t = sb("iota8_t", 128, 8)
    c8_t = sb("c8_t", 128, 8)
    negbig_t = sb("negbig_t", 128, 8)
    iotap_t = sb("iotap_t", 128, 1)
    iotap32_t = sb("iotap32_t", 32, 1)
    ct_t = sb("ct_t", 128, 16 * NB)
    sct_t = sb("sct_t", 128, 16 * NB)
    modsl = sb("modsl", 128, 12 * NB)

    xs = [sb(f"xs{i}", 128, 2 * LCH) for i in range(2)]
    x2t = sb("x2t", 128, 2 * LC, BF16)
    statb = sb("statb", 1, 2 * LC)
    rows = sb("rows", 1, 3 * LC)
    ut = sb("ut", 128, 2 * LC)
    xpa_sb = sb("xpa_sb", 128, LC)
    xpb_sb = sb("xpb_sb", 32, LC)
    dtt = sb("dtt", 128, LC)
    bct = sb("bct", 32, LC, BF16)
    azt = sb("azt", 128, 2 * LC)
    rlt = sb("rlt", 128, 2 * LC)
    dAt = [sb(f"dAt{i}", 128, LC) for i in range(2)]
    htraj = [sb(f"htraj{i}", 128, LC) for i in range(2)]
    yct = [sb(f"yct{i}", 128, LC) for i in range(2)]
    yt = [sb("yt0", 128, 2 * LC)]
    state = sb("state", 128, 128)
    halo = sb("halo", 128, 8 * 3)

    if not skip_p3:
        h2q = sb("h2q", 128, 16 * LC, BF16)
        act_t = sb("act_t", 128, E * 8 * LC, BF16)
        yq = sb("yq", 128, 2 * LC)
        x1q = sb("x1q", 128, 2 * LC)
        x1h = sb("x1h", 128, 2 * LC)
        rows3 = sb("rows3", 1, 3 * LC)
        stat3 = sb("stat3", 1, 2 * LC)
        th1 = sb("th1", 8, LC, BF16)
        lgt = sb("lgt", 8, LC)
        tl = sb("tl", 128, 8)
        tsc = sb("tsc", 128, 12 * 8)
        msk8 = sb("msk8", 128, 3 * 8, mybir.dt.uint8)
        scoreq = [sb(f"scoreq{i}", 8, LC) for i in range(2)]
        wup_ring = [sb(f"wupr{i}", 128, 16 * 128, BF16) for i in range(2)]
        wdn_ring = [sb(f"wdnr{i}", 128, 16 * 128, BF16) for i in range(2)]
        bdg = [sb(f"bdg{i}", 8, 2 * 128) for i in range(2)]
        evt = [sb("evt0", 128, LC)]
        x1r = sb("x1r", 128, LC)
        x1e = [sb("x1e0", 128, LC)]

    p1ps = [e(nc.psum_tensor(f"p1ps{i}", [128, 512], F32)) for i in range(2)]
    repps = [e(nc.psum_tensor(f"repps{i}", [128, 512], F32)) for i in range(2)]
    upps = [e(nc.psum_tensor(f"upps{i}", [128, 512], F32)) for i in range(2)]
    dnps = [e(nc.psum_tensor(f"dnps{i}", [128, 512], F32)) for i in range(2)]

    chn = {}

    def ch(name):
        if name not in chn:
            chn[name] = Ch(stk, nc, name)
        return chn[name]

    P = Prog()
    W = P.wait
    O = P.on
    last_coll = [None]

    def coll_chain():
        if last_coll[0] is not None:
            ch_, v_ = last_coll[0]
            W("gpsimd", ch_, v_)

    def coll_mark(ch_):
        last_coll[0] = (ch_, ch_.n)

    def dma(eng, out, in_, c):
        c.bump(16)
        O(eng, lambda en, o=out, i=in_, h=c.h: en.dma_start(out=o, in_=i).then_inc(h, 16))

    cS_ld = ch("cS_ld")
    dma("sync", iota8_t[:, :], iota8_h.ap(), cS_ld)
    dma("sync", iotap_t[:, :], iotap_h.ap(), cS_ld)
    dma("sync", iotap32_t[:, :], iotap32_h.ap(), cS_ld)
    dma("sync", bsel_t[:, :], bsel_h.ap(), cS_ld)
    dma("sync", xprojT_t[:, :].rearrange("p (dc f) -> p dc f", dc=2)
        .bitcast(F32R),
        xprojT_h.ap().rearrange("(dc p) f -> p dc f", p=128), cS_ld)
    dma("sync", dtprojT_t[:, :].rearrange("p (dc f) -> p dc f", dc=2)
        .bitcast(F32R),
        dtprojT_h.ap().rearrange("p (dc f) -> p dc f", dc=2), cS_ld)
    dma("sync", acoef[:, :], alog_h.ap(), cS_ld)
    dma("sync", convw_t[:, :], convw_h.ap(), cS_ld)
    dma("sync", convb_t[:, :], convb_h.ap(), cS_ld)
    dma("sync", dskip_t[:, :], dskip_h.ap(), cS_ld)
    dma("sync", dtb_t[:, :], dtb_h.ap(), cS_ld)
    dma("sync", ct_t[:, :].rearrange("p (kc b) -> p kc b", kc=16),
        cT_h.ap().rearrange("(kc p) b -> p kc b", p=128), cS_ld)
    dma("sync", bup_t[:, :], bup_h.ap(), cS_ld)

    cS_cast = ch("cS_cast")
    for ei in range(E):
        dma("gpsimd", wup_bf.ap()[ei], wup_h.ap()[ei], cS_cast)
        dma("gpsimd", wdown_bf.ap()[:, ei], wdown_h.ap()[:, ei], cS_cast)
    dma("gpsimd", wg1_bf.ap(), wg1_h.ap(), cS_cast)
    dma("gpsimd", wg2_bf.ap(), wg2_h.ap(), cS_cast)
    cS_gl = ch("cS_gl")
    W("sync", cS_cast)
    dma("sync", wg1_t[:, :].rearrange("p (kc f) -> p kc f", kc=16),
        wg1_bf.ap().rearrange("kc p f -> p kc f"), cS_gl)
    dma("sync", wg2_t[:, :], wg2_bf.ap(), cS_gl)

    cS_const = ch("cS_const")

    def _vconst(v):
        v.wait_ge(cS_ld.h, cS_ld.n)
        v.memset(onesr[:, :], 1.0)
        v.memset(onesc[:, :], 1.0)
        v.memset(onesc_bf[:, :], 1.0)
        v.tensor_copy(out=onesr[:, :].bitcast(F32R), in_=onesr[:, :])
        v.tensor_copy(out=onesc[:, :].bitcast(F32R), in_=onesc[:, :])
        v.memset(can0[:, :], 99.0)
        v.memset(can1[:, :], 99.0)
        v.memset(c8_t[:, :], 8.0)
        v.memset(negbig_t[:, :], -1e30)
        v.memset(state[:, :], 0.0)
        for n in range(128):
            v.memset(eqm[:, n:n + 1], float(n))
        v.tensor_single_scalar(out=eqm[:, :], in_=eqm[:, :],
                               scalar=iotap_t[:, :], op=ALU.is_equal)
        for n in range(32):
            v.memset(selmat[:, n * 128:(n + 1) * 128], float(n))
        v.tensor_single_scalar(out=selmat[:, :], in_=selmat[:, :],
                               scalar=iotap32_t[:, :], op=ALU.is_equal)
        for n in range(8):
            v.memset(sel8[:, n * 128:(n + 1) * 128], float(n))
        v.tensor_single_scalar(out=sel8[:, :].bitcast(F32R), in_=sel8[:, :],
                               scalar=iotap_t[0:8, :], op=ALU.is_equal)
        for dc in range(2):
            for k in range(4):
                ix = dc * 4 + k
                v.tensor_single_scalar(
                    out=convd[:, ix * 128:(ix + 1) * 128].bitcast(F32R),
                    in_=eqm[:, :], scalar=convw_t[:, ix:ix + 1], op=ALU.mult)
        v.tensor_mul(out=cbds_t[:, :], in0=convb_t[:, :],
                     in1=dskip_t[:, :]).then_inc(cS_const.h, 1)

    O("vector", _vconst)
    cS_const.bump()

    cS_sc = ch("cS_sc")

    def _ssetup(s):
        s.wait_ge(cS_ld.h, cS_ld.n)
        s.activation(out=acoef[:, :], in_=acoef[:, :], func=AF.Exp)
        s.activation(out=sct_t[:, :].bitcast(F32R), in_=ct_t[:, :],
                     func=AF.Silu).then_inc(cS_sc.h, 1)

    O("scalar", _ssetup)
    cS_sc.bump()

    cS_acf = ch("cS_acf")
    W("vector", cS_sc)
    O("vector", lambda v: v.tensor_single_scalar(
        out=acoef[:, :], in_=acoef[:, :], scalar=-1.0,
        op=ALU.mult).then_inc(cS_acf.h, 1))
    cS_acf.bump()

    cS_wmod = ch("cS_wmod")
    cS_modmm = ch("cS_modmm")
    cS_modev = ch("cS_modev")
    kwg = 0
    for mt in range(12):
        for grp in range(4):
            rb = kwg % 2
            W("sync", cS_modmm, kwg - 1)
            dma("sync", yct[rb][:, :].rearrange("p (kc f) -> p kc f", kc=4)
                .bitcast(F32R),
                wmod_h.ap().rearrange("(kc p) (m f) -> m p kc f", p=128, f=128)[
                    mt][:, grp * 4:(grp + 1) * 4], cS_wmod)
            W("tensor", cS_wmod)
            W("tensor", cS_sc)
            if grp == 0:
                W("tensor", cS_modev, mt - 1)

            def _modmm(t, mt=mt, grp=grp, rb=rb):
                for kc in range(4):
                    mm = t.matmul(p1ps[mt % 2][0:128, 0:NB],
                                  yct[rb][:, kc * 128:(kc + 1) * 128].bitcast(F32R),
                                  sct_t[:, (grp * 4 + kc) * NB:(grp * 4 + kc + 1) * NB]
                                  .bitcast(F32R),
                                  start=(grp == 0 and kc == 0),
                                  stop=(grp == 3 and kc == 3))
                mm.then_inc(cS_modmm.h, 1)

            O("tensor", _modmm)
            cS_modmm.bump()
            kwg += 1
        W("scalar", cS_modmm)
        O("scalar", lambda s, mt=mt: s.copy(
            out=modsl[:, mt * NB:(mt + 1) * NB],
            in_=p1ps[mt % 2][0:128, 0:NB]).then_inc(cS_modev.h, 1))
        cS_modev.bump()
    v_wmod_done = cS_modmm.n

    c_tap = ch("c_tap")
    if "mod" in tap_h:
        W("scalar", cS_modev)
        dma("scalar", tap_h["mod"].ap(), modsl[:, :], c_tap)

    cS_agw = ch("cS_agw")
    W("scalar", cS_modev)
    dma("scalar", ag_in.ap().rearrange("(m p) b -> p m b", p=128),
        modsl[:, :].rearrange("p (m b) -> p m b", m=12), cS_agw)
    cS_ag = ch("cS_ag")

    coll_chain()

    def _ag(g):
        g.wait_ge(cS_agw.h, cS_agw.n)
        g.collective_compute(
            "AllGather", ALU.bypass, replica_groups=[list(range(8))],
            ins=[ag_in.ap().opt()], outs=[ag_out.ap().opt()],
        ).then_inc(cS_ag.h, 1)

    O("gpsimd", _ag)
    cS_ag.bump()
    coll_mark(cS_ag)

    cS_agr = ch("cS_agr")
    W("sync", cS_ag)
    agv = ag_out.ap()
    for pj in range(4):
        j_ = pj + 2
        for dcg in range(16):
            row0 = (dcg // 2) * 1536 + j_ * DSL + (dcg % 2) * 128
            dma("sync", gall[:, (pj * 16 + dcg) * 4:(pj * 16 + dcg + 1) * 4],
                agv[row0:row0 + 128, :], cS_agr)

    cS_modp = ch("cS_modp")

    def _modp(v):
        v.wait_ge(cS_modev.h, cS_modev.n)
        v.wait_ge(cS_agr.h, cS_agr.n)
        ca_view = modsl[:, 2 * NB:4 * NB].rearrange("p (dc b) -> p b dc", dc=2)
        v.tensor_single_scalar(
            out=ca1pp[:, :].rearrange("p (b dc) -> p b dc", dc=2),
            in_=ca_view, scalar=1.0, op=ALU.add)
        v.tensor_scalar(
            out=nca1pp[:, :].rearrange("p (b dc) -> p b dc", dc=2),
            in0=ca_view, scalar1=-1.0,
            scalar2=-1.0, op0=ALU.mult, op1=ALU.add)
        dsts = (gapp, sbpp, cb1pp, gbpp)
        for pj, dst in enumerate(dsts):
            base = pj * 64
            v.tensor_single_scalar(out=dst[:, :], in_=gall[:, base:base + 64:4],
                                   scalar=bsel_t[:, 0:1], op=ALU.mult)
        for b in range(1, NB):
            for pj, dst in enumerate(dsts):
                base = pj * 64
                v.scalar_tensor_tensor(out=dst[:, :],
                                       in0=gall[:, base + b:base + 64:4],
                                       scalar=bsel_t[:, b:b + 1], in1=dst[:, :],
                                       op0=ALU.mult, op1=ALU.add)
        v.tensor_single_scalar(out=msc[:, :], in_=gapp[:, :], scalar=0.0,
                               op=ALU.is_ge)
        v.tensor_single_scalar(out=msc2[:, :], in_=gbpp[:, :], scalar=0.0,
                               op=ALU.is_ge)
        v.tensor_scalar(out=ncb1pp[:, :], in0=cb1pp[:, :], scalar1=-1.0,
                        scalar2=-1.0, op0=ALU.mult, op1=ALU.add)
        v.tensor_scalar(out=msc[:, :], in0=msc[:, :], scalar1=2.0, scalar2=-1.0,
                        op0=ALU.mult, op1=ALU.add)
        v.tensor_scalar(out=msc2[:, :], in0=msc2[:, :], scalar1=2.0,
                        scalar2=-1.0, op0=ALU.mult, op1=ALU.add)
        v.tensor_single_scalar(out=cb1pp[:, :], in_=cb1pp[:, :], scalar=1.0,
                               op=ALU.add)
        v.tensor_mul(out=gapp[:, :], in0=gapp[:, :], in1=msc[:, :])
        v.tensor_mul(out=gbpp[:, :], in0=gbpp[:, :], in1=msc2[:, :])
        v.tensor_single_scalar(out=gapp[:, :], in_=gapp[:, :], scalar=0.25,
                               op=ALU.max)
        v.tensor_single_scalar(out=gbpp[:, :], in_=gbpp[:, :], scalar=0.25,
                               op=ALU.max)
        v.tensor_mul(out=gapp[:, :], in0=gapp[:, :], in1=msc[:, :])
        v.tensor_mul(out=gbpp[:, :], in0=gbpp[:, :],
                     in1=msc2[:, :]).then_inc(cS_modp.h, 1)

    O("vector", _modp)
    cS_modp.bump()

    def _ncaw_dump(cap):
        dma("sync", tap_h["ncaw"].ap()[cap, :, 0:8], ca1pp[:, :], c_tap)
        dma("sync", tap_h["ncaw"].ap()[cap, :, 8:16], nca1pp[:, :], c_tap)
        dma("sync", tap_h["ncaw"].ap()[cap, :, 16:24], can0[:, :], c_tap)
        dma("sync", tap_h["ncaw"].ap()[cap, :, 24:32], can1[:, :], c_tap)

    if "ncaw" in tap_h:
        W("sync", cS_modp)
        _ncaw_dump(0)

    def _win_dump(row):
        offs = [0]
        for tile, n in ((eqm, 128), (onesr, 128), (sel8, 1024), (convd, 1024),
                        (xprojT_t, 320), (dtprojT_t, 256), (acoef, 32)):
            o = offs[-1]
            dma("sync", tap_h["winf"].ap()[row:row + 1, o:o + n],
                tile[0:1, 0:n], c_tap)
            offs.append(o + n)
        dma("sync", tap_h["winb"].ap()[row:row + 1, :], selmat[0:1, :], c_tap)

    if "winf" in tap_h:
        W("sync", cS_const)
        W("sync", cS_ld)
        _win_dump(0)

    c1 = {k: ch("c1_" + k) for k in (
        "x", "x2", "stmm", "stev", "ar1w", "ar1", "ar1r", "rows_s", "rows",
        "bkA", "bkB", "rrep", "h1a", "h1b", "hsv", "u", "uev", "xp", "xpe",
        "ar2w", "ar2", "ar2r", "bcc", "dp", "spa", "spg", "dbuN", "brep",
        "brf", "crep", "crf", "dA", "dAf", "yc", "ycf", "yg", "ywr")}
    c1_a2a = [ch(f"c1_a2a{g}") for g in range(NQ)]

    k_dA = 0
    k_yc = 0

    for j in range(NJ):
        for b in range(NB):
            ci = j * NB + b
            rc = ci % 2

            if tap_h:
                for _e in ("sync", "scalar", "vector", "gpsimd", "tensor"):
                    W(_e, c_tap)

            W("sync", c1["u"], 2 * (ci - 1))
            W("sync", c1["hsv"], ci - 1)
            dma("sync",
                xs[rc][:, :].rearrange("p (dc l3) -> p dc l3", dc=2)[
                    :, :, 3:LCH].bitcast(F32R),
                xp1.ap()[b].rearrange("(dc p) l -> p dc l", p=128)[
                    :, :, j * LC:(j + 1) * LC], c1["x"])
            vx = c1["x"].n

            W("vector", c1["x"], vx)
            W("vector", c1["stmm"], ci)
            O("vector", lambda v, rc=rc: v.tensor_mul(
                out=x2t[:, :].rearrange("p (dc l) -> p dc l", dc=2),
                in0=xs[rc][:, :].rearrange("p (dc l3) -> p dc l3", dc=2)[:, :, 3:LCH],
                in1=xs[rc][:, :].rearrange("p (dc l3) -> p dc l3", dc=2)[:, :, 3:LCH])
              .then_inc(c1["x2"].h, 1))
            c1["x2"].bump()

            W("tensor", c1["x"], vx)
            W("tensor", c1["x2"])
            W("tensor", cS_const)
            W("tensor", c1["bkA"], 4 * ci)
            W("tensor", c1["xpe"], ci)

            def _stmm(t, rc=rc):
                for dc in range(2):
                    t.matmul(p1ps[0][0:1, :], onesc[:, :].bitcast(F32R),
                             xs[rc][:, dc * LCH + 3:(dc + 1) * LCH].bitcast(F32R),
                             start=(dc == 0), stop=(dc == 1))
                for dc in range(2):
                    mm = t.matmul(p1ps[1][0:1, :], onesc_bf[:, :],
                                  x2t[:, dc * LC:(dc + 1) * LC],
                                  start=(dc == 0), stop=(dc == 1))
                mm.then_inc(c1["stmm"].h, 1)

            O("tensor", _stmm)
            c1["stmm"].bump()

            W("scalar", c1["stmm"])
            W("scalar", c1["ar1w"], 16 * ci)
            def _stev(s):
                s.copy(out=statb[:, 0:LC], in_=p1ps[0][0:1, :]) \
                    .then_inc(c1["bkA"].h, 1)
                s.copy(out=statb[:, LC:2 * LC], in_=p1ps[1][0:1, :]) \
                    .then_inc(c1["bkB"].h, 1)

            O("scalar", _stev)
            c1["bkA"].bump()
            c1["bkB"].bump()
            W("sync", c1["bkB"], 3 * ci + 1)
            dma("sync", ar1_in.ap()[ci], statb[:, :], c1["ar1w"])
            if "stp" in tap_h:
                dma("sync", tap_h["stp"].ap()[ci], statb[:, :], c_tap)
            W("gpsimd", c1["ar1w"])
            coll_chain()
            O("gpsimd", lambda g, ci=ci: g.collective_compute(
                "AllReduce", ALU.add, replica_groups=[list(range(8))],
                ins=[ar1_in.ap()[ci].opt()], outs=[ar1_out.ap()[ci].opt()],
            ).then_inc(c1["ar1"].h, 1))
            c1["ar1"].bump()
            coll_mark(c1["ar1"])
            W("sync", c1["ar1"])
            W("sync", c1["rows_s"], ci)
            W("sync", c1["ar1w"], 16 * (ci + 1))
            dma("sync", statb[:, :], ar1_out.ap()[ci], c1["ar1r"])

            if "stats" in tap_h:
                W("sync", c1["ar1r"])
                dma("sync", tap_h["stats"].ap()[ci], statb[:, :], c_tap)

            W("scalar", c1["ar1r"])
            W("scalar", c1["rrep"], 2 * ci)
            if "rows" in tap_h:
                W("scalar", c_tap)

            def _rows_s(s):
                s.mul(out=rows[:, 0:LC].bitcast(F32R), in_=statb[:, 0:LC],
                      mul=INVD)
                s.mul(out=rows[:, 2 * LC:3 * LC].bitcast(F32R),
                      in_=statb[:, LC:2 * LC],
                      mul=INVD).then_inc(c1["rows_s"].h, 1)

            O("scalar", _rows_s)
            c1["rows_s"].bump()
            if "rows" in tap_h:
                pass
            W("vector", c1["rows_s"])

            def _rows_v(v):
                v.tensor_mul(out=rows[:, LC:2 * LC].bitcast(F32R),
                             in0=rows[:, 0:LC], in1=rows[:, 0:LC])
                v.tensor_sub(out=rows[:, 2 * LC:3 * LC].bitcast(F32R),
                             in0=rows[:, 2 * LC:3 * LC],
                             in1=rows[:, LC:2 * LC])
                v.tensor_single_scalar(out=rows[:, 2 * LC:3 * LC].bitcast(F32R),
                                       in_=rows[:, 2 * LC:3 * LC],
                                       scalar=EPS, op=ALU.add).then_inc(c1["rows"].h, 1)

            O("vector", _rows_v)
            c1["rows"].bump()
            W("scalar", c1["rows"])

            def _rows_s2(s):
                s.activation(out=rows[:, LC:2 * LC].bitcast(F32R),
                             in_=rows[:, 2 * LC:3 * LC], func=AF.Ln)
                s.activation(out=rows[:, LC:2 * LC].bitcast(F32R),
                             in_=rows[:, LC:2 * LC],
                             func=AF.Exp, scale=-0.5).then_inc(c1["rows"].h, 1)

            O("scalar", _rows_s2)
            c1["rows"].bump()
            W("vector", c1["rows"])
            O("vector", lambda v: v.tensor_mul(
                out=rows[:, 2 * LC:3 * LC].bitcast(F32R), in0=rows[:, 0:LC],
                in1=rows[:, LC:2 * LC]).then_inc(c1["rows"].h, 1))
            c1["rows"].bump()
            vrows = c1["rows"].n

            if "rows" in tap_h:
                W("sync", c1["rows"], vrows)
                dma("sync", tap_h["rows"].ap()[ci], rows[:, :], c_tap)

            W("tensor", c1["rows"], vrows)
            W("tensor", c1["bkB"], 3 * ci + 1)
            O("tensor", lambda t: t.matmul(
                p1ps[1][:, :], onesr[:, :].bitcast(F32R),
                rows[:, LC:2 * LC].bitcast(F32R), start=True, stop=True)
                .then_inc(c1["rrep"].h, 1))
            c1["rrep"].bump()
            v_rrep = c1["rrep"].n

            W("vector", cS_modp)
            W("vector", c1["rrep"], v_rrep)
            W("vector", c1["stmm"])

            def _h1a(v, rc=rc, b=b):
                for dc in range(2):
                    a = v.scalar_tensor_tensor(
                        out=xs[rc][:, dc * LCH + 3:(dc + 1) * LCH].bitcast(F32R),
                        in0=xs[rc][:, dc * LCH + 3:(dc + 1) * LCH],
                        scalar=ca1pp[:, (b * 2 + dc):(b * 2 + dc) + 1],
                        in1=p1ps[1][:, :], op0=ALU.mult, op1=ALU.mult)
                a.then_inc(c1["bkB"].h, 1)

            O("vector", _h1a)
            c1["bkB"].bump()

            if "h1m" in tap_h:
                W("scalar", c1["bkB"], 3 * ci + 2)
                dma("scalar",
                    tap_h["h1m"].ap()[ci].rearrange("(dc p) l -> p dc l", p=128),
                    xs[rc][:, :].rearrange("p (dc l3) -> p dc l3", dc=2)[:, :, 3:LCH],
                    c_tap)
                W("vector", c_tap)

            W("tensor", c1["bkB"], 3 * ci + 2)
            O("tensor", lambda t: t.matmul(
                p1ps[1][:, :], onesr[:, :].bitcast(F32R),
                rows[:, 2 * LC:3 * LC].bitcast(F32R), start=True, stop=True)
                .then_inc(c1["rrep"].h, 1))
            c1["rrep"].bump()
            v_mrep = c1["rrep"].n

            if "cst" in tap_h:
                W("sync", c1["rrep"], v_rrep)
                dma("sync", tap_h["cst"].ap()[ci:ci + 1, :], onesr[0:1, :], c_tap)

            if "ncaw" in tap_h and ci == 16:
                _ncaw_dump(1)

            W("vector", c1["rrep"], v_mrep)

            def _h1b(v, rc=rc, b=b, j=j):
                for dc in range(2):
                    a = v.scalar_tensor_tensor(
                        out=xs[rc][:, dc * LCH + 3:(dc + 1) * LCH].bitcast(F32R),
                        in0=p1ps[1][:, :],
                        scalar=nca1pp[:, (b * 2 + dc):(b * 2 + dc) + 1],
                        in1=xs[rc][:, dc * LCH + 3:(dc + 1) * LCH],
                        op0=ALU.mult, op1=ALU.add)
                a.then_inc(c1["bkB"].h, 1)
                for dc in range(2):
                    if j == 0:
                        a = v.memset(xs[rc][:, dc * LCH:dc * LCH + 3], 0.0)
                    else:
                        a = v.tensor_copy(
                            out=xs[rc][:, dc * LCH:dc * LCH + 3].bitcast(F32R),
                            in_=halo[:, (b * 2 + dc) * 3:(b * 2 + dc + 1) * 3])
                a.then_inc(c1["h1b"].h, 1)

            O("vector", _h1b)
            c1["bkB"].bump()
            c1["h1b"].bump()

            W("scalar", c1["bkB"], 3 * ci + 3)
            W("scalar", cS_modp)

            def _h1c(s, rc=rc, b=b):
                for dc in range(2):
                    a = s.activation(
                        out=xs[rc][:, dc * LCH + 3:(dc + 1) * LCH].bitcast(F32R),
                        in_=xs[rc][:, dc * LCH + 3:(dc + 1) * LCH],
                        func=AF.Identity,
                        bias=modsl[:, dc * NB + b:dc * NB + b + 1])
                a.then_inc(c1["h1a"].h, 1)

            O("scalar", _h1c)
            c1["h1a"].bump()

            W("vector", c1["h1a"])

            def _hsv(v, rc=rc, b=b):
                for dc in range(2):
                    a = v.tensor_copy(
                        out=halo[:, (b * 2 + dc) * 3:(b * 2 + dc + 1) * 3],
                        in_=xs[rc][:, dc * LCH + LC:(dc + 1) * LCH])
                a.then_inc(c1["hsv"].h, 1)

            O("vector", _hsv)
            c1["hsv"].bump()

            if "h1" in tap_h:
                W("scalar", c1["h1a"])
                dma("scalar", tap_h["h1"].ap()[ci].rearrange("(dc p) l -> p dc l", p=128),
                    xs[rc][:, :].rearrange("p (dc l3) -> p dc l3", dc=2)[:, :, 3:LCH],
                    c_tap)

            W("tensor", c1["h1a"])
            W("tensor", c1["h1b"])
            W("tensor", c1["bkA"], 4 * ci + 1)
            W("scalar", c1["brf"], 32 * ci)

            def _conv0(t, rc=rc):
                for k in range(4):
                    mm = t.matmul(
                        p1ps[0][:, :],
                        convd[:, k * 128:(k + 1) * 128].bitcast(F32R),
                        xs[rc][:, k:k + LC].bitcast(F32R),
                        start=(k == 0), stop=(k == 3))
                mm.then_inc(c1["u"].h, 1)

            O("tensor", _conv0)
            c1["u"].bump()
            W("scalar", c1["u"])
            W("scalar", c1["ywr"], 16 * ci)

            def _uev0(s, rc=rc):
                s.activation(out=ut[:, 0:LC].bitcast(F32R), in_=p1ps[0][:, :],
                             func=AF.Identity, bias=convb_t[:, 0:1])
                a = s.activation(out=yt[0][:, 0:LC], in_=p1ps[0][:, :],
                                 func=AF.Identity, scale=dskip_t[:, 0:1],
                                 bias=cbds_t[:, 0:1])
                a.then_inc(c1["uev"].h, 1)

            O("scalar", _uev0)
            c1["uev"].bump()
            W("tensor", c1["uev"])
            W("tensor", c1["bkA"], 4 * ci + 1)

            def _conv1(t, rc=rc):
                for k in range(4):
                    mm = t.matmul(
                        p1ps[0][:, :],
                        convd[:, (4 + k) * 128:(5 + k) * 128].bitcast(F32R),
                        xs[rc][:, LCH + k:LCH + k + LC].bitcast(F32R),
                        start=(k == 0), stop=(k == 3))
                mm.then_inc(c1["u"].h, 1)

            O("tensor", _conv1)
            c1["u"].bump()
            W("scalar", c1["u"])

            def _uev1(s, rc=rc):
                s.activation(out=ut[:, LC:2 * LC].bitcast(F32R), in_=p1ps[0][:, :],
                             func=AF.Identity, bias=convb_t[:, 1:2])
                a = s.activation(out=yt[0][:, LC:2 * LC], in_=p1ps[0][:, :],
                                 func=AF.Identity, scale=dskip_t[:, 1:2],
                                 bias=cbds_t[:, 1:2])
                a.then_inc(c1["uev"].h, 1)

            O("scalar", _uev1)
            c1["uev"].bump()
            v_uev = c1["uev"].n

            if "u" in tap_h:
                W("scalar", c1["uev"], v_uev)
                dma("scalar", tap_h["u"].ap()[ci].rearrange("(dc p) l -> p dc l", p=128),
                    ut[:, :].rearrange("p (dc l) -> p dc l", dc=2), c_tap)

            W("tensor", c1["uev"], v_uev)
            W("tensor", c1["bkA"], 4 * ci + 1)
            W("tensor", c1["bkB"], 3 * ci + 3)

            def _xproj(t):
                for dc in range(2):
                    t.matmul(p1ps[0][0:128, :],
                             xprojT_t[:, dc * 160:dc * 160 + 128].bitcast(F32R),
                             ut[:, dc * LC:(dc + 1) * LC].bitcast(F32R),
                             start=(dc == 0), stop=(dc == 1))
                for dc in range(2):
                    mm = t.matmul(p1ps[1][0:32, :],
                                  xprojT_t[:, dc * 160 + 128:(dc + 1) * 160]
                                  .bitcast(F32R),
                                  ut[:, dc * LC:(dc + 1) * LC].bitcast(F32R),
                                  start=(dc == 0), stop=(dc == 1))
                mm.then_inc(c1["xp"].h, 1)

            O("tensor", _xproj)
            c1["xp"].bump()

            W("scalar", c1["xp"])
            W("scalar", c1["ar2w"], 32 * ci)

            def _xpev(s):
                s.copy(out=xpa_sb[:, :], in_=p1ps[0][0:128, :]) \
                    .then_inc(c1["bkA"].h, 1)
                s.copy(out=xpb_sb[:, :], in_=p1ps[1][0:32, :]) \
                    .then_inc(c1["xpe"].h, 1)

            O("scalar", _xpev)
            c1["xpe"].bump()
            c1["bkA"].bump()

            W("sync", c1["xpe"])
            dma("sync", ar2_in.ap()[ci][0:128, :], xpa_sb[:, :], c1["ar2w"])
            dma("sync", ar2_in.ap()[ci][128:160, :], xpb_sb[:, :], c1["ar2w"])
            W("gpsimd", c1["ar2w"])
            coll_chain()
            O("gpsimd", lambda g, ci=ci: g.collective_compute(
                "AllReduce", ALU.add, replica_groups=[list(range(8))],
                ins=[ar2_in.ap()[ci].opt()], outs=[ar2_out.ap()[ci].opt()],
            ).then_inc(c1["ar2"].h, 1))
            c1["ar2"].bump()
            coll_mark(c1["ar2"])

            W("gpsimd", c1["ar2"])
            W("gpsimd", c1["dp"], 2 * ci)
            dma("gpsimd", dtt[:, :].bitcast(F32R), ar2_out.ap()[ci][0:128, :],
                c1["ar2r"])
            var2r = c1["ar2r"].n
            W("gpsimd", c1["ar2"])
            W("gpsimd", c1["crep"], 16 * ci)
            dma("gpsimd", xdblr_bf.ap()[ci], ar2_out.ap()[ci][128:160, :], c1["bcc"])
            W("sync", c1["bcc"])
            dma("sync", bct[:, :], xdblr_bf.ap()[ci], c1["ar2r"])
            var2rb = c1["ar2r"].n

            if "xdbl" in tap_h:
                W("sync", c1["ar2r"], var2r)
                dma("sync", tap_h["xdbl"].ap()[ci][0:128, :], dtt[:, :], c_tap)

            for dc in range(2):
                W("tensor", c1["ar2r"], var2r)
                W("tensor", c1["bkA"], 4 * ci + 2 + dc)
                O("tensor", lambda t, dc=dc: t.matmul(
                    p1ps[0][:, :],
                    dtprojT_t[:, dc * 128:(dc + 1) * 128].bitcast(F32R),
                    dtt[:, :].bitcast(F32R), start=True,
                    stop=True).then_inc(c1["dp"].h, 1))
                c1["dp"].bump()
                W("scalar", c1["dp"])
                W("scalar", c1["dA"], 32 * ci)
                W("scalar", c1["spg"], ci)

                def _sp(s, dc=dc):
                    sl_ = slice(dc * LC, (dc + 1) * LC)
                    s.activation(out=azt[:, sl_], in_=p1ps[0][:, :], func=AF.Abs,
                                 bias=dtb_t[:, dc:dc + 1])
                    a = s.activation(out=rlt[:, sl_], in_=p1ps[0][:, :],
                                     func=AF.Relu, bias=dtb_t[:, dc:dc + 1])
                    a.then_inc(c1["bkA"].h, 1)
                    s.activation(out=azt[:, sl_], in_=azt[:, sl_], func=AF.Exp,
                                 scale=-1.0)
                    a2 = s.activation(out=azt[:, sl_], in_=azt[:, sl_],
                                      func=AF.Ln, bias=1.0)
                    a2.then_inc(c1["spa"].h, 1)

                O("scalar", _sp)
                c1["spa"].bump()
                c1["bkA"].bump()

            W("gpsimd", c1["spa"])
            W("gpsimd", c1["dA"], 32 * ci)
            W("gpsimd", c1["brf"], 32 * ci)
            W("gpsimd", c1["xp"])
            W("gpsimd", c1["uev"], v_uev)

            def _del(g):
                g.tensor_add(out=azt[:, :], in0=azt[:, :], in1=rlt[:, :])
                g.tensor_mul(out=ut[:, :].bitcast(F32R), in0=azt[:, :],
                             in1=ut[:, :]).then_inc(c1["spg"].h, 1)

            O("gpsimd", _del)
            c1["spg"].bump()

            if "delta" in tap_h:
                W("scalar", c1["spg"])
                dma("scalar", tap_h["delta"].ap()[ci].rearrange("(dc p) l -> p dc l", p=128),
                    azt[:, :].rearrange("p (dc l) -> p dc l", dc=2), c_tap)

            for n in range(NST):
                W("tensor", c1["ar2r"], var2rb)
                W("tensor", c1["brf"], 32 * ci + 2 * n)
                O("tensor", lambda t, n=n: t.matmul(
                    repps[0][:, :], selmat[:, n * 128:(n + 1) * 128],
                    bct[:, :], start=True, stop=True).then_inc(c1["brep"].h, 1))
                c1["brep"].bump()
                W("tensor", c1["yc"], 32 * ci + 2 * n)
                O("tensor", lambda t, n=n: t.matmul(
                    repps[1][:, :], selmat[:, (16 + n) * 128:(17 + n) * 128],
                    bct[:, :], start=True, stop=True).then_inc(c1["crep"].h, 1))
                c1["crep"].bump()

                for dc in range(2):
                    scol = (b * 2 + dc) * NST + n
                    kA = k_dA
                    k_dA += 1
                    W("scalar", c1["spg"])
                    W("scalar", c1["dAf"], kA - 1)
                    if kA == 0:
                        W("vector", cS_modmm, v_wmod_done)
                        W("scalar", cS_acf)
                    O("scalar", lambda s, dc=dc, n=n, kA=kA: s.activation(
                        out=dAt[kA % 2][:, :],
                        in_=azt[:, dc * LC:(dc + 1) * LC],
                        func=AF.Exp,
                        scale=acoef[:, dc * NST + n:dc * NST + n + 1])
                        .then_inc(c1["dA"].h, 1))
                    c1["dA"].bump()

                    kY = k_yc
                    k_yc += 1
                    W("vector", c1["spg"])
                    W("vector", c1["brep"])
                    W("vector", c1["dA"])
                    W("vector", c1["ycf"], kY - 1)
                    W("vector", c1["crep"])

                    def _scan(v, dc=dc, kA=kA, kY=kY, j=j, scol=scol):
                        hx = kY % 2
                        v.tensor_mul(out=yct[hx][:, :].bitcast(F32R),
                                     in0=ut[:, dc * LC:(dc + 1) * LC],
                                     in1=repps[0][:, :]) \
                            .then_inc(c1["brf"].h, 1)
                        ini = 0.0 if j == 0 else state[:, scol:scol + 1]
                        v.tensor_tensor_scan(out=htraj[hx][:, :],
                                             data0=dAt[kA % 2][:, :],
                                             data1=yct[hx][:, :], initial=ini,
                                             op0=ALU.mult, op1=ALU.add) \
                            .then_inc(c1["dAf"].h, 1)
                        v.tensor_mul(out=yct[hx][:, :].bitcast(F32R),
                                     in0=htraj[hx][:, :],
                                     in1=repps[1][:, :]) \
                            .then_inc(c1["yc"].h, 1)
                        v.tensor_copy(out=state[:, scol:scol + 1],
                                      in_=htraj[hx][:, LC - 1:LC])

                    O("vector", _scan)
                    c1["dAf"].bump()
                    c1["yc"].bump()
                    c1["brf"].bump()

                    W("gpsimd", c1["yc"])
                    W("gpsimd", c1["uev"], v_uev)

                    def _yadd(g, dc=dc, kY=kY):
                        g.tensor_add(out=yt[0][:, dc * LC:(dc + 1) * LC],
                                     in0=yt[0][:, dc * LC:(dc + 1) * LC],
                                     in1=yct[kY % 2][:, :]) \
                            .then_inc(c1["ycf"].h, 1)

                    O("gpsimd", _yadd)
                    c1["ycf"].bump()

            g_ = j % 4
            dest = 2 * b + (0 if j < 4 else 1)
            W("sync", c1["ycf"], 32 * (ci + 1))
            dma("sync", a2a_in[g_].ap()[dest].rearrange("(dc p) l -> p dc l", p=128),
                yt[0][:, :].rearrange("p (dc l) -> p dc l", dc=2), c1["ywr"])
            if "y" in tap_h:
                dma("sync", tap_h["y"].ap()[ci].rearrange("(dc p) l -> p dc l", p=128),
                    yt[0][:, :].rearrange("p (dc l) -> p dc l", dc=2), c_tap)

        if j >= 4:
            g_ = j - 4
            W("gpsimd", c1["ywr"])
            coll_chain()

            def _a2a(g, g_=g_):
                g.collective_compute(
                    "AllToAll", ALU.bypass, replica_groups=[list(range(8))],
                    ins=[a2a_in[g_].ap().opt()], outs=[a2a_out[g_].ap().opt()],
                ).then_inc(c1_a2a[g_].h, 1)

            O("gpsimd", _a2a)
            c1_a2a[g_].bump()
            coll_mark(c1_a2a[g_])

    c_out = ch("c_out")
    if not skip_p3:
        c3 = {k: ch("c3_" + k) for k in (
            "yx", "x1", "x1w", "stp", "st", "rows_s", "rows", "upf",
            "x1r", "h2", "lg", "lgev", "t2", "sig", "sc", "wup", "wupf",
            "wdn", "wdnf", "bdw", "up", "act", "acts", "dn", "dnev", "er",
            "res")}

        k_up = [0]
        up_rel = []
        up_ring_rel = []

        def upps_acq():
            k = k_up[0]
            k_up[0] += 1
            if k >= 2:
                ch_, v_ = up_rel[k - 2]
                assert (ch_, v_) != (None, None), f"usage {k - 2} never released"
                W("tensor", ch_, v_)
            up_rel.append((None, None))
            return k

        def upps_tile(k):
            return upps[k % 2]

        def upps_mark(k, ch_, v_=None):
            assert up_rel[k] == (None, None)
            up_rel[k] = (ch_, ch_.n if v_ is None else v_)

        xpflat = xp3.ap().rearrange("(gg p) l -> gg p l", p=128)
        x1flat = x1_d.ap().rearrange("(gg p) l -> gg p l", p=128)
        outflat = out_h.ap().rearrange("(gg p) l -> gg p l", p=128)
        pstate = {}

        def prep_block(q):
            if tap_h:
                for _e in ("sync", "scalar", "vector", "gpsimd", "tensor"):
                    W(_e, c_tap)
            qr = q % 2
            a2flat = a2a_out[q].ap().rearrange("s d l -> (s d) l").rearrange("(gg p) l -> gg p l", p=128)

            k_st = upps_acq()
            k_st2 = upps_acq()
            st_bank = upps_tile(k_st)
            st_bank2 = upps_tile(k_st2)
            for dcg in range(16):
                gs = dcg % 2
                idx = q * 16 + dcg
                W("sync", c1_a2a[q])
                W("sync", c3["x1"], idx - 1)
                dma("sync", yq[:, gs * LC:(gs + 1) * LC], a2flat[dcg], c3["yx"])
                W("sync", c3["x1w"], 16 * (idx - 1))
                W("sync", c3["stp"], idx - 1)
                dma("sync", x1q[:, gs * LC:(gs + 1) * LC].bitcast(F32R),
                    xpflat[dcg][:, q * LC:(q + 1) * LC], c3["yx"])
                vyx = c3["yx"].n
                W("vector", c3["yx"], vyx)
                W("vector", cS_modp)
                W("vector", c3["stp"], idx - 1)

                def _x1(v, dcg=dcg, gs=gs):
                    sl_ = slice(gs * LC, (gs + 1) * LC)
                    v.scalar_tensor_tensor(out=x1q[:, sl_].bitcast(F32R),
                                           in0=yq[:, sl_],
                                           scalar=gapp[:, dcg:dcg + 1],
                                           in1=x1q[:, sl_],
                                           op0=ALU.mult, op1=ALU.add)
                    v.tensor_mul(out=x1h[:, sl_].bitcast(F32R), in0=x1q[:, sl_],
                                 in1=x1q[:, sl_]).then_inc(c3["x1"].h, 1)

                O("vector", _x1)
                c3["x1"].bump()
                W("sync", c3["x1"])
                dma("sync", x1flat[dcg][:, q * LC:(q + 1) * LC],
                    x1q[:, gs * LC:(gs + 1) * LC], c3["x1w"])
                if "x1" in tap_h:
                    dma("sync", tap_h["x1"].ap().rearrange(
                        "(gg p) l -> gg p l", p=128)[dcg][:, q * LC:(q + 1) * LC],
                        x1q[:, gs * LC:(gs + 1) * LC], c_tap)

                W("tensor", c3["x1"])

                def _st3(t, dcg=dcg, gs=gs, bank=st_bank, bank2=st_bank2):
                    sl_ = slice(gs * LC, (gs + 1) * LC)
                    t.matmul(bank[0:1, :], onesc[:, :].bitcast(F32R),
                             x1q[:, sl_].bitcast(F32R), start=(dcg == 0),
                             stop=(dcg == 15))
                    mm = t.matmul(bank2[0:1, :], onesc[:, :].bitcast(F32R),
                                  x1h[:, sl_].bitcast(F32R), start=(dcg == 0),
                                  stop=(dcg == 15))
                    mm.then_inc(c3["stp"].h, 1)

                O("tensor", _st3)
                c3["stp"].bump()

            W("scalar", c3["stp"], 16 * (q + 1))

            def _r3a(s, bank=st_bank, bank2=st_bank2):
                s.copy(out=stat3[:, 0:LC], in_=bank[0:1, :])
                s.copy(out=stat3[:, LC:2 * LC], in_=bank2[0:1, :])
                s.mul(out=rows3[:, 0:LC].bitcast(F32R),
                      in_=stat3[:, 0:LC], mul=INVD)
                s.mul(out=rows3[:, 2 * LC:3 * LC].bitcast(F32R),
                      in_=stat3[:, LC:2 * LC], mul=INVD) \
                    .then_inc(c3["rows_s"].h, 1)

            O("scalar", _r3a)
            c3["rows_s"].bump()
            upps_mark(k_st, c3["rows_s"])
            upps_mark(k_st2, c3["rows_s"])
            W("vector", c3["rows_s"])

            def _r3c(v):
                v.tensor_mul(out=rows3[:, LC:2 * LC].bitcast(F32R),
                             in0=rows3[:, 0:LC], in1=rows3[:, 0:LC])
                v.tensor_sub(out=rows3[:, 2 * LC:3 * LC].bitcast(F32R),
                             in0=rows3[:, 2 * LC:3 * LC],
                             in1=rows3[:, LC:2 * LC])
                v.tensor_single_scalar(out=rows3[:, 2 * LC:3 * LC].bitcast(F32R),
                                       in_=rows3[:, 2 * LC:3 * LC], scalar=EPS,
                                       op=ALU.add).then_inc(c3["rows"].h, 1)

            O("vector", _r3c)
            c3["rows"].bump()
            W("scalar", c3["rows"])

            def _r3d(s):
                s.activation(out=rows3[:, LC:2 * LC].bitcast(F32R),
                             in_=rows3[:, 2 * LC:3 * LC], func=AF.Ln)
                s.activation(out=rows3[:, LC:2 * LC].bitcast(F32R),
                             in_=rows3[:, LC:2 * LC],
                             func=AF.Exp, scale=-0.5).then_inc(c3["rows"].h, 1)

            O("scalar", _r3d)
            c3["rows"].bump()
            W("vector", c3["rows"])
            O("vector", lambda v: v.tensor_mul(
                out=rows3[:, 2 * LC:3 * LC].bitcast(F32R), in0=rows3[:, 0:LC],
                in1=rows3[:, LC:2 * LC]).then_inc(c3["rows"].h, 1))
            c3["rows"].bump()
            v3rows = c3["rows"].n

            k_rr = upps_acq()
            rrep_bank = upps_tile(k_rr)
            W("tensor", c3["rows"], v3rows)
            O("tensor", lambda t, bank=rrep_bank: t.matmul(
                bank[:, :], onesr[:, :].bitcast(F32R),
                rows3[:, LC:2 * LC].bitcast(F32R), start=True, stop=True))
            k_mr = upps_acq()
            mrep_bank = upps_tile(k_mr)
            O("tensor", lambda t, bank=mrep_bank: t.matmul(
                bank[:, :], onesr[:, :].bitcast(F32R),
                rows3[:, 2 * LC:3 * LC].bitcast(F32R), start=True,
                stop=True).then_inc(c3["rows"].h, 1))
            c3["rows"].bump()
            v3rep = c3["rows"].n

            for dcg in range(16):
                idx = q * 16 + dcg
                W("sync", c3["x1w"], 16 * (idx + 1))
                W("sync", c3["h2"], idx)
                dma("sync", x1r[:, :],
                    x1flat[dcg][:, q * LC:(q + 1) * LC], c3["x1r"])
                W("vector", c3["x1r"])
                W("vector", c3["rows"], v3rep)
                W("vector", cS_modp)

                def _h2(v, dcg=dcg):
                    v.scalar_tensor_tensor(out=x1r[:, :], in0=x1r[:, :],
                                           scalar=cb1pp[:, dcg:dcg + 1],
                                           in1=rrep_bank[:, :],
                                           op0=ALU.mult, op1=ALU.mult)
                    v.scalar_tensor_tensor(out=x1r[:, :], in0=mrep_bank[:, :],
                                           scalar=ncb1pp[:, dcg:dcg + 1],
                                           in1=x1r[:, :], op0=ALU.mult,
                                           op1=ALU.add).then_inc(c3["x1r"].h, 1)

                O("vector", _h2)
                c3["x1r"].bump()
                W("scalar", c3["x1r"])
                if q >= 1:
                    W("scalar", c3["up"], 72 * q)
                    W("scalar", c3["lg"], 2 * q)

                def _h2b(s, dcg=dcg):
                    s.activation(out=h2q[:, dcg * LC:(dcg + 1) * LC],
                                 in_=x1r[:, :], func=AF.Identity,
                                 bias=sbpp[:, dcg:dcg + 1]).then_inc(c3["h2"].h, 1)

                O("scalar", _h2b)
                c3["h2"].bump()
            v_h2q = c3["h2"].n
            upps_mark(k_rr, c3["x1r"])
            upps_mark(k_mr, c3["x1r"])

            if "h2" in tap_h:
                for dcg in range(16):
                    W("scalar", c3["h2"], v_h2q)
                    dma("scalar", tap_h["h2"].ap().rearrange(
                        "(gg p) l -> gg p l", p=128)[dcg][:, q * LC:(q + 1) * LC],
                        h2q[:, dcg * LC:(dcg + 1) * LC], c_tap)

            k_lg = upps_acq()
            lg_bank = upps_tile(k_lg)
            W("tensor", c3["h2"], v_h2q)
            W("tensor", cS_gl)

            def _lg1(t, bank=lg_bank):
                for kc in range(16):
                    mm = t.matmul(bank[0:8, :], wg1_t[:, kc * 8:(kc + 1) * 8],
                                  h2q[:, kc * LC:(kc + 1) * LC],
                                  start=(kc == 0), stop=(kc == 15))
                mm.then_inc(c3["lg"].h, 1)

            O("tensor", _lg1)
            c3["lg"].bump()
            W("scalar", c3["lg"])

            def _lgev1(s, bank=lg_bank):
                s.activation(out=th1[:, :], in_=bank[0:8, :], func=AF.Tanh) \
                    .then_inc(c3["lgev"].h, 1)

            O("scalar", _lgev1)
            c3["lgev"].bump()
            upps_mark(k_lg, c3["lgev"])
            k_lg2 = upps_acq()
            lg2_bank = upps_tile(k_lg2)
            W("tensor", c3["lgev"])
            O("tensor", lambda t, bank=lg2_bank: t.matmul(
                bank[0:8, :], wg2_t[:, :], th1[:, :], start=True,
                stop=True).then_inc(c3["lg"].h, 1))
            c3["lg"].bump()
            W("scalar", c3["lg"])

            def _lgev2(s, bank=lg2_bank):
                s.copy(out=lgt[:, :], in_=bank[0:8, :]) \
                    .then_inc(c3["lgev"].h, 1)

            O("scalar", _lgev2)
            c3["lgev"].bump()
            upps_mark(k_lg2, c3["lgev"])
            v_lgt = c3["lgev"].n

            for tc in range(4):
                k_tp = upps_acq()
                tp_bank = upps_tile(k_tp)
                W("tensor", c3["lgev"], v_lgt)
                W("tensor", cS_const)
                O("tensor", lambda t, tc=tc, bank=tp_bank: t.transpose(
                    bank[0:128, 0:8], lgt[:, tc * 128:(tc + 1) * 128],
                    eqm[0:8, 0:8]).then_inc(c3["t2"].h, 1))
                c3["t2"].bump()
                W("vector", c3["t2"])

                def _top2(v, bank=tp_bank):
                    c = tsc

                    def pad():
                        v.engine_nop()
                        v.engine_nop()
                        v.engine_nop()
                    v.tensor_copy(out=tl[:, :], in_=bank[0:128, 0:8])
                    pad()
                    v.tensor_reduce(out=c[:, 16:17], in_=tl[:, :], axis=AX.X,
                                    op=ALU.max)
                    pad()
                    v.tensor_single_scalar(out=msk8[:, 0:8], in_=tl[:, :],
                                           scalar=c[:, 16:17], op=ALU.is_equal)
                    pad()
                    v.select(out=c[:, 32:40], mask=msk8[:, 0:8],
                             on_true=iota8_t[:, :], on_false=c8_t[:, :])
                    pad()
                    v.tensor_reduce(out=c[:, 17:18], in_=c[:, 32:40], axis=AX.X,
                                    op=ALU.min)
                    pad()
                    v.tensor_single_scalar(out=c[:, 40:48], in_=iota8_t[:, :],
                                           scalar=c[:, 17:18], op=ALU.is_equal)
                    v.tensor_single_scalar(out=msk8[:, 8:16], in_=iota8_t[:, :],
                                           scalar=c[:, 17:18], op=ALU.is_equal)
                    pad()
                    v.select(out=c[:, 48:56], mask=msk8[:, 8:16],
                             on_true=negbig_t[:, :], on_false=tl[:, :])
                    pad()
                    v.tensor_reduce(out=c[:, 18:19], in_=c[:, 48:56], axis=AX.X,
                                    op=ALU.max)
                    pad()
                    v.tensor_single_scalar(out=msk8[:, 16:24], in_=c[:, 48:56],
                                           scalar=c[:, 18:19], op=ALU.is_equal)
                    pad()
                    v.select(out=c[:, 64:72], mask=msk8[:, 16:24],
                             on_true=iota8_t[:, :], on_false=c8_t[:, :])
                    pad()
                    v.tensor_reduce(out=c[:, 19:20], in_=c[:, 64:72], axis=AX.X,
                                    op=ALU.min)
                    pad()
                    v.tensor_single_scalar(out=c[:, 72:80], in_=iota8_t[:, :],
                                           scalar=c[:, 19:20], op=ALU.is_equal)
                    v.tensor_sub(out=c[:, 20:21], in0=c[:, 16:17],
                                 in1=c[:, 18:19]).then_inc(c3["t2"].h, 1)

                O("vector", _top2)
                c3["t2"].bump()
                W("scalar", c3["t2"])

                def _sig(s):
                    s.activation(out=tsc[:, 22:23], in_=tsc[:, 20:21],
                                 func=AF.Sigmoid)
                    s.activation(out=tsc[:, 23:24], in_=tsc[:, 20:21],
                                 func=AF.Sigmoid, scale=-1.0).then_inc(c3["sig"].h, 1)

                O("scalar", _sig)
                c3["sig"].bump()
                W("vector", c3["sig"])

                def _sc2(v, bank=tp_bank):
                    v.tensor_single_scalar(out=tsc[:, 80:88], in_=tsc[:, 40:48],
                                           scalar=tsc[:, 22:23], op=ALU.mult)
                    v.engine_nop()
                    v.engine_nop()
                    v.engine_nop()
                    v.scalar_tensor_tensor(out=tsc[:, 80:88], in0=tsc[:, 72:80],
                                           scalar=tsc[:, 23:24],
                                           in1=tsc[:, 80:88],
                                           op0=ALU.mult, op1=ALU.add) \
                        .then_inc(c3["t2"].h, 1)

                O("vector", _sc2)
                c3["t2"].bump()
                upps_mark(k_tp, c3["t2"])
                k_tb = upps_acq()
                tb_bank = upps_tile(k_tb)
                W("tensor", c3["t2"])
                O("tensor", lambda t, bank=tb_bank: t.transpose(
                    bank[0:8, 0:128], tsc[:, 80:88],
                    eqm[:, :]).then_inc(c3["t2"].h, 1))
                c3["t2"].bump()
                W("scalar", c3["t2"])
                if q >= 2:
                    W("scalar", c3["wdnf"], 64 * (q - 1))

                def _scev(s, tc=tc, qr=qr, bank=tb_bank):
                    s.copy(out=scoreq[qr][:, tc * 128:(tc + 1) * 128]
                           .bitcast(F32R), in_=bank[0:8, 0:128]) \
                        .then_inc(c3["sc"].h, 1)

                O("scalar", _scev)
                c3["sc"].bump()
                upps_mark(k_tb, c3["sc"])
            v_sc = c3["sc"].n

            if "scores" in tap_h:
                W("scalar", c3["sc"], v_sc)
                dma("scalar", tap_h["scores"].ap()[:, q * LC:(q + 1) * LC],
                    scoreq[qr][:, :], c_tap)
            pstate[q] = dict(qr=qr, v_sc=v_sc, v_h2q=v_h2q)

        def up_block(q):
            qr = pstate[q]["qr"]
            v_sc = pstate[q]["v_sc"]
            v_h2q = pstate[q]["v_h2q"]
            for mk in (up_rel[-2], up_rel[-1]):
                W("tensor", mk[0], mk[1])
            for ei in range(E):
                sr_bank = upps[0]
                W("tensor", c3["sc"], v_sc)
                if ei >= 1:
                    W("tensor", c3["acts"], c3["acts"].n)
                O("tensor", lambda t, ei=ei, qr=qr, bank=sr_bank: t.matmul(
                    bank[:, :], sel8[:, ei * 128:(ei + 1) * 128].bitcast(F32R),
                    scoreq[qr][:, :].bitcast(F32R), start=True,
                    stop=True).then_inc(c3["up"].h, 1))
                c3["up"].bump()
                v_srep = c3["up"].n
                for mc in range(8):
                    gidx = q * 64 + ei * 8 + mc
                    if gidx >= 2:
                        W("sync", c3["up"], up_ring_rel[gidx - 2])
                    dma("sync", wup_ring[gidx % 2][:, :].rearrange(
                        "p (kc f) -> p kc f", kc=16),
                        wup_bf.ap()[ei, mc].rearrange("kc p f -> p kc f"), c3["wup"])
                    vw = c3["wup"].n
                    ub_bank = upps[1]
                    W("tensor", c3["act"], c3["act"].n)
                    W("tensor", c3["wup"], vw)
                    W("tensor", c3["h2"], v_h2q)

                    def _up(t, gidx=gidx, bank=ub_bank):
                        for kc in range(16):
                            mm = t.matmul(
                                bank[:, :],
                                wup_ring[gidx % 2][:, kc * 128:(kc + 1) * 128],
                                h2q[:, kc * LC:(kc + 1) * LC],
                                start=(kc == 0), stop=(kc == 15))
                        mm.then_inc(c3["up"].h, 1)

                    O("tensor", _up)
                    c3["up"].bump()
                    up_ring_rel.append(c3["up"].n)
                    assert len(up_ring_rel) == gidx + 1
                    W("scalar", c3["up"])
                    if q >= 1:
                        W("scalar", c3["wdnf"], 64 * q)

                    def _silu(s, ei=ei, mc=mc, bank=ub_bank):
                        s.activation(
                            out=act_t[:, (ei * 8 + mc) * LC:(ei * 8 + mc + 1) * LC],
                            in_=bank[:, :], func=AF.Silu,
                            bias=bup_t[:, ei * 8 + mc:ei * 8 + mc + 1]) \
                            .then_inc(c3["act"].h, 1)

                    O("scalar", _silu)
                    c3["act"].bump()
                    W("vector", c3["act"])
                    W("vector", c3["up"], v_srep)

                    def _ascale(v, ei=ei, mc=mc, bank=sr_bank):
                        v.tensor_mul(
                            out=act_t[:, (ei * 8 + mc) * LC:(ei * 8 + mc + 1) * LC],
                            in0=act_t[:, (ei * 8 + mc) * LC:(ei * 8 + mc + 1) * LC],
                            in1=bank[:, :]).then_inc(c3["acts"].h, 1)

                    O("vector", _ascale)
                    c3["acts"].bump()
            k_syn0 = upps_acq()
            k_syn1 = upps_acq()
            upps_mark(k_syn0, c3["acts"])
            upps_mark(k_syn1, c3["act"])
            pstate[q]["v_acts"] = c3["acts"].n

        def down_block(q):
            qr = pstate[q]["qr"]
            v_sc = pstate[q]["v_sc"]
            v_acts = pstate[q]["v_acts"]
            for og in range(8):
                widx = q * 8 + og
                W("sync", c3["wdnf"], 8 * (widx - 1))
                dma("sync", bdg[widx % 2][:, :].bitcast(F32R),
                    bdown_h.ap()[:, og * 256:(og + 1) * 256], c3["bdw"])
                vbd = c3["bdw"].n
                W("tensor", c3["bdw"], vbd)
                W("tensor", c3["acts"], v_acts)
                W("tensor", c3["sc"], v_sc)
                W("tensor", c3["dnev"], 2 * widx)

                def _dstart(t, qr=qr, widx=widx):
                    for ocl in range(2):
                        t.matmul(dnps[ocl][:, :],
                                 bdg[widx % 2][:, ocl * 128:(ocl + 1) * 128]
                                 .bitcast(F32R),
                                 scoreq[qr][:, :].bitcast(F32R),
                                 start=True, stop=False)

                O("tensor", _dstart)
                for ei in range(E):
                    widx2 = widx * 8 + ei
                    W("sync", c3["wdnf"], widx2 - 1)
                    dma("sync", wdn_ring[widx2 % 2][:, :].rearrange(
                        "p (ocl hc f) -> p ocl hc f", ocl=2, f=128),
                        wdown_bf.ap()[og, ei].rearrange("ocl hc p f -> p ocl hc f"),
                        c3["wdn"])
                    vw = c3["wdn"].n
                    W("tensor", c3["wdn"], vw)

                    def _down(t, ei=ei, widx2=widx2):
                        ring = wdn_ring[widx2 % 2]
                        for ocl in range(2):
                            for hc in range(8):
                                tix = ocl * 8 + hc
                                mm = t.matmul(
                                    dnps[ocl][:, :],
                                    ring[:, tix * 128:(tix + 1) * 128],
                                    act_t[:, (ei * 8 + hc) * LC:(ei * 8 + hc + 1) * LC],
                                    start=False,
                                    stop=(ei == E - 1 and hc == 7))
                                if hc == 7 and ocl == 1:
                                    mm.then_inc(c3["wdnf"].h, 1)

                    O("tensor", _down)
                    c3["wdnf"].bump()

                for ocl in range(2):
                    oc = og * 2 + ocl
                    kres = 2 * widx + ocl
                    W("sync", c3["x1w"])
                    W("sync", c3["res"], kres)
                    dma("sync", x1e[0][:, :],
                        x1flat[oc][:, q * LC:(q + 1) * LC], c3["er"])
                    if "x1e" in tap_h and q == 0:
                        W("sync", c3["er"])
                        dma("sync", tap_h["x1e"].ap()[oc], x1e[0][:, :], c_tap)
                    W("vector", c3["er"])
                    W("vector", c3["wdnf"], 8 * (widx + 1))
                    W("vector", c_out, 16 * kres)

                    def _res(v, ocl=ocl, oc=oc):
                        v.scalar_tensor_tensor(out=evt[0][:, :],
                                               in0=dnps[ocl][:, :],
                                               scalar=gbpp[:, oc:oc + 1],
                                               in1=x1e[0][:, :],
                                               op0=ALU.mult, op1=ALU.add) \
                            .then_inc(c3["dnev"].h, 1)
                        v.tensor_copy(out=evt[0][:, :], in_=evt[0][:, :]) \
                            .then_inc(c3["res"].h, 1)

                    O("vector", _res)
                    c3["dnev"].bump()
                    c3["res"].bump()
                    W("sync", c3["res"])
                    if "evt" in tap_h and q == 0:
                        dma("sync", tap_h["evt"].ap()[oc], evt[0][:, :], c_tap)
                        W("sync", c_tap)
                    dma("sync", outflat[oc][:, q * LC:(q + 1) * LC],
                        evt[0][:, :], c_out)

        prep_block(0)
        for q in range(NQ):
            up_block(q)
            if q + 1 < NQ:
                prep_block(q + 1)
            down_block(q)

    if "ncaw" in tap_h:
        if skip_p3:
            for g_ in range(NQ):
                W("sync", c1_a2a[g_])
            W("sync", c1["ywr"])
        else:
            W("sync", c_out)
        _ncaw_dump(2)

    if "cmod" in tap_h:
        if skip_p3:
            for g_ in range(NQ):
                W("sync", c1_a2a[g_])
            W("sync", c1["ywr"])
        else:
            W("sync", c_out)
        dma("sync", tap_h["cmod"].ap()[:, 0:48], modsl[:, :], c_tap)
        dma("sync", tap_h["cmod"].ap()[:, 48:56], ca1pp[:, :], c_tap)
        dma("sync", tap_h["cmod"].ap()[:, 56:64], nca1pp[:, :], c_tap)

    W("scalar", c_out)
    if "winf" in tap_h:
        W("sync", c_out)
        _win_dump(1)
    if tap_h:
        W("scalar", c_tap)
    if skip_p3:
        for g_ in range(NQ):
            W("scalar", c1_a2a[g_])
        W("scalar", c1["ywr"])

    block = e(nc.Block())
    with allow_ici_all_to_all():
        P.finish(block)
    stk.close()
    return nc


def kernel(**inputs):
    in_maps = stage_inputs(inputs)
    nc = build_nc(taps=tuple(DEBUG_TAPS), skip_p3=SKIP_P3)
    with allow_ici_all_to_all():
        res = run_bass_kernel_spmd(nc, in_maps, core_ids=list(range(8)))
    out = np.empty((NB, L, D), dtype=np.float32)
    for c in range(8):
        bc, hc3 = c // 2, c % 2
        out[bc, hc3 * TOK3:(hc3 + 1) * TOK3, :] = res.results[c]["out"].T
    kernel.last_results = res.results
    return out


# version 07
# speedup vs baseline: 1.0196x; 1.0047x over previous
import numpy as np
from contextlib import ExitStack

import concourse.bass as bass
import concourse.mybir as mybir
from concourse.bass_utils import run_bass_kernel_spmd
from concourse.aot_env import allow_ici_all_to_all

F32 = mybir.dt.float32
F32R = mybir.dt.float32r
BF16 = mybir.dt.bfloat16
AF = mybir.ActivationFunctionType
ALU = mybir.AluOpType
AX = mybir.AxisListType

D, DSL, NB, L, NST, RK, E, HE = 2048, 256, 4, 4096, 16, 128, 8, 1024
T = NB * L
LC = 512
LCH = LC + 3
NJ = L // LC
NCH = NB * NJ
TOK3 = 2048
NQ = 4
EPS = 1e-5
INVD = 1.0 / D

DEBUG_TAPS = []
SKIP_P3 = False


def _r(x):
    return np.ascontiguousarray(x)


def stage_inputs(inputs):
    x = np.asarray(inputs["L0_attn_x"], dtype=np.float32)
    x_dmaj = _r(x.transpose(0, 2, 1))
    conv_w = np.asarray(inputs["L0_attn_conv_w"], np.float32)[:, 0, :]
    conv_b = np.asarray(inputs["L0_attn_conv_b"], np.float32)
    xprojT = _r(np.asarray(inputs["L0_attn_xproj_w"], np.float32).T)
    dtprojT = _r(np.asarray(inputs["L0_attn_dtproj_w"], np.float32).T)
    dtproj_b = np.asarray(inputs["L0_attn_dtproj_b"], np.float32)
    A_log = np.asarray(inputs["L0_attn_A_log"], np.float32)
    D_skip = np.asarray(inputs["L0_attn_D_skip"], np.float32)
    Wg1T = _r(np.asarray(inputs["L0_ffn_Wg1"], np.float32).T)
    Wg2T = _r(np.asarray(inputs["L0_ffn_Wg2"], np.float32).T)
    Wup = np.asarray(inputs["L0_ffn_Wup"], np.float32)
    bup = np.asarray(inputs["L0_ffn_bup"], np.float32)
    Wdown = np.asarray(inputs["L0_ffn_Wdown"], np.float32)
    bdown = np.asarray(inputs["L0_ffn_bdown"], np.float32)
    cc = np.asarray(inputs["L0_c"], np.float32)
    wmod = np.asarray(inputs["L0_wmod"], np.float32)

    WupT = Wup.transpose(0, 2, 1)
    wup_tiles = _r(WupT.reshape(E, 16, 128, 8, 128).transpose(0, 3, 1, 2, 4))
    WdownT = Wdown.transpose(0, 2, 1)
    a = WdownT.reshape(E, 8, 128, 16, 128).transpose(3, 0, 1, 2, 4)
    wdown_tiles = _r(a.reshape(8, 2, E, 8, 128, 128).transpose(0, 2, 1, 3, 4, 5))
    wmod_r = wmod.reshape(D, 6, 8, DSL)
    cT = _r(cc.T)

    in_maps = []
    for c in range(8):
        sl = slice(DSL * c, DSL * (c + 1))
        bc, hc3 = c // 2, c % 2
        bsel = np.zeros((128, NB), np.float32)
        bsel[:, bc] = 1.0
        in_maps.append({
            "xp1": _r(x_dmaj[:, sl, :]),
            "xp3": _r(x_dmaj[bc, :, hc3 * TOK3:(hc3 + 1) * TOK3]),
            "cT": cT,
            "wmod_sl": _r(wmod_r[:, :, c, :].reshape(D, 6 * DSL)),
            "xprojT_sl": _r(xprojT[sl, :]),
            "dtprojT_sl": _r(dtprojT[:, sl]),
            "dtproj_b_sl": _r(dtproj_b[sl].reshape(2, 128).T),
            "conv_w_sl": _r(conv_w[sl].reshape(2, 128, 4).transpose(1, 0, 2)
                            .reshape(128, 8)),
            "conv_b_sl": _r(conv_b[sl].reshape(2, 128).T),
            "A_log_sl": _r(A_log[sl].reshape(2, 128, NST).transpose(1, 0, 2)
                           .reshape(128, 2 * NST)),
            "D_skip_sl": _r(D_skip[sl].reshape(2, 128).T),
            "Wg1T_t": _r(Wg1T.reshape(16, 128, 8)),
            "Wg2T_t": Wg2T,
            "wup_tiles": wup_tiles,
            "wdown_tiles": wdown_tiles,
            "bup_t": _r(bup.reshape(E, 8, 128).transpose(2, 0, 1).reshape(128, E * 8)),
            "bdown_t": bdown,
            "iota8": _r(np.tile(np.arange(8, dtype=np.float32), (128, 1))),
            "iotap": _r(np.arange(128, dtype=np.float32).reshape(128, 1)),
            "iotap32": _r(np.arange(32, dtype=np.float32).reshape(32, 1)),
            "bsel": bsel,
        })
    return in_maps


class Ch:
    def __init__(self, stk, nc, name):
        self.h = stk.enter_context(nc.semaphore(name))
        self.n = 0

    def bump(self, k=1):
        self.n += k
        return self.n


class Prog:
    ENGS = ("sync", "scalar", "vector", "gpsimd", "tensor")

    def __init__(self):
        self.th = {e: [] for e in self.ENGS}
        self.waited = {e: {} for e in self.ENGS}

    def on(self, eng, fn):
        self.th[eng].append(fn)

    def wait(self, eng, ch, val=None):
        v = ch.n if val is None else val
        if v <= 0:
            return
        if self.waited[eng].get(id(ch), -1) >= v:
            return
        self.waited[eng][id(ch)] = v
        self.on(eng, lambda e, h=ch.h, v=v: e.wait_ge(h, v))

    def finish(self, block):
        for name in self.ENGS:
            fns = self.th[name]
            if fns:
                getattr(block, name)(lambda e, fns=fns: [f(e) for f in fns])


def build_nc(taps=(), skip_p3=False):
    nc = bass.Bass("TRN2", target_bir_lowering=False,
                   dynamic_dma_scratch_size=10240)
    stk = ExitStack()
    e = stk.enter_context

    def dram(name, shape, dtype=F32, kind="Internal", addr_space="Local"):
        return nc.dram_tensor(name, list(shape), dtype, kind=kind,
                              addr_space=addr_space)

    xp1 = dram("xp1", [NB, DSL, L], F32R, kind="ExternalInput")
    xp3 = dram("xp3", [D, TOK3], F32R, kind="ExternalInput")
    cT_h = dram("cT", [D, NB], kind="ExternalInput")
    wmod_h = dram("wmod_sl", [D, 1536], F32R, kind="ExternalInput")
    xprojT_h = dram("xprojT_sl", [DSL, 160], F32R, kind="ExternalInput")
    dtprojT_h = dram("dtprojT_sl", [RK, DSL], F32R, kind="ExternalInput")
    dtb_h = dram("dtproj_b_sl", [128, 2], kind="ExternalInput")
    convw_h = dram("conv_w_sl", [128, 8], kind="ExternalInput")
    convb_h = dram("conv_b_sl", [128, 2], kind="ExternalInput")
    alog_h = dram("A_log_sl", [128, 2 * NST], kind="ExternalInput")
    dskip_h = dram("D_skip_sl", [128, 2], kind="ExternalInput")
    wg1_h = dram("Wg1T_t", [16, 128, 8], kind="ExternalInput")
    wg2_h = dram("Wg2T_t", [8, 8], kind="ExternalInput")
    wup_h = dram("wup_tiles", [E, 8, 16, 128, 128], kind="ExternalInput")
    wdown_h = dram("wdown_tiles", [8, E, 2, 8, 128, 128], kind="ExternalInput")
    bup_h = dram("bup_t", [128, E * 8], kind="ExternalInput")
    bdown_h = dram("bdown_t", [8, D], F32R, kind="ExternalInput")
    iota8_h = dram("iota8", [128, 8], kind="ExternalInput")
    iotap_h = dram("iotap", [128, 1], kind="ExternalInput")
    iotap32_h = dram("iotap32", [32, 1], kind="ExternalInput")
    bsel_h = dram("bsel", [128, NB], kind="ExternalInput")
    out_h = dram("out", [D, TOK3], kind="ExternalOutput")

    tap_h = {}
    for tname, shape in (("y", [NCH, DSL, LC]), ("h1", [NCH, DSL, LC]),
                         ("u", [NCH, DSL, LC]), ("delta", [NCH, DSL, LC]),
                         ("xdbl", [NCH, 160, LC]), ("mod", [128, 48]),
                         ("scores", [8, TOK3]), ("x1", [D, TOK3]),
                         ("stp", [NCH, 2, LC]), ("stats", [NCH, 2, LC]),
                         ("rows", [NCH, 1, 3 * LC]), ("h1m", [NCH, DSL, LC]),
                         ("cst", [NCH, 128]), ("cmod", [128, 64]),
                         ("ncaw", [3, 128, 32]), ("x1e", [16, 128, LC]),
                         ("evt", [16, 128, LC])):
        if tname in taps:
            tap_h[tname] = dram("tap_" + tname, shape, kind="ExternalOutput")
    if "h2" in taps:
        tap_h["h2"] = dram("tap_h2", [D, TOK3], BF16, kind="ExternalOutput")
    if "win" in taps:
        tap_h["winf"] = dram("tap_winf", [2, 2912], kind="ExternalOutput")
        tap_h["winb"] = dram("tap_winb", [2, 4096], BF16, kind="ExternalOutput")

    wup_bf = dram("wup_bf", [E, 8, 16, 128, 128], BF16)
    wdown_bf = dram("wdown_bf", [8, E, 2, 8, 128, 128], BF16)
    wg1_bf = dram("wg1_bf", [16, 128, 8], BF16)
    wg2_bf = dram("wg2_bf", [8, 8], BF16)
    xdblr_bf = dram("xdblr_bf", [NCH, 32, LC], BF16)
    ar1_in = dram("ar1_in", [NCH, 2, LC])
    ar1_out = dram("ar1_out", [NCH, 2, LC], addr_space="Shared")
    ar2_in = dram("ar2_in", [NCH, 160, LC])
    ar2_out = dram("ar2_out", [NCH, 160, LC], addr_space="Shared")
    ag_in = dram("ag_in", [1536, NB])
    ag_out = dram("ag_out", [8 * 1536, NB], addr_space="Shared")
    a2a_in = [dram(f"a2a_in{g}", [8, DSL, LC]) for g in range(NQ)]
    a2a_out = [dram(f"a2a_out{g}", [8, DSL, LC]) for g in range(NQ)]
    x1_d = dram("x1_d", [D, TOK3])

    def sb(name, p, f, dtype=F32):
        return e(nc.sbuf_tensor(name, [p, f], dtype))

    eqm = sb("eqm", 128, 128)
    onesr = sb("onesr", 1, 128)
    onesc = sb("onesc", 128, 1)
    onesc_bf = sb("onesc_bf", 128, 1, BF16)
    selmat = sb("selmat", 32, 32 * 128, BF16)
    sel8 = sb("sel8", 8, 8 * 128)
    convd = sb("convd", 128, 8 * 128)
    xprojT_t = sb("xprojT_t", 128, 2 * 160)
    dtprojT_t = sb("dtprojT_t", 128, 2 * 128)
    acoef = sb("acoef", 128, 2 * NST)
    convw_t = sb("convw_t", 128, 8)
    convb_t = sb("convb_t", 128, 2)
    dskip_t = sb("dskip_t", 128, 2)
    cbds_t = sb("cbds_t", 128, 2)
    dtb_t = sb("dtb_t", 128, 2)
    can0 = sb("can0", 128, 8)
    ca1pp = sb("ca1pp", 128, 8)
    nca1pp = sb("nca1pp", 128, 8)
    can1 = sb("can1", 128, 8)
    gapp = sb("gapp", 128, 16)
    gbpp = sb("gbpp", 128, 16)
    sbpp = sb("sbpp", 128, 16)
    cb1pp = sb("cb1pp", 128, 16)
    ncb1pp = sb("ncb1pp", 128, 16)
    gall = sb("gall", 128, 4 * 64)
    msc = sb("msc", 128, 16)
    msc2 = sb("msc2", 128, 16)
    bsel_t = sb("bsel_t", 128, NB)
    wg1_t = sb("wg1_t", 128, 16 * 8, BF16)
    wg2_t = sb("wg2_t", 8, 8, BF16)
    bup_t = sb("bup_t_s", 128, E * 8)
    iota8_t = sb("iota8_t", 128, 8)
    c8_t = sb("c8_t", 128, 8)
    negbig_t = sb("negbig_t", 128, 8)
    iotap_t = sb("iotap_t", 128, 1)
    iotap32_t = sb("iotap32_t", 32, 1)
    ct_t = sb("ct_t", 128, 16 * NB)
    sct_t = sb("sct_t", 128, 16 * NB)
    modsl = sb("modsl", 128, 12 * NB)

    xs = [sb(f"xs{i}", 128, 2 * LCH) for i in range(2)]
    x2t = sb("x2t", 128, 2 * LC, BF16)
    statb = sb("statb", 1, 2 * LC)
    rows = sb("rows", 1, 3 * LC)
    ut = sb("ut", 128, 2 * LC)
    xpa_sb = sb("xpa_sb", 128, LC)
    xpb_sb = sb("xpb_sb", 32, LC)
    dtt = sb("dtt", 128, LC)
    bct = sb("bct", 32, LC, BF16)
    azt = sb("azt", 128, 2 * LC)
    rlt = sb("rlt", 128, 2 * LC)
    dAt = [sb(f"dAt{i}", 128, LC) for i in range(2)]
    htraj = [sb(f"htraj{i}", 128, LC) for i in range(2)]
    yct = [sb(f"yct{i}", 128, LC) for i in range(2)]
    yt = [sb("yt0", 128, 2 * LC)]
    state = sb("state", 128, 128)
    halo = sb("halo", 128, 8 * 3)

    if not skip_p3:
        h2q = sb("h2q", 128, 16 * LC, BF16)
        act_t = sb("act_t", 128, E * 8 * LC, BF16)
        yq = sb("yq", 128, 2 * LC)
        x1q = sb("x1q", 128, 2 * LC)
        x1h = sb("x1h", 128, 2 * LC)
        rows3 = sb("rows3", 1, 3 * LC)
        stat3 = sb("stat3", 1, 2 * LC)
        th1 = sb("th1", 8, LC, BF16)
        lgt = sb("lgt", 8, LC)
        tl = sb("tl", 128, 8)
        tsc = sb("tsc", 128, 12 * 8)
        msk8 = sb("msk8", 128, 3 * 8, mybir.dt.uint8)
        scoreq = [sb(f"scoreq{i}", 8, LC) for i in range(2)]
        wup_ring = [sb(f"wupr{i}", 128, 16 * 128, BF16) for i in range(2)]
        wdn_ring = [sb(f"wdnr{i}", 128, 16 * 128, BF16) for i in range(2)]
        bdg = [sb(f"bdg{i}", 8, 2 * 128) for i in range(2)]
        evt = [sb("evt0", 128, LC)]
        x1r = sb("x1r", 128, LC)
        x1e = [sb("x1e0", 128, LC)]

    p1ps = [e(nc.psum_tensor(f"p1ps{i}", [128, 512], F32)) for i in range(2)]
    repps = [e(nc.psum_tensor(f"repps{i}", [128, 512], F32)) for i in range(2)]
    upps = [e(nc.psum_tensor(f"upps{i}", [128, 512], F32)) for i in range(2)]
    dnps = [e(nc.psum_tensor(f"dnps{i}", [128, 512], F32)) for i in range(2)]

    chn = {}

    def ch(name):
        if name not in chn:
            chn[name] = Ch(stk, nc, name)
        return chn[name]

    P = Prog()
    W = P.wait
    O = P.on
    last_coll = [None]

    def coll_chain():
        if last_coll[0] is not None:
            ch_, v_ = last_coll[0]
            W("gpsimd", ch_, v_)

    def coll_mark(ch_):
        last_coll[0] = (ch_, ch_.n)

    def dma(eng, out, in_, c):
        c.bump(16)
        O(eng, lambda en, o=out, i=in_, h=c.h: en.dma_start(out=o, in_=i).then_inc(h, 16))

    cS_ld = ch("cS_ld")
    dma("sync", iota8_t[:, :], iota8_h.ap(), cS_ld)
    dma("sync", iotap_t[:, :], iotap_h.ap(), cS_ld)
    dma("sync", iotap32_t[:, :], iotap32_h.ap(), cS_ld)
    dma("sync", bsel_t[:, :], bsel_h.ap(), cS_ld)
    dma("sync", xprojT_t[:, :].rearrange("p (dc f) -> p dc f", dc=2)
        .bitcast(F32R),
        xprojT_h.ap().rearrange("(dc p) f -> p dc f", p=128), cS_ld)
    dma("sync", dtprojT_t[:, :].rearrange("p (dc f) -> p dc f", dc=2)
        .bitcast(F32R),
        dtprojT_h.ap().rearrange("p (dc f) -> p dc f", dc=2), cS_ld)
    dma("sync", acoef[:, :], alog_h.ap(), cS_ld)
    dma("sync", convw_t[:, :], convw_h.ap(), cS_ld)
    dma("sync", convb_t[:, :], convb_h.ap(), cS_ld)
    dma("sync", dskip_t[:, :], dskip_h.ap(), cS_ld)
    dma("sync", dtb_t[:, :], dtb_h.ap(), cS_ld)
    dma("sync", ct_t[:, :].rearrange("p (kc b) -> p kc b", kc=16),
        cT_h.ap().rearrange("(kc p) b -> p kc b", p=128), cS_ld)
    dma("sync", bup_t[:, :], bup_h.ap(), cS_ld)

    cS_cast = ch("cS_cast")
    for ei in range(E):
        dma("gpsimd", wup_bf.ap()[ei], wup_h.ap()[ei], cS_cast)
        dma("gpsimd", wdown_bf.ap()[:, ei], wdown_h.ap()[:, ei], cS_cast)
    dma("gpsimd", wg1_bf.ap(), wg1_h.ap(), cS_cast)
    dma("gpsimd", wg2_bf.ap(), wg2_h.ap(), cS_cast)
    cS_gl = ch("cS_gl")
    W("sync", cS_cast)
    dma("sync", wg1_t[:, :].rearrange("p (kc f) -> p kc f", kc=16),
        wg1_bf.ap().rearrange("kc p f -> p kc f"), cS_gl)
    dma("sync", wg2_t[:, :], wg2_bf.ap(), cS_gl)

    cS_const = ch("cS_const")

    def _vconst(v):
        v.wait_ge(cS_ld.h, cS_ld.n)
        v.memset(onesr[:, :], 1.0)
        v.memset(onesc[:, :], 1.0)
        v.memset(onesc_bf[:, :], 1.0)
        v.tensor_copy(out=onesr[:, :].bitcast(F32R), in_=onesr[:, :])
        v.tensor_copy(out=onesc[:, :].bitcast(F32R), in_=onesc[:, :])
        v.memset(can0[:, :], 99.0)
        v.memset(can1[:, :], 99.0)
        v.memset(c8_t[:, :], 8.0)
        v.memset(negbig_t[:, :], -1e30)
        v.memset(state[:, :], 0.0)
        for n in range(128):
            v.memset(eqm[:, n:n + 1], float(n))
        v.tensor_single_scalar(out=eqm[:, :], in_=eqm[:, :],
                               scalar=iotap_t[:, :], op=ALU.is_equal)
        for n in range(32):
            v.memset(selmat[:, n * 128:(n + 1) * 128], float(n))
        v.tensor_single_scalar(out=selmat[:, :], in_=selmat[:, :],
                               scalar=iotap32_t[:, :], op=ALU.is_equal)
        for n in range(8):
            v.memset(sel8[:, n * 128:(n + 1) * 128], float(n))
        v.tensor_single_scalar(out=sel8[:, :].bitcast(F32R), in_=sel8[:, :],
                               scalar=iotap_t[0:8, :], op=ALU.is_equal)
        for dc in range(2):
            for k in range(4):
                ix = dc * 4 + k
                v.tensor_single_scalar(
                    out=convd[:, ix * 128:(ix + 1) * 128].bitcast(F32R),
                    in_=eqm[:, :], scalar=convw_t[:, ix:ix + 1], op=ALU.mult)
        v.tensor_mul(out=cbds_t[:, :], in0=convb_t[:, :],
                     in1=dskip_t[:, :]).then_inc(cS_const.h, 1)

    O("vector", _vconst)
    cS_const.bump()

    cS_sc = ch("cS_sc")

    def _ssetup(s):
        s.wait_ge(cS_ld.h, cS_ld.n)
        s.activation(out=acoef[:, :], in_=acoef[:, :], func=AF.Exp)
        s.activation(out=sct_t[:, :].bitcast(F32R), in_=ct_t[:, :],
                     func=AF.Silu).then_inc(cS_sc.h, 1)

    O("scalar", _ssetup)
    cS_sc.bump()

    cS_acf = ch("cS_acf")
    W("vector", cS_sc)
    O("vector", lambda v: v.tensor_single_scalar(
        out=acoef[:, :], in_=acoef[:, :], scalar=-1.0,
        op=ALU.mult).then_inc(cS_acf.h, 1))
    cS_acf.bump()

    cS_wmod = ch("cS_wmod")
    cS_modmm = ch("cS_modmm")
    cS_modev = ch("cS_modev")
    kwg = 0
    for mt in range(12):
        for grp in range(4):
            rb = kwg % 2
            W("sync", cS_modmm, kwg - 1)
            dma("sync", yct[rb][:, :].rearrange("p (kc f) -> p kc f", kc=4)
                .bitcast(F32R),
                wmod_h.ap().rearrange("(kc p) (m f) -> m p kc f", p=128, f=128)[
                    mt][:, grp * 4:(grp + 1) * 4], cS_wmod)
            W("tensor", cS_wmod)
            W("tensor", cS_sc)
            if grp == 0:
                W("tensor", cS_modev, mt - 1)

            def _modmm(t, mt=mt, grp=grp, rb=rb):
                for kc in range(4):
                    mm = t.matmul(p1ps[mt % 2][0:128, 0:NB],
                                  yct[rb][:, kc * 128:(kc + 1) * 128].bitcast(F32R),
                                  sct_t[:, (grp * 4 + kc) * NB:(grp * 4 + kc + 1) * NB]
                                  .bitcast(F32R),
                                  start=(grp == 0 and kc == 0),
                                  stop=(grp == 3 and kc == 3))
                mm.then_inc(cS_modmm.h, 1)

            O("tensor", _modmm)
            cS_modmm.bump()
            kwg += 1
        W("scalar", cS_modmm)
        O("scalar", lambda s, mt=mt: s.copy(
            out=modsl[:, mt * NB:(mt + 1) * NB],
            in_=p1ps[mt % 2][0:128, 0:NB]).then_inc(cS_modev.h, 1))
        cS_modev.bump()
    v_wmod_done = cS_modmm.n

    c_tap = ch("c_tap")
    if "mod" in tap_h:
        W("scalar", cS_modev)
        dma("scalar", tap_h["mod"].ap(), modsl[:, :], c_tap)

    cS_agw = ch("cS_agw")
    W("scalar", cS_modev)
    dma("scalar", ag_in.ap().rearrange("(m p) b -> p m b", p=128),
        modsl[:, :].rearrange("p (m b) -> p m b", m=12), cS_agw)
    cS_ag = ch("cS_ag")

    coll_chain()

    def _ag(g):
        g.wait_ge(cS_agw.h, cS_agw.n)
        g.collective_compute(
            "AllGather", ALU.bypass, replica_groups=[list(range(8))],
            ins=[ag_in.ap().opt()], outs=[ag_out.ap().opt()],
        ).then_inc(cS_ag.h, 1)

    O("gpsimd", _ag)
    cS_ag.bump()
    coll_mark(cS_ag)

    cS_agr = ch("cS_agr")
    W("sync", cS_ag)
    agv = ag_out.ap()
    for pj in range(4):
        j_ = pj + 2
        for dcg in range(16):
            row0 = (dcg // 2) * 1536 + j_ * DSL + (dcg % 2) * 128
            dma("sync", gall[:, (pj * 16 + dcg) * 4:(pj * 16 + dcg + 1) * 4],
                agv[row0:row0 + 128, :], cS_agr)

    cS_modp = ch("cS_modp")

    def _modp(v):
        v.wait_ge(cS_modev.h, cS_modev.n)
        v.wait_ge(cS_agr.h, cS_agr.n)
        ca_view = modsl[:, 2 * NB:4 * NB].rearrange("p (dc b) -> p b dc", dc=2)
        v.tensor_single_scalar(
            out=ca1pp[:, :].rearrange("p (b dc) -> p b dc", dc=2),
            in_=ca_view, scalar=1.0, op=ALU.add)
        v.tensor_scalar(
            out=nca1pp[:, :].rearrange("p (b dc) -> p b dc", dc=2),
            in0=ca_view, scalar1=-1.0,
            scalar2=-1.0, op0=ALU.mult, op1=ALU.add)
        dsts = (gapp, sbpp, cb1pp, gbpp)
        for pj, dst in enumerate(dsts):
            base = pj * 64
            v.tensor_single_scalar(out=dst[:, :], in_=gall[:, base:base + 64:4],
                                   scalar=bsel_t[:, 0:1], op=ALU.mult)
        for b in range(1, NB):
            for pj, dst in enumerate(dsts):
                base = pj * 64
                v.scalar_tensor_tensor(out=dst[:, :],
                                       in0=gall[:, base + b:base + 64:4],
                                       scalar=bsel_t[:, b:b + 1], in1=dst[:, :],
                                       op0=ALU.mult, op1=ALU.add)
        v.tensor_single_scalar(out=msc[:, :], in_=gapp[:, :], scalar=0.0,
                               op=ALU.is_ge)
        v.tensor_single_scalar(out=msc2[:, :], in_=gbpp[:, :], scalar=0.0,
                               op=ALU.is_ge)
        v.tensor_scalar(out=ncb1pp[:, :], in0=cb1pp[:, :], scalar1=-1.0,
                        scalar2=-1.0, op0=ALU.mult, op1=ALU.add)
        v.tensor_scalar(out=msc[:, :], in0=msc[:, :], scalar1=2.0, scalar2=-1.0,
                        op0=ALU.mult, op1=ALU.add)
        v.tensor_scalar(out=msc2[:, :], in0=msc2[:, :], scalar1=2.0,
                        scalar2=-1.0, op0=ALU.mult, op1=ALU.add)
        v.tensor_single_scalar(out=cb1pp[:, :], in_=cb1pp[:, :], scalar=1.0,
                               op=ALU.add)
        v.tensor_mul(out=gapp[:, :], in0=gapp[:, :], in1=msc[:, :])
        v.tensor_mul(out=gbpp[:, :], in0=gbpp[:, :], in1=msc2[:, :])
        v.tensor_single_scalar(out=gapp[:, :], in_=gapp[:, :], scalar=0.25,
                               op=ALU.max)
        v.tensor_single_scalar(out=gbpp[:, :], in_=gbpp[:, :], scalar=0.25,
                               op=ALU.max)
        v.tensor_mul(out=gapp[:, :], in0=gapp[:, :], in1=msc[:, :])
        v.tensor_mul(out=gbpp[:, :], in0=gbpp[:, :],
                     in1=msc2[:, :]).then_inc(cS_modp.h, 1)

    O("vector", _modp)
    cS_modp.bump()

    def _ncaw_dump(cap):
        dma("sync", tap_h["ncaw"].ap()[cap, :, 0:8], ca1pp[:, :], c_tap)
        dma("sync", tap_h["ncaw"].ap()[cap, :, 8:16], nca1pp[:, :], c_tap)
        dma("sync", tap_h["ncaw"].ap()[cap, :, 16:24], can0[:, :], c_tap)
        dma("sync", tap_h["ncaw"].ap()[cap, :, 24:32], can1[:, :], c_tap)

    if "ncaw" in tap_h:
        W("sync", cS_modp)
        _ncaw_dump(0)

    def _win_dump(row):
        offs = [0]
        for tile, n in ((eqm, 128), (onesr, 128), (sel8, 1024), (convd, 1024),
                        (xprojT_t, 320), (dtprojT_t, 256), (acoef, 32)):
            o = offs[-1]
            dma("sync", tap_h["winf"].ap()[row:row + 1, o:o + n],
                tile[0:1, 0:n], c_tap)
            offs.append(o + n)
        dma("sync", tap_h["winb"].ap()[row:row + 1, :], selmat[0:1, :], c_tap)

    if "winf" in tap_h:
        W("sync", cS_const)
        W("sync", cS_ld)
        _win_dump(0)

    c1 = {k: ch("c1_" + k) for k in (
        "x", "x2", "stmm", "stev", "ar1w", "ar1", "ar1r", "rows_s", "rows",
        "bkA", "bkB", "rrep", "h1a", "h1b", "hsv", "u", "uev", "xp", "xpe",
        "ar2w", "ar2", "ar2r", "bcc", "dp", "spa", "spg", "dbuN", "brep",
        "brf", "crep", "crf", "dA", "dAf", "yc", "ycf", "yg", "ywr")}
    c1_a2a = [ch(f"c1_a2a{g}") for g in range(NQ)]

    k_dA = 0
    k_yc = 0

    for j in range(NJ):
        for b in range(NB):
            ci = j * NB + b
            rc = ci % 2

            if tap_h:
                for _e in ("sync", "scalar", "vector", "gpsimd", "tensor"):
                    W(_e, c_tap)

            W("sync", c1["u"], 2 * (ci - 1))
            W("sync", c1["hsv"], ci - 1)
            dma("sync",
                xs[rc][:, :].rearrange("p (dc l3) -> p dc l3", dc=2)[
                    :, :, 3:LCH].bitcast(F32R),
                xp1.ap()[b].rearrange("(dc p) l -> p dc l", p=128)[
                    :, :, j * LC:(j + 1) * LC], c1["x"])
            vx = c1["x"].n

            W("vector", c1["x"], vx)
            W("vector", c1["stmm"], ci)
            O("vector", lambda v, rc=rc: v.tensor_mul(
                out=x2t[:, :].rearrange("p (dc l) -> p dc l", dc=2),
                in0=xs[rc][:, :].rearrange("p (dc l3) -> p dc l3", dc=2)[:, :, 3:LCH],
                in1=xs[rc][:, :].rearrange("p (dc l3) -> p dc l3", dc=2)[:, :, 3:LCH])
              .then_inc(c1["x2"].h, 1))
            c1["x2"].bump()

            W("tensor", c1["x"], vx)
            W("tensor", c1["x2"])
            W("tensor", cS_const)
            W("tensor", c1["bkA"], 4 * ci)
            W("tensor", c1["xpe"], ci)

            def _stmm(t, rc=rc):
                for dc in range(2):
                    t.matmul(p1ps[0][0:1, :], onesc[:, :].bitcast(F32R),
                             xs[rc][:, dc * LCH + 3:(dc + 1) * LCH].bitcast(F32R),
                             start=(dc == 0), stop=(dc == 1))
                for dc in range(2):
                    mm = t.matmul(p1ps[1][0:1, :], onesc_bf[:, :],
                                  x2t[:, dc * LC:(dc + 1) * LC],
                                  start=(dc == 0), stop=(dc == 1))
                mm.then_inc(c1["stmm"].h, 1)

            O("tensor", _stmm)
            c1["stmm"].bump()

            W("scalar", c1["stmm"])
            W("scalar", c1["ar1w"], 16 * ci)
            def _stev(s):
                s.copy(out=statb[:, 0:LC], in_=p1ps[0][0:1, :]) \
                    .then_inc(c1["bkA"].h, 1)
                s.copy(out=statb[:, LC:2 * LC], in_=p1ps[1][0:1, :]) \
                    .then_inc(c1["bkB"].h, 1)

            O("scalar", _stev)
            c1["bkA"].bump()
            c1["bkB"].bump()
            W("sync", c1["bkB"], 3 * ci + 1)
            dma("sync", ar1_in.ap()[ci], statb[:, :], c1["ar1w"])
            if "stp" in tap_h:
                dma("sync", tap_h["stp"].ap()[ci], statb[:, :], c_tap)
            W("gpsimd", c1["ar1w"])
            coll_chain()
            O("gpsimd", lambda g, ci=ci: g.collective_compute(
                "AllReduce", ALU.add, replica_groups=[list(range(8))],
                ins=[ar1_in.ap()[ci].opt()], outs=[ar1_out.ap()[ci].opt()],
            ).then_inc(c1["ar1"].h, 1))
            c1["ar1"].bump()
            coll_mark(c1["ar1"])
            W("sync", c1["ar1"])
            W("sync", c1["rows_s"], ci)
            W("sync", c1["ar1w"], 16 * (ci + 1))
            dma("sync", statb[:, :], ar1_out.ap()[ci], c1["ar1r"])

            if "stats" in tap_h:
                W("sync", c1["ar1r"])
                dma("sync", tap_h["stats"].ap()[ci], statb[:, :], c_tap)

            W("scalar", c1["ar1r"])
            W("scalar", c1["rrep"], 2 * ci)
            if "rows" in tap_h:
                W("scalar", c_tap)

            def _rows_s(s):
                s.mul(out=rows[:, 0:LC].bitcast(F32R), in_=statb[:, 0:LC],
                      mul=INVD)
                s.mul(out=rows[:, 2 * LC:3 * LC].bitcast(F32R),
                      in_=statb[:, LC:2 * LC],
                      mul=INVD).then_inc(c1["rows_s"].h, 1)

            O("scalar", _rows_s)
            c1["rows_s"].bump()
            if "rows" in tap_h:
                pass
            W("vector", c1["rows_s"])

            def _rows_v(v):
                v.tensor_mul(out=rows[:, LC:2 * LC].bitcast(F32R),
                             in0=rows[:, 0:LC], in1=rows[:, 0:LC])
                v.tensor_sub(out=rows[:, 2 * LC:3 * LC].bitcast(F32R),
                             in0=rows[:, 2 * LC:3 * LC],
                             in1=rows[:, LC:2 * LC])
                v.tensor_single_scalar(out=rows[:, 2 * LC:3 * LC].bitcast(F32R),
                                       in_=rows[:, 2 * LC:3 * LC],
                                       scalar=EPS, op=ALU.add).then_inc(c1["rows"].h, 1)

            O("vector", _rows_v)
            c1["rows"].bump()
            W("scalar", c1["rows"])

            def _rows_s2(s):
                s.activation(out=rows[:, LC:2 * LC].bitcast(F32R),
                             in_=rows[:, 2 * LC:3 * LC], func=AF.Ln)
                s.activation(out=rows[:, LC:2 * LC].bitcast(F32R),
                             in_=rows[:, LC:2 * LC],
                             func=AF.Exp, scale=-0.5).then_inc(c1["rows"].h, 1)

            O("scalar", _rows_s2)
            c1["rows"].bump()
            W("vector", c1["rows"])
            O("vector", lambda v: v.tensor_mul(
                out=rows[:, 2 * LC:3 * LC].bitcast(F32R), in0=rows[:, 0:LC],
                in1=rows[:, LC:2 * LC]).then_inc(c1["rows"].h, 1))
            c1["rows"].bump()
            vrows = c1["rows"].n

            if "rows" in tap_h:
                W("sync", c1["rows"], vrows)
                dma("sync", tap_h["rows"].ap()[ci], rows[:, :], c_tap)

            W("tensor", c1["rows"], vrows)
            W("tensor", c1["bkB"], 3 * ci + 1)
            O("tensor", lambda t: t.matmul(
                p1ps[1][:, :], onesr[:, :].bitcast(F32R),
                rows[:, LC:2 * LC].bitcast(F32R), start=True, stop=True)
                .then_inc(c1["rrep"].h, 1))
            c1["rrep"].bump()
            v_rrep = c1["rrep"].n

            W("vector", cS_modp)
            W("vector", c1["rrep"], v_rrep)
            W("vector", c1["stmm"])

            def _h1a(v, rc=rc, b=b):
                for dc in range(2):
                    a = v.scalar_tensor_tensor(
                        out=xs[rc][:, dc * LCH + 3:(dc + 1) * LCH].bitcast(F32R),
                        in0=xs[rc][:, dc * LCH + 3:(dc + 1) * LCH],
                        scalar=ca1pp[:, (b * 2 + dc):(b * 2 + dc) + 1],
                        in1=p1ps[1][:, :], op0=ALU.mult, op1=ALU.mult)
                a.then_inc(c1["bkB"].h, 1)

            O("vector", _h1a)
            c1["bkB"].bump()

            if "h1m" in tap_h:
                W("scalar", c1["bkB"], 3 * ci + 2)
                dma("scalar",
                    tap_h["h1m"].ap()[ci].rearrange("(dc p) l -> p dc l", p=128),
                    xs[rc][:, :].rearrange("p (dc l3) -> p dc l3", dc=2)[:, :, 3:LCH],
                    c_tap)
                W("vector", c_tap)

            W("tensor", c1["bkB"], 3 * ci + 2)
            O("tensor", lambda t: t.matmul(
                p1ps[1][:, :], onesr[:, :].bitcast(F32R),
                rows[:, 2 * LC:3 * LC].bitcast(F32R), start=True, stop=True)
                .then_inc(c1["rrep"].h, 1))
            c1["rrep"].bump()
            v_mrep = c1["rrep"].n

            if "cst" in tap_h:
                W("sync", c1["rrep"], v_rrep)
                dma("sync", tap_h["cst"].ap()[ci:ci + 1, :], onesr[0:1, :], c_tap)

            if "ncaw" in tap_h and ci == 16:
                _ncaw_dump(1)

            W("vector", c1["rrep"], v_mrep)

            def _h1b(v, rc=rc, b=b, j=j):
                for dc in range(2):
                    a = v.scalar_tensor_tensor(
                        out=xs[rc][:, dc * LCH + 3:(dc + 1) * LCH].bitcast(F32R),
                        in0=p1ps[1][:, :],
                        scalar=nca1pp[:, (b * 2 + dc):(b * 2 + dc) + 1],
                        in1=xs[rc][:, dc * LCH + 3:(dc + 1) * LCH],
                        op0=ALU.mult, op1=ALU.add)
                a.then_inc(c1["bkB"].h, 1)
                for dc in range(2):
                    if j == 0:
                        a = v.memset(xs[rc][:, dc * LCH:dc * LCH + 3], 0.0)
                    else:
                        a = v.tensor_copy(
                            out=xs[rc][:, dc * LCH:dc * LCH + 3].bitcast(F32R),
                            in_=halo[:, (b * 2 + dc) * 3:(b * 2 + dc + 1) * 3])
                a.then_inc(c1["h1b"].h, 1)

            O("vector", _h1b)
            c1["bkB"].bump()
            c1["h1b"].bump()

            W("scalar", c1["bkB"], 3 * ci + 3)
            W("scalar", cS_modp)

            def _h1c(s, rc=rc, b=b):
                for dc in range(2):
                    a = s.activation(
                        out=xs[rc][:, dc * LCH + 3:(dc + 1) * LCH].bitcast(F32R),
                        in_=xs[rc][:, dc * LCH + 3:(dc + 1) * LCH],
                        func=AF.Identity,
                        bias=modsl[:, dc * NB + b:dc * NB + b + 1])
                a.then_inc(c1["h1a"].h, 1)

            O("scalar", _h1c)
            c1["h1a"].bump()

            W("vector", c1["h1a"])

            def _hsv(v, rc=rc, b=b):
                for dc in range(2):
                    a = v.tensor_copy(
                        out=halo[:, (b * 2 + dc) * 3:(b * 2 + dc + 1) * 3],
                        in_=xs[rc][:, dc * LCH + LC:(dc + 1) * LCH])
                a.then_inc(c1["hsv"].h, 1)

            O("vector", _hsv)
            c1["hsv"].bump()

            if "h1" in tap_h:
                W("scalar", c1["h1a"])
                dma("scalar", tap_h["h1"].ap()[ci].rearrange("(dc p) l -> p dc l", p=128),
                    xs[rc][:, :].rearrange("p (dc l3) -> p dc l3", dc=2)[:, :, 3:LCH],
                    c_tap)

            W("tensor", c1["h1a"])
            W("tensor", c1["h1b"])
            W("tensor", c1["bkA"], 4 * ci + 1)
            W("scalar", c1["brf"], 32 * ci)

            def _conv0(t, rc=rc):
                for k in range(4):
                    mm = t.matmul(
                        p1ps[0][:, :],
                        convd[:, k * 128:(k + 1) * 128].bitcast(F32R),
                        xs[rc][:, k:k + LC].bitcast(F32R),
                        start=(k == 0), stop=(k == 3))
                mm.then_inc(c1["u"].h, 1)

            O("tensor", _conv0)
            c1["u"].bump()
            W("scalar", c1["u"])
            W("scalar", c1["ywr"], 16 * ci)

            def _uev0(s, rc=rc):
                s.activation(out=ut[:, 0:LC].bitcast(F32R), in_=p1ps[0][:, :],
                             func=AF.Identity, bias=convb_t[:, 0:1])
                a = s.activation(out=yt[0][:, 0:LC], in_=p1ps[0][:, :],
                                 func=AF.Identity, scale=dskip_t[:, 0:1],
                                 bias=cbds_t[:, 0:1])
                a.then_inc(c1["uev"].h, 1)

            O("scalar", _uev0)
            c1["uev"].bump()
            W("tensor", c1["uev"])
            W("tensor", c1["bkA"], 4 * ci + 1)

            def _conv1(t, rc=rc):
                for k in range(4):
                    mm = t.matmul(
                        p1ps[0][:, :],
                        convd[:, (4 + k) * 128:(5 + k) * 128].bitcast(F32R),
                        xs[rc][:, LCH + k:LCH + k + LC].bitcast(F32R),
                        start=(k == 0), stop=(k == 3))
                mm.then_inc(c1["u"].h, 1)

            O("tensor", _conv1)
            c1["u"].bump()
            W("scalar", c1["u"])

            def _uev1(s, rc=rc):
                s.activation(out=ut[:, LC:2 * LC].bitcast(F32R), in_=p1ps[0][:, :],
                             func=AF.Identity, bias=convb_t[:, 1:2])
                a = s.activation(out=yt[0][:, LC:2 * LC], in_=p1ps[0][:, :],
                                 func=AF.Identity, scale=dskip_t[:, 1:2],
                                 bias=cbds_t[:, 1:2])
                a.then_inc(c1["uev"].h, 1)

            O("scalar", _uev1)
            c1["uev"].bump()
            v_uev = c1["uev"].n

            if "u" in tap_h:
                W("scalar", c1["uev"], v_uev)
                dma("scalar", tap_h["u"].ap()[ci].rearrange("(dc p) l -> p dc l", p=128),
                    ut[:, :].rearrange("p (dc l) -> p dc l", dc=2), c_tap)

            W("tensor", c1["uev"], v_uev)
            W("tensor", c1["bkA"], 4 * ci + 1)
            W("tensor", c1["bkB"], 3 * ci + 3)

            def _xproj(t):
                for dc in range(2):
                    t.matmul(p1ps[0][0:128, :],
                             xprojT_t[:, dc * 160:dc * 160 + 128].bitcast(F32R),
                             ut[:, dc * LC:(dc + 1) * LC].bitcast(F32R),
                             start=(dc == 0), stop=(dc == 1))
                for dc in range(2):
                    mm = t.matmul(p1ps[1][0:32, :],
                                  xprojT_t[:, dc * 160 + 128:(dc + 1) * 160]
                                  .bitcast(F32R),
                                  ut[:, dc * LC:(dc + 1) * LC].bitcast(F32R),
                                  start=(dc == 0), stop=(dc == 1))
                mm.then_inc(c1["xp"].h, 1)

            O("tensor", _xproj)
            c1["xp"].bump()

            W("scalar", c1["xp"])
            W("scalar", c1["ar2w"], 32 * ci)

            def _xpev(s):
                s.copy(out=xpa_sb[:, :], in_=p1ps[0][0:128, :]) \
                    .then_inc(c1["bkA"].h, 1)
                s.copy(out=xpb_sb[:, :], in_=p1ps[1][0:32, :]) \
                    .then_inc(c1["xpe"].h, 1)

            O("scalar", _xpev)
            c1["xpe"].bump()
            c1["bkA"].bump()

            W("sync", c1["xpe"])
            dma("sync", ar2_in.ap()[ci][0:128, :], xpa_sb[:, :], c1["ar2w"])
            dma("sync", ar2_in.ap()[ci][128:160, :], xpb_sb[:, :], c1["ar2w"])
            W("gpsimd", c1["ar2w"])
            coll_chain()
            O("gpsimd", lambda g, ci=ci: g.collective_compute(
                "AllReduce", ALU.add, replica_groups=[list(range(8))],
                ins=[ar2_in.ap()[ci].opt()], outs=[ar2_out.ap()[ci].opt()],
            ).then_inc(c1["ar2"].h, 1))
            c1["ar2"].bump()
            coll_mark(c1["ar2"])

            W("gpsimd", c1["ar2"])
            W("gpsimd", c1["dp"], 2 * ci)
            dma("gpsimd", dtt[:, :].bitcast(F32R), ar2_out.ap()[ci][0:128, :],
                c1["ar2r"])
            var2r = c1["ar2r"].n
            W("gpsimd", c1["ar2"])
            W("gpsimd", c1["crep"], 16 * ci)
            dma("gpsimd", xdblr_bf.ap()[ci], ar2_out.ap()[ci][128:160, :], c1["bcc"])
            W("sync", c1["bcc"])
            dma("sync", bct[:, :], xdblr_bf.ap()[ci], c1["ar2r"])
            var2rb = c1["ar2r"].n

            if "xdbl" in tap_h:
                W("sync", c1["ar2r"], var2r)
                dma("sync", tap_h["xdbl"].ap()[ci][0:128, :], dtt[:, :], c_tap)

            for dc in range(2):
                W("tensor", c1["ar2r"], var2r)
                W("tensor", c1["bkA"], 4 * ci + 2 + dc)
                O("tensor", lambda t, dc=dc: t.matmul(
                    p1ps[0][:, :],
                    dtprojT_t[:, dc * 128:(dc + 1) * 128].bitcast(F32R),
                    dtt[:, :].bitcast(F32R), start=True,
                    stop=True).then_inc(c1["dp"].h, 1))
                c1["dp"].bump()
                W("scalar", c1["dp"])
                W("scalar", c1["dA"], 32 * ci)
                W("scalar", c1["spg"], ci)

                def _sp(s, dc=dc):
                    sl_ = slice(dc * LC, (dc + 1) * LC)
                    s.activation(out=azt[:, sl_], in_=p1ps[0][:, :], func=AF.Abs,
                                 bias=dtb_t[:, dc:dc + 1])
                    a = s.activation(out=rlt[:, sl_], in_=p1ps[0][:, :],
                                     func=AF.Relu, bias=dtb_t[:, dc:dc + 1])
                    a.then_inc(c1["bkA"].h, 1)
                    s.activation(out=azt[:, sl_], in_=azt[:, sl_], func=AF.Exp,
                                 scale=-1.0)
                    a2 = s.activation(out=azt[:, sl_], in_=azt[:, sl_],
                                      func=AF.Ln, bias=1.0)
                    a2.then_inc(c1["spa"].h, 1)

                O("scalar", _sp)
                c1["spa"].bump()
                c1["bkA"].bump()

            W("gpsimd", c1["spa"])
            W("gpsimd", c1["dA"], 32 * ci)
            W("gpsimd", c1["brf"], 32 * ci)
            W("gpsimd", c1["xp"])
            W("gpsimd", c1["uev"], v_uev)

            def _del(g):
                g.tensor_add(out=azt[:, :], in0=azt[:, :], in1=rlt[:, :])
                g.tensor_mul(out=ut[:, :].bitcast(F32R), in0=azt[:, :],
                             in1=ut[:, :]).then_inc(c1["spg"].h, 1)

            O("gpsimd", _del)
            c1["spg"].bump()

            if "delta" in tap_h:
                W("scalar", c1["spg"])
                dma("scalar", tap_h["delta"].ap()[ci].rearrange("(dc p) l -> p dc l", p=128),
                    azt[:, :].rearrange("p (dc l) -> p dc l", dc=2), c_tap)

            for n in range(NST):
                W("tensor", c1["ar2r"], var2rb)
                W("tensor", c1["brf"], 32 * ci + 2 * n)
                O("tensor", lambda t, n=n: t.matmul(
                    repps[0][:, :], selmat[:, n * 128:(n + 1) * 128],
                    bct[:, :], start=True, stop=True).then_inc(c1["brep"].h, 1))
                c1["brep"].bump()
                W("tensor", c1["yc"], 32 * ci + 2 * n)
                O("tensor", lambda t, n=n: t.matmul(
                    repps[1][:, :], selmat[:, (16 + n) * 128:(17 + n) * 128],
                    bct[:, :], start=True, stop=True).then_inc(c1["crep"].h, 1))
                c1["crep"].bump()

                for dc in range(2):
                    scol = (b * 2 + dc) * NST + n
                    kA = k_dA
                    k_dA += 1
                    W("scalar", c1["spg"])
                    W("scalar", c1["dAf"], kA - 1)
                    if kA == 0:
                        W("vector", cS_modmm, v_wmod_done)
                        W("scalar", cS_acf)
                    O("scalar", lambda s, dc=dc, n=n, kA=kA: s.activation(
                        out=dAt[kA % 2][:, :],
                        in_=azt[:, dc * LC:(dc + 1) * LC],
                        func=AF.Exp,
                        scale=acoef[:, dc * NST + n:dc * NST + n + 1])
                        .then_inc(c1["dA"].h, 1))
                    c1["dA"].bump()

                    kY = k_yc
                    k_yc += 1
                    W("vector", c1["spg"])
                    W("vector", c1["brep"])
                    W("vector", c1["dA"])
                    W("vector", c1["ycf"], kY - 1)
                    W("vector", c1["crep"])

                    def _scan(v, dc=dc, kA=kA, kY=kY, j=j, scol=scol):
                        hx = kY % 2
                        v.tensor_mul(out=yct[hx][:, :].bitcast(F32R),
                                     in0=ut[:, dc * LC:(dc + 1) * LC],
                                     in1=repps[0][:, :]) \
                            .then_inc(c1["brf"].h, 1)
                        ini = 0.0 if j == 0 else state[:, scol:scol + 1]
                        v.tensor_tensor_scan(out=htraj[hx][:, :],
                                             data0=dAt[kA % 2][:, :],
                                             data1=yct[hx][:, :], initial=ini,
                                             op0=ALU.mult, op1=ALU.add) \
                            .then_inc(c1["dAf"].h, 1)
                        v.tensor_mul(out=yct[hx][:, :].bitcast(F32R),
                                     in0=htraj[hx][:, :],
                                     in1=repps[1][:, :]) \
                            .then_inc(c1["yc"].h, 1)
                        v.tensor_copy(out=state[:, scol:scol + 1],
                                      in_=htraj[hx][:, LC - 1:LC])

                    O("vector", _scan)
                    c1["dAf"].bump()
                    c1["yc"].bump()
                    c1["brf"].bump()

                    W("gpsimd", c1["yc"])
                    W("gpsimd", c1["uev"], v_uev)

                    def _yadd(g, dc=dc, kY=kY):
                        g.tensor_add(out=yt[0][:, dc * LC:(dc + 1) * LC],
                                     in0=yt[0][:, dc * LC:(dc + 1) * LC],
                                     in1=yct[kY % 2][:, :]) \
                            .then_inc(c1["ycf"].h, 1)

                    O("gpsimd", _yadd)
                    c1["ycf"].bump()

            g_ = j % 4
            dest = 2 * b + (0 if j < 4 else 1)
            W("sync", c1["ycf"], 32 * (ci + 1))
            dma("sync", a2a_in[g_].ap()[dest].rearrange("(dc p) l -> p dc l", p=128),
                yt[0][:, :].rearrange("p (dc l) -> p dc l", dc=2), c1["ywr"])
            if "y" in tap_h:
                dma("sync", tap_h["y"].ap()[ci].rearrange("(dc p) l -> p dc l", p=128),
                    yt[0][:, :].rearrange("p (dc l) -> p dc l", dc=2), c_tap)

        if j >= 4:
            g_ = j - 4
            W("gpsimd", c1["ywr"])
            coll_chain()

            def _a2a(g, g_=g_):
                g.collective_compute(
                    "AllToAll", ALU.bypass, replica_groups=[list(range(8))],
                    ins=[a2a_in[g_].ap().opt()], outs=[a2a_out[g_].ap().opt()],
                ).then_inc(c1_a2a[g_].h, 1)

            O("gpsimd", _a2a)
            c1_a2a[g_].bump()
            coll_mark(c1_a2a[g_])

    c_out = ch("c_out")
    if not skip_p3:
        c3 = {k: ch("c3_" + k) for k in (
            "yx", "x1", "x1w", "stp", "st", "rows_s", "rows", "upf",
            "x1r", "h2", "lg", "lgev", "t2", "sig", "sc", "wup", "wupf",
            "wdn", "wdnf", "bdw", "up", "act", "acts", "dn", "dnev", "er",
            "res")}

        k_up = [0]
        up_rel = []
        up_ring_rel = []

        def upps_acq():
            k = k_up[0]
            k_up[0] += 1
            if k >= 2:
                ch_, v_ = up_rel[k - 2]
                assert (ch_, v_) != (None, None), f"usage {k - 2} never released"
                W("tensor", ch_, v_)
            up_rel.append((None, None))
            return k

        def upps_tile(k):
            return upps[k % 2]

        def upps_mark(k, ch_, v_=None):
            assert up_rel[k] == (None, None)
            up_rel[k] = (ch_, ch_.n if v_ is None else v_)

        xpflat = xp3.ap().rearrange("(gg p) l -> gg p l", p=128)
        x1flat = x1_d.ap().rearrange("(gg p) l -> gg p l", p=128)
        outflat = out_h.ap().rearrange("(gg p) l -> gg p l", p=128)
        pstate = {}

        def prep_block(q):
            if tap_h:
                for _e in ("sync", "scalar", "vector", "gpsimd", "tensor"):
                    W(_e, c_tap)
            qr = q % 2
            a2flat = a2a_out[q].ap().rearrange("s d l -> (s d) l").rearrange("(gg p) l -> gg p l", p=128)

            k_st = upps_acq()
            k_st2 = upps_acq()
            st_bank = upps_tile(k_st)
            st_bank2 = upps_tile(k_st2)
            for dcg in range(16):
                gs = dcg % 2
                idx = q * 16 + dcg
                W("sync", c1_a2a[q])
                W("sync", c3["x1"], idx - 1)
                dma("sync", yq[:, gs * LC:(gs + 1) * LC], a2flat[dcg], c3["yx"])
                W("sync", c3["x1w"], 16 * (idx - 1))
                W("sync", c3["stp"], idx - 1)
                dma("sync", x1q[:, gs * LC:(gs + 1) * LC].bitcast(F32R),
                    xpflat[dcg][:, q * LC:(q + 1) * LC], c3["yx"])
                vyx = c3["yx"].n
                W("vector", c3["yx"], vyx)
                W("vector", cS_modp)
                W("vector", c3["stp"], idx - 1)

                def _x1(v, dcg=dcg, gs=gs):
                    sl_ = slice(gs * LC, (gs + 1) * LC)
                    v.scalar_tensor_tensor(out=x1q[:, sl_].bitcast(F32R),
                                           in0=yq[:, sl_],
                                           scalar=gapp[:, dcg:dcg + 1],
                                           in1=x1q[:, sl_],
                                           op0=ALU.mult, op1=ALU.add)
                    v.tensor_mul(out=x1h[:, sl_].bitcast(F32R), in0=x1q[:, sl_],
                                 in1=x1q[:, sl_]).then_inc(c3["x1"].h, 1)

                O("vector", _x1)
                c3["x1"].bump()
                W("sync", c3["x1"])
                dma("sync", x1flat[dcg][:, q * LC:(q + 1) * LC],
                    x1q[:, gs * LC:(gs + 1) * LC], c3["x1w"])
                if "x1" in tap_h:
                    dma("sync", tap_h["x1"].ap().rearrange(
                        "(gg p) l -> gg p l", p=128)[dcg][:, q * LC:(q + 1) * LC],
                        x1q[:, gs * LC:(gs + 1) * LC], c_tap)

                W("tensor", c3["x1"])

                def _st3(t, dcg=dcg, gs=gs, bank=st_bank, bank2=st_bank2):
                    sl_ = slice(gs * LC, (gs + 1) * LC)
                    t.matmul(bank[0:1, :], onesc[:, :].bitcast(F32R),
                             x1q[:, sl_].bitcast(F32R), start=(dcg == 0),
                             stop=(dcg == 15))
                    mm = t.matmul(bank2[0:1, :], onesc[:, :].bitcast(F32R),
                                  x1h[:, sl_].bitcast(F32R), start=(dcg == 0),
                                  stop=(dcg == 15))
                    mm.then_inc(c3["stp"].h, 1)

                O("tensor", _st3)
                c3["stp"].bump()

            W("scalar", c3["stp"], 16 * (q + 1))

            def _r3a(s, bank=st_bank, bank2=st_bank2):
                s.copy(out=stat3[:, 0:LC], in_=bank[0:1, :])
                s.copy(out=stat3[:, LC:2 * LC], in_=bank2[0:1, :])
                s.mul(out=rows3[:, 0:LC].bitcast(F32R),
                      in_=stat3[:, 0:LC], mul=INVD)
                s.mul(out=rows3[:, 2 * LC:3 * LC].bitcast(F32R),
                      in_=stat3[:, LC:2 * LC], mul=INVD) \
                    .then_inc(c3["rows_s"].h, 1)

            O("scalar", _r3a)
            c3["rows_s"].bump()
            upps_mark(k_st, c3["rows_s"])
            upps_mark(k_st2, c3["rows_s"])
            W("vector", c3["rows_s"])

            def _r3c(v):
                v.tensor_mul(out=rows3[:, LC:2 * LC].bitcast(F32R),
                             in0=rows3[:, 0:LC], in1=rows3[:, 0:LC])
                v.tensor_sub(out=rows3[:, 2 * LC:3 * LC].bitcast(F32R),
                             in0=rows3[:, 2 * LC:3 * LC],
                             in1=rows3[:, LC:2 * LC])
                v.tensor_single_scalar(out=rows3[:, 2 * LC:3 * LC].bitcast(F32R),
                                       in_=rows3[:, 2 * LC:3 * LC], scalar=EPS,
                                       op=ALU.add).then_inc(c3["rows"].h, 1)

            O("vector", _r3c)
            c3["rows"].bump()
            W("scalar", c3["rows"])

            def _r3d(s):
                s.activation(out=rows3[:, LC:2 * LC].bitcast(F32R),
                             in_=rows3[:, 2 * LC:3 * LC], func=AF.Ln)
                s.activation(out=rows3[:, LC:2 * LC].bitcast(F32R),
                             in_=rows3[:, LC:2 * LC],
                             func=AF.Exp, scale=-0.5).then_inc(c3["rows"].h, 1)

            O("scalar", _r3d)
            c3["rows"].bump()
            W("vector", c3["rows"])
            O("vector", lambda v: v.tensor_mul(
                out=rows3[:, 2 * LC:3 * LC].bitcast(F32R), in0=rows3[:, 0:LC],
                in1=rows3[:, LC:2 * LC]).then_inc(c3["rows"].h, 1))
            c3["rows"].bump()
            v3rows = c3["rows"].n

            k_rr = upps_acq()
            rrep_bank = upps_tile(k_rr)
            W("tensor", c3["rows"], v3rows)
            O("tensor", lambda t, bank=rrep_bank: t.matmul(
                bank[:, :], onesr[:, :].bitcast(F32R),
                rows3[:, LC:2 * LC].bitcast(F32R), start=True, stop=True))
            k_mr = upps_acq()
            mrep_bank = upps_tile(k_mr)
            O("tensor", lambda t, bank=mrep_bank: t.matmul(
                bank[:, :], onesr[:, :].bitcast(F32R),
                rows3[:, 2 * LC:3 * LC].bitcast(F32R), start=True,
                stop=True).then_inc(c3["rows"].h, 1))
            c3["rows"].bump()
            v3rep = c3["rows"].n

            for dcg in range(16):
                idx = q * 16 + dcg
                W("sync", c3["x1w"], 16 * (idx + 1))
                W("sync", c3["h2"], idx)
                dma("sync", x1r[:, :],
                    x1flat[dcg][:, q * LC:(q + 1) * LC], c3["x1r"])
                W("vector", c3["x1r"])
                W("vector", c3["rows"], v3rep)
                W("vector", cS_modp)

                def _h2(v, dcg=dcg):
                    v.scalar_tensor_tensor(out=x1r[:, :], in0=x1r[:, :],
                                           scalar=cb1pp[:, dcg:dcg + 1],
                                           in1=rrep_bank[:, :],
                                           op0=ALU.mult, op1=ALU.mult)
                    v.scalar_tensor_tensor(out=x1r[:, :], in0=mrep_bank[:, :],
                                           scalar=ncb1pp[:, dcg:dcg + 1],
                                           in1=x1r[:, :], op0=ALU.mult,
                                           op1=ALU.add).then_inc(c3["x1r"].h, 1)

                O("vector", _h2)
                c3["x1r"].bump()
                W("scalar", c3["x1r"])
                if q >= 1:
                    W("scalar", c3["up"], 72 * q)
                    W("scalar", c3["lg"], 2 * q)

                def _h2b(s, dcg=dcg):
                    s.activation(out=h2q[:, dcg * LC:(dcg + 1) * LC],
                                 in_=x1r[:, :], func=AF.Identity,
                                 bias=sbpp[:, dcg:dcg + 1]).then_inc(c3["h2"].h, 1)

                O("scalar", _h2b)
                c3["h2"].bump()
            v_h2q = c3["h2"].n
            upps_mark(k_rr, c3["x1r"])
            upps_mark(k_mr, c3["x1r"])

            if "h2" in tap_h:
                for dcg in range(16):
                    W("scalar", c3["h2"], v_h2q)
                    dma("scalar", tap_h["h2"].ap().rearrange(
                        "(gg p) l -> gg p l", p=128)[dcg][:, q * LC:(q + 1) * LC],
                        h2q[:, dcg * LC:(dcg + 1) * LC], c_tap)

            k_lg = upps_acq()
            lg_bank = upps_tile(k_lg)
            W("tensor", c3["h2"], v_h2q)
            W("tensor", cS_gl)

            def _lg1(t, bank=lg_bank):
                for kc in range(16):
                    mm = t.matmul(bank[0:8, :], wg1_t[:, kc * 8:(kc + 1) * 8],
                                  h2q[:, kc * LC:(kc + 1) * LC],
                                  start=(kc == 0), stop=(kc == 15))
                mm.then_inc(c3["lg"].h, 1)

            O("tensor", _lg1)
            c3["lg"].bump()
            W("scalar", c3["lg"])

            def _lgev1(s, bank=lg_bank):
                s.activation(out=th1[:, :], in_=bank[0:8, :], func=AF.Tanh) \
                    .then_inc(c3["lgev"].h, 1)

            O("scalar", _lgev1)
            c3["lgev"].bump()
            upps_mark(k_lg, c3["lgev"])
            k_lg2 = upps_acq()
            lg2_bank = upps_tile(k_lg2)
            W("tensor", c3["lgev"])
            O("tensor", lambda t, bank=lg2_bank: t.matmul(
                bank[0:8, :], wg2_t[:, :], th1[:, :], start=True,
                stop=True).then_inc(c3["lg"].h, 1))
            c3["lg"].bump()
            W("scalar", c3["lg"])

            def _lgev2(s, bank=lg2_bank):
                s.copy(out=lgt[:, :], in_=bank[0:8, :]) \
                    .then_inc(c3["lgev"].h, 1)

            O("scalar", _lgev2)
            c3["lgev"].bump()
            upps_mark(k_lg2, c3["lgev"])
            v_lgt = c3["lgev"].n

            for tc in range(4):
                k_tp = upps_acq()
                tp_bank = upps_tile(k_tp)
                W("tensor", c3["lgev"], v_lgt)
                W("tensor", cS_const)
                O("tensor", lambda t, tc=tc, bank=tp_bank: t.transpose(
                    bank[0:128, 0:8], lgt[:, tc * 128:(tc + 1) * 128],
                    eqm[0:8, 0:8]).then_inc(c3["t2"].h, 1))
                c3["t2"].bump()
                W("vector", c3["t2"])

                def _top2(v, bank=tp_bank):
                    c = tsc

                    def pad():
                        v.engine_nop()
                        v.engine_nop()
                        v.engine_nop()
                    v.tensor_copy(out=tl[:, :], in_=bank[0:128, 0:8])
                    pad()
                    v.tensor_reduce(out=c[:, 16:17], in_=tl[:, :], axis=AX.X,
                                    op=ALU.max)
                    pad()
                    v.tensor_single_scalar(out=msk8[:, 0:8], in_=tl[:, :],
                                           scalar=c[:, 16:17], op=ALU.is_equal)
                    pad()
                    v.select(out=c[:, 32:40], mask=msk8[:, 0:8],
                             on_true=iota8_t[:, :], on_false=c8_t[:, :])
                    pad()
                    v.tensor_reduce(out=c[:, 17:18], in_=c[:, 32:40], axis=AX.X,
                                    op=ALU.min)
                    pad()
                    v.tensor_single_scalar(out=c[:, 40:48], in_=iota8_t[:, :],
                                           scalar=c[:, 17:18], op=ALU.is_equal)
                    v.tensor_single_scalar(out=msk8[:, 8:16], in_=iota8_t[:, :],
                                           scalar=c[:, 17:18], op=ALU.is_equal)
                    pad()
                    v.select(out=c[:, 48:56], mask=msk8[:, 8:16],
                             on_true=negbig_t[:, :], on_false=tl[:, :])
                    pad()
                    v.tensor_reduce(out=c[:, 18:19], in_=c[:, 48:56], axis=AX.X,
                                    op=ALU.max)
                    pad()
                    v.tensor_single_scalar(out=msk8[:, 16:24], in_=c[:, 48:56],
                                           scalar=c[:, 18:19], op=ALU.is_equal)
                    pad()
                    v.select(out=c[:, 64:72], mask=msk8[:, 16:24],
                             on_true=iota8_t[:, :], on_false=c8_t[:, :])
                    pad()
                    v.tensor_reduce(out=c[:, 19:20], in_=c[:, 64:72], axis=AX.X,
                                    op=ALU.min)
                    pad()
                    v.tensor_single_scalar(out=c[:, 72:80], in_=iota8_t[:, :],
                                           scalar=c[:, 19:20], op=ALU.is_equal)
                    v.tensor_sub(out=c[:, 20:21], in0=c[:, 16:17],
                                 in1=c[:, 18:19]).then_inc(c3["t2"].h, 1)

                O("vector", _top2)
                c3["t2"].bump()
                W("scalar", c3["t2"])

                def _sig(s):
                    s.activation(out=tsc[:, 22:23], in_=tsc[:, 20:21],
                                 func=AF.Sigmoid)
                    s.activation(out=tsc[:, 23:24], in_=tsc[:, 20:21],
                                 func=AF.Sigmoid, scale=-1.0).then_inc(c3["sig"].h, 1)

                O("scalar", _sig)
                c3["sig"].bump()
                W("vector", c3["sig"])

                def _sc2(v, bank=tp_bank):
                    v.tensor_single_scalar(out=tsc[:, 80:88], in_=tsc[:, 40:48],
                                           scalar=tsc[:, 22:23], op=ALU.mult)
                    v.engine_nop()
                    v.engine_nop()
                    v.engine_nop()
                    v.scalar_tensor_tensor(out=tsc[:, 80:88], in0=tsc[:, 72:80],
                                           scalar=tsc[:, 23:24],
                                           in1=tsc[:, 80:88],
                                           op0=ALU.mult, op1=ALU.add) \
                        .then_inc(c3["t2"].h, 1)

                O("vector", _sc2)
                c3["t2"].bump()
                upps_mark(k_tp, c3["t2"])
                k_tb = upps_acq()
                tb_bank = upps_tile(k_tb)
                W("tensor", c3["t2"])
                O("tensor", lambda t, bank=tb_bank: t.transpose(
                    bank[0:8, 0:128], tsc[:, 80:88],
                    eqm[:, :]).then_inc(c3["t2"].h, 1))
                c3["t2"].bump()
                W("scalar", c3["t2"])
                if q >= 2:
                    W("scalar", c3["wdnf"], 64 * (q - 1))

                def _scev(s, tc=tc, qr=qr, bank=tb_bank):
                    s.copy(out=scoreq[qr][:, tc * 128:(tc + 1) * 128]
                           .bitcast(F32R), in_=bank[0:8, 0:128]) \
                        .then_inc(c3["sc"].h, 1)

                O("scalar", _scev)
                c3["sc"].bump()
                upps_mark(k_tb, c3["sc"])
            v_sc = c3["sc"].n

            if "scores" in tap_h:
                W("scalar", c3["sc"], v_sc)
                dma("scalar", tap_h["scores"].ap()[:, q * LC:(q + 1) * LC],
                    scoreq[qr][:, :], c_tap)
            pstate[q] = dict(qr=qr, v_sc=v_sc, v_h2q=v_h2q)

        def up_block(q):
            qr = pstate[q]["qr"]
            v_sc = pstate[q]["v_sc"]
            v_h2q = pstate[q]["v_h2q"]
            for mk in (up_rel[-2], up_rel[-1]):
                W("tensor", mk[0], mk[1])
            for ei in range(E):
                sr_bank = upps[0]
                W("tensor", c3["sc"], v_sc)
                if ei >= 1:
                    W("tensor", c3["acts"], c3["acts"].n)
                O("tensor", lambda t, ei=ei, qr=qr, bank=sr_bank: t.matmul(
                    bank[:, :], sel8[:, ei * 128:(ei + 1) * 128].bitcast(F32R),
                    scoreq[qr][:, :].bitcast(F32R), start=True,
                    stop=True).then_inc(c3["up"].h, 1))
                c3["up"].bump()
                v_srep = c3["up"].n
                for mc in range(8):
                    gidx = q * 64 + ei * 8 + mc
                    if gidx >= 2:
                        W("sync", c3["up"], up_ring_rel[gidx - 2])
                    dma("sync", wup_ring[gidx % 2][:, :].rearrange(
                        "p (kc f) -> p kc f", kc=16),
                        wup_bf.ap()[ei, mc].rearrange("kc p f -> p kc f"), c3["wup"])
                    vw = c3["wup"].n
                    ub_bank = upps[1]
                    W("tensor", c3["act"], c3["act"].n)
                    W("tensor", c3["wup"], vw)
                    W("tensor", c3["h2"], v_h2q)

                    def _up(t, gidx=gidx, bank=ub_bank):
                        for kc in range(16):
                            mm = t.matmul(
                                bank[:, :],
                                wup_ring[gidx % 2][:, kc * 128:(kc + 1) * 128],
                                h2q[:, kc * LC:(kc + 1) * LC],
                                start=(kc == 0), stop=(kc == 15))
                        mm.then_inc(c3["up"].h, 1)

                    O("tensor", _up)
                    c3["up"].bump()
                    up_ring_rel.append(c3["up"].n)
                    assert len(up_ring_rel) == gidx + 1
                    W("scalar", c3["up"])
                    if q >= 1:
                        W("scalar", c3["wdnf"], 64 * q)

                    def _silu(s, ei=ei, mc=mc, bank=ub_bank):
                        s.activation(
                            out=act_t[:, (ei * 8 + mc) * LC:(ei * 8 + mc + 1) * LC],
                            in_=bank[:, :], func=AF.Silu,
                            bias=bup_t[:, ei * 8 + mc:ei * 8 + mc + 1]) \
                            .then_inc(c3["act"].h, 1)

                    O("scalar", _silu)
                    c3["act"].bump()
                    W("vector", c3["act"])
                    W("vector", c3["up"], v_srep)

                    def _ascale(v, ei=ei, mc=mc, bank=sr_bank):
                        v.tensor_mul(
                            out=act_t[:, (ei * 8 + mc) * LC:(ei * 8 + mc + 1) * LC],
                            in0=act_t[:, (ei * 8 + mc) * LC:(ei * 8 + mc + 1) * LC],
                            in1=bank[:, :]).then_inc(c3["acts"].h, 1)

                    O("vector", _ascale)
                    c3["acts"].bump()
            k_syn0 = upps_acq()
            k_syn1 = upps_acq()
            upps_mark(k_syn0, c3["acts"])
            upps_mark(k_syn1, c3["act"])
            pstate[q]["v_acts"] = c3["acts"].n

        def down_block(q):
            qr = pstate[q]["qr"]
            v_sc = pstate[q]["v_sc"]
            v_acts = pstate[q]["v_acts"]
            for og in range(8):
                widx = q * 8 + og
                W("sync", c3["wdnf"], 8 * (widx - 1))
                dma("sync", bdg[widx % 2][:, :].bitcast(F32R),
                    bdown_h.ap()[:, og * 256:(og + 1) * 256], c3["bdw"])
                vbd = c3["bdw"].n
                W("tensor", c3["bdw"], vbd)
                W("tensor", c3["acts"], v_acts)
                W("tensor", c3["sc"], v_sc)
                W("tensor", c3["dnev"], 2 * widx)

                def _dstart(t, qr=qr, widx=widx):
                    for ocl in range(2):
                        t.matmul(dnps[ocl][:, :],
                                 bdg[widx % 2][:, ocl * 128:(ocl + 1) * 128]
                                 .bitcast(F32R),
                                 scoreq[qr][:, :].bitcast(F32R),
                                 start=True, stop=False)

                O("tensor", _dstart)
                for ei in range(E):
                    widx2 = widx * 8 + ei
                    W("scalar", c3["wdnf"], widx2 - 1)
                    dma("scalar", wdn_ring[widx2 % 2][:, :].rearrange(
                        "p (ocl hc f) -> p ocl hc f", ocl=2, f=128),
                        wdown_bf.ap()[og, ei].rearrange("ocl hc p f -> p ocl hc f"),
                        c3["wdn"])
                    vw = c3["wdn"].n
                    W("tensor", c3["wdn"], vw)

                    def _down(t, ei=ei, widx2=widx2):
                        ring = wdn_ring[widx2 % 2]
                        for ocl in range(2):
                            for hc in range(8):
                                tix = ocl * 8 + hc
                                mm = t.matmul(
                                    dnps[ocl][:, :],
                                    ring[:, tix * 128:(tix + 1) * 128],
                                    act_t[:, (ei * 8 + hc) * LC:(ei * 8 + hc + 1) * LC],
                                    start=False,
                                    stop=(ei == E - 1 and hc == 7))
                                if hc == 7 and ocl == 1:
                                    mm.then_inc(c3["wdnf"].h, 1)

                    O("tensor", _down)
                    c3["wdnf"].bump()

                for ocl in range(2):
                    oc = og * 2 + ocl
                    kres = 2 * widx + ocl
                    W("sync", c3["x1w"])
                    W("sync", c3["res"], kres)
                    dma("sync", x1e[0][:, :],
                        x1flat[oc][:, q * LC:(q + 1) * LC], c3["er"])
                    if "x1e" in tap_h and q == 0:
                        W("sync", c3["er"])
                        dma("sync", tap_h["x1e"].ap()[oc], x1e[0][:, :], c_tap)
                    W("vector", c3["er"])
                    W("vector", c3["wdnf"], 8 * (widx + 1))
                    W("vector", c_out, 16 * kres)

                    def _res(v, ocl=ocl, oc=oc):
                        v.scalar_tensor_tensor(out=evt[0][:, :],
                                               in0=dnps[ocl][:, :],
                                               scalar=gbpp[:, oc:oc + 1],
                                               in1=x1e[0][:, :],
                                               op0=ALU.mult, op1=ALU.add) \
                            .then_inc(c3["dnev"].h, 1)
                        v.tensor_copy(out=evt[0][:, :], in_=evt[0][:, :]) \
                            .then_inc(c3["res"].h, 1)

                    O("vector", _res)
                    c3["dnev"].bump()
                    c3["res"].bump()
                    W("sync", c3["res"])
                    if "evt" in tap_h and q == 0:
                        dma("sync", tap_h["evt"].ap()[oc], evt[0][:, :], c_tap)
                        W("sync", c_tap)
                    dma("sync", outflat[oc][:, q * LC:(q + 1) * LC],
                        evt[0][:, :], c_out)

        prep_block(0)
        for q in range(NQ):
            up_block(q)
            if q + 1 < NQ:
                prep_block(q + 1)
            down_block(q)

    if "ncaw" in tap_h:
        if skip_p3:
            for g_ in range(NQ):
                W("sync", c1_a2a[g_])
            W("sync", c1["ywr"])
        else:
            W("sync", c_out)
        _ncaw_dump(2)

    if "cmod" in tap_h:
        if skip_p3:
            for g_ in range(NQ):
                W("sync", c1_a2a[g_])
            W("sync", c1["ywr"])
        else:
            W("sync", c_out)
        dma("sync", tap_h["cmod"].ap()[:, 0:48], modsl[:, :], c_tap)
        dma("sync", tap_h["cmod"].ap()[:, 48:56], ca1pp[:, :], c_tap)
        dma("sync", tap_h["cmod"].ap()[:, 56:64], nca1pp[:, :], c_tap)

    W("scalar", c_out)
    if "winf" in tap_h:
        W("sync", c_out)
        _win_dump(1)
    if tap_h:
        W("scalar", c_tap)
    if skip_p3:
        for g_ in range(NQ):
            W("scalar", c1_a2a[g_])
        W("scalar", c1["ywr"])

    block = e(nc.Block())
    with allow_ici_all_to_all():
        P.finish(block)
    stk.close()
    return nc


def kernel(**inputs):
    in_maps = stage_inputs(inputs)
    nc = build_nc(taps=tuple(DEBUG_TAPS), skip_p3=SKIP_P3)
    with allow_ici_all_to_all():
        res = run_bass_kernel_spmd(nc, in_maps, core_ids=list(range(8)))
    out = np.empty((NB, L, D), dtype=np.float32)
    for c in range(8):
        bc, hc3 = c // 2, c % 2
        out[bc, hc3 * TOK3:(hc3 + 1) * TOK3, :] = res.results[c]["out"].T
    kernel.last_results = res.results
    return out
